# version 02
# baseline (speedup 1.0000x reference)
import contextlib
from collections import defaultdict

import numpy as np
import concourse.bass as bass
import concourse.mybir as mybir
from concourse.bass_utils import run_bass_kernel_spmd

F32 = mybir.dt.float32
F32R = mybir.dt.float32r
BF16 = mybir.dt.bfloat16
FP8 = mybir.dt.float8e4
AF = mybir.ActivationFunctionType
ALU = mybir.AluOpType
AX = mybir.AxisListType

D = 2048
NH = 16
TOWN = 2048
TALL = 4096
NTC = 32
TC0 = 16
SP = 512
NSP = TALL // SP
EPS = 1e-5
NEGM = -30000.0

ENGS = ("sync", "gpsimd", "tensor", "vector", "scalar")

_DBG = ()
_DBG_SEL = (4, 16)
_PHASES = 6


class Prog:
    def __init__(self, nc):
        self.nc = nc
        self.ops = {e: [] for e in ENGS}
        self.cnt = defaultdict(int)
        self.semh = {}
        self.last_wait = {e: defaultdict(int) for e in ENGS}
        self.es = contextlib.ExitStack()

    def sem(self, name):
        if name not in self.semh:
            self.semh[name] = self.es.enter_context(self.nc.semaphore(name))
        return self.semh[name]

    def op(self, eng, fn):
        self.ops[eng].append(fn)

    def wait(self, eng, sem_name, tgt):
        if tgt <= 0:
            return
        if self.last_wait[eng][sem_name] >= tgt:
            return
        self.last_wait[eng][sem_name] = tgt
        s = self.sem(sem_name)
        self.op(eng, lambda e, s=s, t=tgt: e.wait_ge(s, t))

    def waitp(self, eng, pair):
        if pair is not None:
            self.wait(eng, pair[0], pair[1])

    def run(self, eng_name, eng):
        for fn in self.ops[eng_name]:
            fn(eng)

    def dma(self, eng, sem_name, out, in_):
        s = self.sem(sem_name)
        self.cnt[sem_name] += 16
        self.op(eng, lambda e, o=out, i=in_, s=s: e.dma_start(out=o, in_=i).then_inc(s, 16))
        return self.cnt[sem_name]

    def emit(self, eng, sem_name, fn):
        s = self.sem(sem_name)
        self.cnt[sem_name] += 1
        self.op(eng, lambda e, f=fn, s=s: f(e).then_inc(s, 1))
        return self.cnt[sem_name]

    def emitq(self, eng, fn):
        self.op(eng, fn)

    def schain(self, eng, fn):
        sem = {"vector": "V", "scalar": "S", "gpsimd": "G"}[eng]
        self.wait(eng, sem, self.cnt[sem])
        return self.emit(eng, sem, fn)

    def tap(self, dst, src, wait_sem, wait_tgt):
        self.wait("gpsimd", wait_sem, wait_tgt)
        t = self.dma("gpsimd", "dbg", dst, src)
        for e in ENGS:
            self.wait(e, "dbg", t)

    def barrier(self):
        sems = [k for k, v in self.cnt.items() if v > 0]
        for e in ENGS:
            for s in sems:
                self.wait(e, s, self.cnt[s])


def _finalize(nc, P):
    for snm in list(P.cnt.keys()):
        if P.cnt[snm] > 0:
            P.wait("sync", snm, P.cnt[snm])
    if True:
        with nc.Block() as blk:
            @blk.sync
            def _(e):
                P.run("sync", e)

            @blk.gpsimd
            def _(e):
                P.run("gpsimd", e)

            @blk.tensor
            def _(e):
                P.run("tensor", e)

            @blk.vector
            def _(e):
                P.run("vector", e)

            @blk.scalar
            def _(e):
                P.run("scalar", e)
    return nc


def build():
    nc = bass.Bass("TRN2", target_bir_lowering=False, detect_race_conditions=False)
    P = Prog(nc)
    es = P.es
    sbt = lambda name, shape, dt: es.enter_context(nc.sbuf_tensor(name, shape, dt))
    pst = lambda name, shape, dt: es.enter_context(nc.psum_tensor(name, shape, dt))
    dram = lambda name, shape, dt, kind: nc.dram_tensor(name, shape, dt, kind=kind).ap()

    xpro = dram("xpro", [TOWN, D], F32, "ExternalInput")
    xown = dram("xown", [TOWN, D], F32, "ExternalInput")
    cvec = dram("cvec", [16, 128], F32R, "ExternalInput")
    pmask = dram("pmask", [128, 1], F32, "ExternalInput")
    wmod_t = dram("wmod_t", [96, 16, 128, 128], F32R, "ExternalInput")
    wq_t = dram("wq_t", [2048, 2048], F32, "ExternalInput")
    wk_t = dram("wk_t", [2048, 2048], F32, "ExternalInput")
    wv_t = dram("wv_t", [2048, 2048], F32, "ExternalInput")
    wig_t = dram("wig_t", [2048, 2048], F32, "ExternalInput")
    wog_t = dram("wog_t", [2048, 2048], F32, "ExternalInput")
    wo_t = dram("wo_t", [2048, 2048], F32, "ExternalInput")
    wg_t = dram("wg_t", [2048, 16], F32, "ExternalInput")
    big_d = dram("big", [1, 2048], F32, "ExternalInput")
    bog_d = dram("bog", [1, 2048], F32, "ExternalInput")
    bgr_d = dram("bgr", [1, 16], F32, "ExternalInput")
    convw_d = dram("convw", [128, 64], F32, "ExternalInput")
    convb_d = dram("convb", [128, 16], F32, "ExternalInput")
    mnw_d = dram("mnw", [1, 2048], F32, "ExternalInput")
    mnb_d = dram("mnb", [1, 2048], F32, "ExternalInput")
    gnw_d = dram("gnw", [1, 2048], F32, "ExternalInput")
    gnb_d = dram("gnb", [1, 2048], F32, "ExternalInput")
    wg0_t = dram("wg0_t", [64, 16, 128, 128], F32, "ExternalInput")
    wg1_t = dram("wg1_t", [64, 16, 128, 128], F32, "ExternalInput")
    wl_c = dram("wl_c", [16, 64, 128, 128], F32, "ExternalInput")
    eye_d = dram("eye", [128, 128], F32R, "ExternalInput")
    eyeb_d = dram("eyeb", [128, 128], F32, "ExternalInput")
    tril_d = dram("tril", [128, 128], F32R, "ExternalInput")
    maskm_d = dram("maskm", [128, 128], F32, "ExternalInput")
    ones_d = dram("ones_d", [1, 128], F32, "ExternalInput")
    onesr_d = dram("onesr_d", [1, 128], F32R, "ExternalInput")

    out = dram("out", [TOWN, D], F32, "ExternalOutput")
    dbg = {}
    for nm, shp in (("mod", [128, 96]), ("hfm", [128, 16 * (SP + 3)]),
                    ("xconv", [128, 16 * SP]), ("ltall", [16, NTC * 128]),
                    ("bt", [128, 2048]), ("mem", [128, 2048]),
                    ("gam", [128, 16]), ("lg", [128, 16]), ("lsv", [128, 16]),
                    ("halo", [128, 48]), ("hz0", [128, 48]), ("lt1", [16, 4096]),
                    ("lta5", [16, 4096]), ("lta6", [16, 4096]), ("lta7", [16, 4096]),
                    ("xnb", [128, 2048]), ("lnst", [128, 16]), ("mm", [128, 2048]),
                    ("z", [128, 2048]), ("x2", [128, 2048]),
                    ("h2", [128, 2048]), ("srow", [1, 2048]), ("zfl", [128, 2048]),
                    ("gab", [128, 2048]), ("scrw", [128, 2048]),
                    ("gaf", [128, 16]), ("gar", [1, 2048]),
                    ("kn", [128, 2048]), ("q", [128, 2048]), ("og", [128, 2048])):
        if nm in _DBG:
            dbg[nm] = dram("dbg_" + nm, shp, F32, "ExternalOutput")

    bt_sp = dram("bt_sp", [TALL, D], BF16, "Internal")
    q_sp = dram("q_sp", [TOWN, D], BF16, "Internal")
    og_sp = dram("og_sp", [TOWN, D], BF16, "Internal")
    zf_sp = dram("zf_sp", [TOWN, D], BF16, "Internal")
    x2_sp = dram("x2_sp", [TOWN, D], F32, "Internal")
    h2_sp = dram("h2_sp", [D, TOWN], FP8, "Internal")

    P0 = pst("P0", [128, 1024], F32)
    P1 = pst("P1", [128, 1024], F32)
    P2 = pst("P2", [128, 512], F32)
    PT4 = pst("PT4", [128, 512], F32R)
    PT5 = pst("PT5", [128, 2048], BF16)

    eyeR = sbt("eyeR", [128, 128], F32R)
    eyeB = sbt("eyeB", [128, 128], BF16)
    trilR = sbt("trilR", [128, 128], F32R)
    maskm = sbt("maskm_s", [128, 128], F32)
    onesbf = sbt("onesbf_s", [1, 128], BF16)
    onesr = sbt("onesr_s", [1, 128], F32R)
    convw = sbt("convw_s", [128, 64], F32)
    convb = sbt("convb_s", [128, 16], F32)
    wgt_s = sbt("wgt_s", [128, 256], BF16)
    bigrow = sbt("bigrow", [1, 2048], BF16)
    bogrow = sbt("bogrow", [1, 2048], BF16)
    bgrow = sbt("bgrow", [1, 16], BF16)
    pm_s = sbt("pm_s", [128, 1], F32)
    sc16 = sbt("sc16", [16, 128], F32R)
    sccols = sbt("sccols", [128, 16], F32R)
    mod_sb = sbt("mod_sb", [128, 96], F32)
    cam_pro = sbt("cam_pro", [128, 16], F32)
    sam_pro = sbt("sam_pro", [128, 16], F32)
    cam = sbt("cam", [128, 16], F32)
    sam = sbt("sam", [128, 16], F32)
    cbm = sbt("cbm", [128, 16], F32)
    sbm = sbt("sbm", [128, 16], F32)
    gam_f = sbt("gam_f", [128, 16], F32)
    gbm_f = sbt("gbm_f", [128, 16], F32)
    gab = sbt("gab", [128, 2048], F32)
    halo = sbt("halo", [128, 48], BF16)

    smalls = {}
    for nm in ("st1", "st2", "mu", "ms", "mu2", "var", "rv", "rr", "nmr",
               "ss", "ssn", "ssr",
               "g1s", "g2s", "g1n", "g1r", "g2n", "g2r", "gmu", "gms", "gm2",
               "gva", "grv", "gsa", "gsb", "gsc"):
        smalls[nm] = sbt("sm_" + nm, [128, 16], F32)

    P.dma("sync", "dc", eyeR[:], eye_d)
    P.dma("sync", "dc", trilR[:], tril_d)
    P.dma("sync", "dc", maskm[:], maskm_d)
    P.dma("sync", "dc", convw[:], convw_d)
    P.dma("sync", "dc", convb[:], convb_d)
    P.dma("sync", "dc", pm_s[:], pmask)
    P.dma("sync", "dc", sc16[:], cvec)
    t_cdx = P.dma("sync", "dc", onesr[:], onesr_d)
    P.dma("gpsimd", "dgc", eyeB[:], eyeb_d)
    P.dma("gpsimd", "dgc", onesbf[:], ones_d)
    P.dma("gpsimd", "dgc", bigrow[:], big_d)
    P.dma("gpsimd", "dgc", bogrow[:], bog_d)
    t_cdg = P.dma("gpsimd", "dgc", bgrow[:], bgr_d)
    for k in range(16):
        t_cdg = P.dma("gpsimd", "dgc", wgt_s[:, k * 16:(k + 1) * 16],
                      wg_t[k * 128:(k + 1) * 128, :])

    with contextlib.ExitStack() as ph0:
        wmbuf = ph0.enter_context(nc.sbuf_tensor("wmbuf", [128, 2 * 2048], F32R))
        garowS = ph0.enter_context(nc.sbuf_tensor("garowS", [16, 128], F32R))
        garow2 = ph0.enter_context(nc.sbuf_tensor("garow2", [1, 2048], F32R))
        gxm_fr = ph0.enter_context(nc.sbuf_tensor("gxm_fr", [128, 16], F32R))

        P.wait("scalar", "dc", t_cdx)
        t_silu = P.emit("scalar", "S", lambda e: e.activation(sc16[:], sc16[:], AF.Silu))
        P.wait("tensor", "S", t_silu)
        t_tr = P.emit("tensor", "T", lambda e: e.transpose(PT4[:, 0:16], sc16[:], eyeR[0:16, 0:16]))
        P.wait("scalar", "T", t_tr)
        t_sc = P.emit("scalar", "S", lambda e: e.activation(sccols[:], PT4[:, 0:16], AF.Identity))

        scpad = ph0.enter_context(nc.sbuf_tensor("scpad", [128, 32], F32R))
        t_mz = P.emit("vector", "V", lambda e: e.memzero(scpad[:]))
        P.wait("vector", "V", t_mz)
        P.wait("vector", "S", t_sc)
        t_scp = P.emit("vector", "V", lambda e: e.tensor_copy(
            scpad[:].rearrange("p (k two) -> p k two", two=2)[:, :, 0:1],
            sccols[:].rearrange("p k -> p k ()")))
        P.wait("tensor", "V", t_scp)
        t_mm_hist = {}
        t_mm = 0
        for oc in range(96):
            slot = oc % 2
            if oc >= 2:
                P.wait("sync", "T", t_mm_hist[oc - 2])
            tgt = P.dma("sync", f"dw{slot}",
                        wmbuf[:, slot * 2048:(slot + 1) * 2048].rearrange("p (k q) -> p k q", k=16),
                        wmod_t[oc].rearrange("k p q -> p k q"))
            P.wait("tensor", f"dw{slot}", tgt)
            for k in range(16):
                fn = (lambda e, oc=oc, k=k, slot=slot:
                      e.matmul(P2[:, 2 * oc:2 * oc + 2],
                               wmbuf[:, slot * 2048 + k * 128:slot * 2048 + (k + 1) * 128],
                               scpad[:, 2 * k:2 * k + 2], start=(k == 0), stop=(k == 15)))
                if k == 15:
                    t_mm = P.emit("tensor", "T", fn)
                else:
                    P.emitq("tensor", fn)
            t_mm_hist[oc] = t_mm
        P.wait("vector", "T", t_mm)
        t_mod = P.emit("vector", "V", lambda e: e.tensor_copy(
            mod_sb[:].rearrange("p o -> p o ()"),
            P2[:, 0:192].rearrange("p (o two) -> p o two", two=2)[:, :, 0:1]))

        P.wait("vector", "V", t_mod)
        P.wait("vector", "dc", t_cdx)
        P.schain("vector", lambda e: e.tensor_scalar_add(cam[:], mod_sb[:, 16:32], 1.0))
        P.schain("vector", lambda e: e.tensor_copy(sam[:], mod_sb[:, 0:16]))
        P.schain("vector", lambda e: e.tensor_scalar_mul(cam_pro[:], cam[:], pm_s[:, 0:1]))
        P.schain("vector", lambda e: e.tensor_scalar_mul(sam_pro[:], sam[:], pm_s[:, 0:1]))
        P.schain("vector", lambda e: e.tensor_scalar_add(cbm[:], mod_sb[:, 64:80], 1.0))
        P.schain("vector", lambda e: e.tensor_copy(sbm[:], mod_sb[:, 48:64]))
        g1, g2, g3 = smalls["gsa"], smalls["gsb"], smalls["gsc"]
        for off, gdst in ((32, gam_f), (80, gbm_f)):
            P.schain("vector", lambda e, o=off: e.tensor_single_scalar(g1[:], mod_sb[:, o:o + 16], 0.0, ALU.is_ge))
            P.schain("vector", lambda e: e.tensor_scalar(g2[:], g1[:], 2.0, -1.0, ALU.mult, ALU.add))
            P.schain("vector", lambda e, o=off: e.scalar_tensor_tensor(
                g3[:], g2[:], 0.25, mod_sb[:, o:o + 16], ALU.mult, ALU.add))
            P.schain("vector", lambda e, o=off: e.tensor_scalar(
                g1[:], mod_sb[:, o:o + 16], -0.25, 0.25, ALU.max, ALU.min))
            P.schain("vector", lambda e, d=gdst: e.tensor_sub(d[:], g3[:], g1[:]))
        t_modp = P.emit("vector", "V", lambda e: e.tensor_copy(gxm_fr[:], gam_f[:]))
        if "gaf" in _DBG:
            P.tap(dbg["gaf"], gam_f[:], "V", t_modp)

        if "mod" in _DBG:
            P.wait("gpsimd", "V", t_modp)
            P.dma("gpsimd", "dbg", dbg["mod"], mod_sb[:])

        td = ts = tt2 = 0
        for which, (gsrc, gdst) in enumerate(((gam_f, gab),)):
            if which == 1:
                P.wait("vector", "dfl", td)
                P.wait("vector", "T", tt2)
                P.emit("vector", "V", lambda e, g=gsrc: e.tensor_copy(gxm_fr[:], g[:]))
            tv = P.cnt["V"]
            P.wait("tensor", "V", tv)
            if which == 1:
                P.wait("tensor", "S", ts)
            tt = P.emit("tensor", "T", lambda e: e.transpose(PT4[0:16, 128:256], gxm_fr[:], eyeR[:]))
            P.wait("scalar", "T", tt)
            if which == 1:
                P.wait("scalar", "dfl", td)
            ts = P.emit("scalar", "S", lambda e: e.activation(garowS[:], PT4[0:16, 128:256], AF.Identity))
            P.wait("sync", "S", ts)
            td = P.dma("sync", "dfl", garow2[0:1, :], garowS[:])
            if "gar" in _DBG:
                P.tap(dbg["gar"], garow2[0:1, :], "dfl", td)
            P.wait("tensor", "dfl", td)
            for s4 in range(4):
                tt2 = P.emit("tensor", "T", lambda e, s4=s4: e.matmul(
                    P0[:, (s4 % 2) * 512:(s4 % 2) * 512 + 512],
                    onesr[:], garow2[0:1, s4 * 512:(s4 + 1) * 512], start=True, stop=True))
                P.wait("vector", "T", tt2)
                tv2 = P.emit("vector", "V", lambda e, s4=s4, gd=gdst: e.tensor_copy(
                    gd[:, s4 * 512:(s4 + 1) * 512], P0[:, (s4 % 2) * 512:(s4 % 2) * 512 + 512]))
                P.wait("tensor", "V", tv2)
        P.barrier()
    if _PHASES <= 1:
        return _finalize(nc, P)

    span_es = contextlib.ExitStack()
    LT_all = span_es.enter_context(nc.sbuf_tensor("LT_all", [16, NTC * 128], F32))
    srowA = span_es.enter_context(nc.sbuf_tensor("srowA", [1, 2048], BF16))
    srowB = span_es.enter_context(nc.sbuf_tensor("srowB", [1, 2048], BF16))

    with contextlib.ExitStack() as phA:
        asb = lambda name, shape, dt: phA.enter_context(nc.sbuf_tensor(name, shape, dt))
        xb = asb("xb", [128, 2048], F32)
        xnb = asb("xnb", [128, 2048], BF16)
        scrA = asb("scrA", [128, 2048], F32R)
        scrB = asb("scrB", [128, 2048], F32R)
        h_fm = asb("h_fm", [128, 16 * (SP + 3)], BF16)
        xc_fm = asb("xc_fm", [128, 16 * SP], BF16)
        wsl = asb("wsl", [128, 2 * 16 * 512], BF16)
        k_sb = asb("k_sb", [128, 4 * 2048], BF16)
        v_sb = asb("v_sb", [128, 4 * 2048], BF16)
        btb = asb("btb", [128, 4 * 2048], BF16)
        cacc = asb("cacc", [128, 2 * 512], F32)
        gamv = asb("gamv", [128, 16], F32)
        lgv = asb("lgv", [128, 16], F32R)
        lsv = asb("lsv", [128, 16], F32R)

        t_hz = P.emit("vector", "V", lambda e: e.memzero(halo[:]))
        P.wait("vector", "V", t_hz)
        if "hz0" in _DBG:
            P.tap(dbg["hz0"], halo[:], "V", t_hz)

        pt4_free = [None] * 4
        pt4_used = [0] * 4
        pt5_free = [None, None]
        t_xn_tr = 0
        pslab_free = [None] * 4
        wsl_t_done = [0, 0]
        t_xn_prev = 0
        t_proj_done_sp = 0
        t_gamma_evac_prev = 0
        t_scrA_vuse = [0]
        slabslot = [0]

        def pslab_ap(i):
            return (P0 if i < 2 else P1)[:, (i % 2) * 512:(i % 2) * 512 + 512]

        for sp in range(NSP):
            own = sp >= 4
            tcs = list(range(sp * 4, sp * 4 + 4))
            if f"lta{sp}" in _DBG:
                P.tap(dbg[f"lta{sp}"], LT_all[:], "S", P.cnt["S"])
            for j, tc in enumerate(tcs):
                src = xpro if tc < TC0 else xown
                r0 = (tc * 128) % 2048
                P.wait("sync", "S", t_xn_prev)
                t_x = P.dma("sync", "dxx", xb[:], src[r0:r0 + 128, :])
                P.wait("scalar", "dxx", t_x)
                P.wait("scalar", "V", t_scrA_vuse[0])
                P.emitq("scalar", lambda e: e.activation(
                    scrA[:], xb[:], AF.Square, accum_out=smalls["st2"][:, 0:1]))
                t_st = P.emit("scalar", "S", lambda e: e.activation(
                    scrA[:], xb[:], AF.Identity, accum_out=smalls["st1"][:, 0:1]))
                P.wait("vector", "S", t_st)
                P.schain("vector", lambda e: e.tensor_scalar_mul(smalls["mu"][:, 0:1], smalls["st1"][:, 0:1], 1.0 / 2048))
                t_v0 = P.emit("vector", "V", lambda e: e.tensor_scalar_mul(smalls["ms"][:, 0:1], smalls["st2"][:, 0:1], 1.0 / 2048))
                P.wait("scalar", "V", t_v0)
                t_s0 = P.emit("scalar", "S", lambda e: e.activation(smalls["mu2"][:, 0:1], smalls["mu"][:, 0:1], AF.Square))
                P.wait("vector", "S", t_s0)
                P.schain("vector", lambda e: e.tensor_sub(smalls["var"][:, 0:1], smalls["ms"][:, 0:1], smalls["mu2"][:, 0:1]))
                P.schain("vector", lambda e: e.tensor_scalar_add(smalls["var"][:, 0:1], smalls["var"][:, 0:1], EPS))
                P.wait("vector", "V", P.cnt["V"])
                t_v1 = P.emit("vector", "V", lambda e: e.reciprocal(smalls["rv"][:, 0:1], smalls["var"][:, 0:1]))
                P.wait("scalar", "V", t_v1)
                t_s1 = P.emit("scalar", "S", lambda e: e.activation(smalls["rr"][:, 0:1], smalls["rv"][:, 0:1], AF.Sqrt))
                P.wait("vector", "S", t_s1)
                t_v2 = P.emit("vector", "V", lambda e: e.scalar_tensor_tensor(
                    smalls["nmr"][:, 0:1], smalls["mu"][:, 0:1], -1.0, smalls["rr"][:, 0:1], ALU.mult, ALU.mult))
                P.wait("scalar", "V", t_v2)
                P.wait("scalar", "T", t_xn_tr[0] if isinstance(t_xn_tr, list) else t_xn_tr)
                t_xn = P.emit("scalar", "S", lambda e: e.activation(
                    xnb[:], xb[:], AF.Identity, bias=smalls["nmr"][:, 0:1], scale=smalls["rr"][:, 0:1]))
                t_xn_prev = t_xn
                if "xnb" in _DBG and tc == _DBG_SEL[1]:
                    P.tap(dbg["xnb"], xnb[:], "S", t_xn)
                    P.emitq("vector", lambda e: e.tensor_copy(dbg and smalls["gsa"][:, 0:1], smalls["st1"][:, 0:1]) if False else e.tensor_copy(smalls["gsa"][:, 0:1], smalls["st1"][:, 0:1]))
                    P.emitq("vector", lambda e: e.tensor_copy(smalls["gsa"][:, 1:2], smalls["st2"][:, 0:1]))
                    P.emitq("vector", lambda e: e.tensor_copy(smalls["gsa"][:, 2:3], smalls["mu"][:, 0:1]))
                    P.emitq("vector", lambda e: e.tensor_copy(smalls["gsa"][:, 3:4], smalls["var"][:, 0:1]))
                    P.emitq("vector", lambda e: e.tensor_copy(smalls["gsa"][:, 4:5], smalls["rv"][:, 0:1]))
                    P.emitq("vector", lambda e: e.tensor_copy(smalls["gsa"][:, 5:6], smalls["rr"][:, 0:1]))
                    tln = P.emit("vector", "V", lambda e: e.tensor_copy(smalls["gsa"][:, 6:7], smalls["nmr"][:, 0:1]))
                    P.tap(dbg["lnst"], smalls["gsa"][:], "V", tln)
                camx = cam_pro if tc < TC0 else cam
                samx = sam_pro if tc < TC0 else sam
                P.wait("tensor", "S", t_xn)
                for grp in range(2):
                    P.waitp("tensor", pt5_free[grp])
                    for cc in range(8):
                        c = grp * 8 + cc
                        fn = lambda e, c=c, cc=cc, grp=grp: e.transpose(
                            PT5[:, grp * 1024 + cc * 128:grp * 1024 + (cc + 1) * 128],
                            xnb[:, c * 128:(c + 1) * 128], eyeB[:])
                        if cc == 7:
                            tt = P.emit("tensor", "T", fn)
                        else:
                            P.emitq("tensor", fn)
                    pt4_used[3] = max(pt4_used[3], 0)
                    P.wait("vector", "T", tt)
                    P.wait("vector", "T", t_proj_done_sp)
                    tv = 0
                    for cc in range(8):
                        c = grp * 8 + cc
                        tv = P.emit("vector", "V", lambda e, c=c, cc=cc, grp=grp, j=j, camx=camx, samx=samx: e.tensor_scalar(
                            h_fm[:, c * (SP + 3) + 3 + j * 128: c * (SP + 3) + 3 + (j + 1) * 128],
                            PT5[:, grp * 1024 + cc * 128:grp * 1024 + (cc + 1) * 128],
                            camx[:, c:c + 1], samx[:, c:c + 1], ALU.mult, ALU.add))
                    pt5_free[grp] = ("V", tv)
                t_xn_tr = tt
            if "halo" in _DBG and sp == _DBG_SEL[0]:
                P.tap(dbg["halo"], halo[:], "V", P.cnt["V"])
            for c in range(16):
                P.emitq("vector", lambda e, c=c: e.tensor_copy(
                    h_fm[:, c * (SP + 3): c * (SP + 3) + 3], halo[:, c * 3:c * 3 + 3]))
            P.wait("scalar", "T", t_proj_done_sp)
            t_silu_prev = 0
            for c in range(16):
                base = c * (SP + 3)
                P.wait("vector", "S", t_silu_prev)
                P.emitq("vector", lambda e, c=c, base=base: e.tensor_scalar(
                    cacc[:, 0:512], h_fm[:, base:base + 512],
                    convw[:, c * 4:c * 4 + 1], convb[:, c:c + 1], ALU.mult, ALU.add))
                P.emitq("vector", lambda e, c=c, base=base: e.scalar_tensor_tensor(
                    cacc[:, 512:1024], h_fm[:, base + 1:base + 513],
                    convw[:, c * 4 + 1:c * 4 + 2], cacc[:, 0:512], ALU.mult, ALU.add))
                P.emitq("vector", lambda e, c=c, base=base: e.scalar_tensor_tensor(
                    cacc[:, 0:512], h_fm[:, base + 2:base + 514],
                    convw[:, c * 4 + 2:c * 4 + 3], cacc[:, 512:1024], ALU.mult, ALU.add))
                tvc = P.emit("vector", "V", lambda e, c=c, base=base: e.scalar_tensor_tensor(
                    cacc[:, 512:1024], h_fm[:, base + 3:base + 515],
                    convw[:, c * 4 + 3:c * 4 + 4], cacc[:, 0:512], ALU.mult, ALU.add))
                P.wait("scalar", "V", tvc)
                t_silu_prev = P.emit("scalar", "S", lambda e, c=c: e.activation(
                    xc_fm[:, c * 512:(c + 1) * 512], cacc[:, 512:1024], AF.Silu))
            for c in range(16):
                P.emitq("vector", lambda e, c=c: e.tensor_copy(
                    halo[:, c * 3:c * 3 + 3], h_fm[:, c * (SP + 3) + 512:c * (SP + 3) + 515]))
            t_hevac_sp = P.cnt["V"]
            t_conv_done = P.cnt["S"]

            if "hfm" in _DBG and sp == _DBG_SEL[0]:
                P.tap(dbg["hfm"], h_fm[:], "V", t_hevac_sp)
            if "xconv" in _DBG and sp == _DBG_SEL[0]:
                P.tap(dbg["xconv"], xc_fm[:], "S", t_conv_done)

            passes = [("k", wk_t, True, None), ("v", wv_t, True, None),
                      ("ig", wig_t, False, bigrow)]
            if own:
                passes += [("q", wq_t, True, None), ("og", wog_t, False, bogrow)]

            for pname, wsrc, use_h, brow in passes:
                if pname in ("q", "og"):
                    P.wait("scalar", "V", P.cnt["V"])
                for slab in range(4):
                    wslot = slab % 2
                    P.wait("gpsimd", "T", wsl_t_done[wslot])
                    tw = 0
                    for k in range(16):
                        tw = P.dma("gpsimd", f"dws{wslot}",
                                   wsl[:, wslot * 8192 + k * 512:wslot * 8192 + (k + 1) * 512],
                                   wsrc[k * 128:(k + 1) * 128, slab * 512:(slab + 1) * 512])
                    for j, tc in enumerate(tcs):
                        sslot = slabslot[0] % 4
                        slabslot[0] += 1
                        pap = pslab_ap(sslot)
                        P.wait("tensor", f"dws{wslot}", tw)
                        P.wait("tensor", "dgc", t_cdg)
                        P.waitp("tensor", pslab_free[sslot])
                        P.wait("tensor", "V", t_hevac_sp)
                        P.wait("tensor", "S", t_conv_done)
                        nmm = 16
                        for k in range(16):
                            if use_h:
                                lhs = h_fm[:, k * (SP + 3) + 3 + j * 128:k * (SP + 3) + 3 + (j + 1) * 128]
                            else:
                                lhs = xc_fm[:, k * 512 + j * 128:k * 512 + (j + 1) * 128]
                            last = (k == 15) and (brow is None)
                            fn = lambda e, k=k, lhs=lhs, pap=pap, wslot=wslot, last=last: e.matmul(
                                pap, lhs, wsl[:, wslot * 8192 + k * 512:wslot * 8192 + (k + 1) * 512],
                                start=(k == 0), stop=last)
                            if last:
                                tmm = P.emit("tensor", "T", fn)
                            else:
                                P.emitq("tensor", fn)
                        if brow is not None:
                            tmm = P.emit("tensor", "T", lambda e, pap=pap, brow=brow, slab=slab: e.matmul(
                                pap, onesbf[:], brow[0:1, slab * 512:(slab + 1) * 512],
                                start=False, stop=True))
                        if pname == "ig" and slab == 0:
                            P.wait("tensor", "S", t_gamma_evac_prev)
                            for k in range(16):
                                P.emitq("tensor", lambda e, k=k, j=j: e.matmul(
                                    P2[:, j * 16:(j + 1) * 16],
                                    xc_fm[:, k * 512 + j * 128:k * 512 + (j + 1) * 128],
                                    wgt_s[:, k * 16:(k + 1) * 16], start=(k == 0), stop=False))
                            tmm = P.emit("tensor", "T", lambda e, j=j: e.matmul(
                                P2[:, j * 16:(j + 1) * 16], onesbf[:], bgrow[:],
                                start=False, stop=True))
                        wsl_t_done[wslot] = tmm
                        P.wait("scalar", "T", tmm)
                        if pname in ("k", "q"):
                            te = P.emit("scalar", "S", lambda e, j=j, slab=slab, pap=pap: e.activation(
                                k_sb[:, j * 2048 + slab * 512:j * 2048 + (slab + 1) * 512], pap, AF.Identity))
                            pslab_free[sslot] = ("S", te)
                        elif pname == "v":
                            P.wait("scalar", f"dsv{j}", P.cnt[f"dsv{j}"])
                            te = P.emit("scalar", "S", lambda e, j=j, slab=slab, pap=pap: e.activation(
                                v_sb[:, j * 2048 + slab * 512:j * 2048 + (slab + 1) * 512], pap, AF.Identity))
                            pslab_free[sslot] = ("S", te)
                        elif pname == "og":
                            P.wait("scalar", f"dsv{j}", P.cnt[f"dsv{j}"])
                            te = P.emit("scalar", "S", lambda e, j=j, slab=slab, pap=pap: e.activation(
                                v_sb[:, j * 2048 + slab * 512:j * 2048 + (slab + 1) * 512], pap, AF.Sigmoid))
                            pslab_free[sslot] = ("S", te)
                        else:
                            P.wait("scalar", "V", t_scrA_vuse[0])
                            ts1 = P.emit("scalar", "S", lambda e, pap=pap, slab=slab: e.activation(
                                scrA[:, slab * 512:(slab + 1) * 512], pap, AF.Sigmoid))
                            P.wait("vector", "S", ts1)
                            P.wait("vector", f"dsb{j}", P.cnt[f"dsb{j}"])
                            tvb = P.emit("vector", "V", lambda e, j=j, slab=slab: e.tensor_mul(
                                btb[:, j * 2048 + slab * 512:j * 2048 + (slab + 1) * 512],
                                k_sb[:, j * 2048 + slab * 512:j * 2048 + (slab + 1) * 512],
                                scrA[:, slab * 512:(slab + 1) * 512]))
                            pslab_free[sslot] = ("V", tvb)
                            t_scrA_vuse[0] = tvb

                if pname in ("v", "q"):
                    for j, tc in enumerate(tcs):
                        P.wait("scalar", "V", P.cnt["V"])
                        tq1 = P.emit("scalar", "S", lambda e, j=j: e.activation(
                            scrB[:], k_sb[:, j * 2048:(j + 1) * 2048], AF.Square))
                        P.wait("vector", "S", tq1)
                        tv3 = P.emit("vector", "V", lambda e: e.tensor_reduce(
                            smalls["ss"][:].rearrange("p s -> p s ()"),
                            scrB[:].rearrange("p (s d) -> p s d", s=16), AX.X, ALU.add))
                        P.wait("scalar", "V", tv3)
                        ts3 = P.emit("scalar", "S", lambda e: e.activation(
                            smalls["ssn"][:], smalls["ss"][:], AF.Sqrt))
                        P.wait("vector", "S", ts3)
                        P.schain("vector", lambda e: e.tensor_scalar_max(
                            smalls["ssn"][:], smalls["ssn"][:], 1e-12))
                        P.wait("vector", "V", P.cnt["V"])
                        tv4 = P.emit("vector", "V", lambda e: e.reciprocal(
                            smalls["ssr"][:], smalls["ssn"][:]))
                        P.wait("vector", "V", tv4)
                        ksl = k_sb[:, j * 2048:(j + 1) * 2048]
                        k3 = ksl.rearrange("p (s d) -> p s d", s=16)
                        r3 = bass.broadcast_tensor_aps(
                            smalls["ssr"][:].rearrange("p s -> p s ()"), k3)[0]
                        if pname == "v":
                            P.emitq("vector", lambda e, k3=k3, r3=r3: e.tensor_tensor(
                                scrB[:].rearrange("p (s d) -> p s d", s=16), k3, r3, ALU.mult))
                            if "kn" in _DBG and tc == _DBG_SEL[1]:
                                P.emitq("vector", lambda e: e.tensor_copy(xnb[:], scrB[:]))
                                tdk = P.emit("vector", "V", lambda e: e.tensor_copy(xnb[:], scrB[:]))
                                P.wait("gpsimd", "V", tdk)
                                P.dma("gpsimd", "dbg", dbg["kn"], xnb[:])
                            P.emit("vector", "V", lambda e, j=j, ksl=ksl: e.tensor_mul(
                                ksl, scrB[:], v_sb[:, j * 2048:(j + 1) * 2048]))
                        else:
                            P.wait("vector", f"dsb{j}", P.cnt[f"dsb{j}"])
                            tq5 = P.emit("vector", "V", lambda e, j=j, k3=k3, r3=r3: e.tensor_tensor(
                                btb[:, j * 2048:(j + 1) * 2048].rearrange("p (s d) -> p s d", s=16),
                                k3, r3, ALU.mult))
                            P.wait("sync", "V", tq5)
                            P.dma("sync", f"dsb{j}", q_sp[(tc - TC0) * 128:(tc - TC0) * 128 + 128, :],
                                  btb[:, j * 2048:(j + 1) * 2048])
                            if "q" in _DBG and tc == _DBG_SEL[1]:
                                P.wait("gpsimd", "V", tq5)
                                P.dma("gpsimd", "dbg", dbg["q"], btb[:, j * 2048:(j + 1) * 2048])
                if pname == "ig":
                    for j, tc in enumerate(tcs):
                        tbt = P.cnt["V"]
                        P.wait("sync", "V", tbt)
                        t_btsp = P.dma("sync", f"dsb{j}", bt_sp[tc * 128:(tc + 1) * 128, :],
                                       btb[:, j * 2048:(j + 1) * 2048])
                        if "bt" in _DBG and tc == _DBG_SEL[1]:
                            P.wait("gpsimd", "V", tbt)
                            P.dma("gpsimd", "dbg", dbg["bt"], btb[:, j * 2048:(j + 1) * 2048])
                        P.wait("scalar", "T", P.cnt["T"])
                        P.emit("scalar", "S", lambda e, j=j: e.activation(
                            gamv[:], P2[:, j * 16:(j + 1) * 16], AF.Sigmoid))
                        P.wait("vector", "S", P.cnt["S"])
                        if "gam" in _DBG and tc == _DBG_SEL[1]:
                            P.tap(dbg["gam"], gamv[:], "S", P.cnt["S"])
                        tv6 = P.emit("vector", "V", lambda e: e.tensor_scalar_max(gamv[:], gamv[:], 1e-30))
                        P.wait("scalar", "V", tv6)
                        ts6 = P.emit("scalar", "S", lambda e: e.activation(lgv[:], gamv[:], AF.Ln))
                        t_gamma_evac_prev = ts6
                        if "lg" in _DBG and tc == _DBG_SEL[1]:
                            P.tap(dbg["lg"], lgv[:], "S", ts6)
                        P.wait("tensor", "S", ts6)
                        tt6 = P.emit("tensor", "T", lambda e: e.matmul(
                            P2[:, 80:96], trilR[:], lgv[:], start=True, stop=True))
                        P.wait("scalar", "T", tt6)
                        ts7 = P.emit("scalar", "S", lambda e: e.activation(lsv[:], P2[:, 80:96], AF.Identity))
                        if "lsv" in _DBG and tc == _DBG_SEL[1]:
                            P.tap(dbg["lsv"], lsv[:], "S", ts7)
                        P.wait("tensor", "S", ts7)
                        P.waitp("tensor", pt4_free[3])
                        tt7 = P.emit("tensor", "T", lambda e: e.transpose(PT4[0:16, 384:512], lsv[:], eyeR[:]))
                        pt4_used[3] = tt7
                        P.wait("scalar", "T", tt7)
                        ts8 = P.emit("scalar", "S", lambda e, tc=tc: e.activation(
                            LT_all[:, tc * 128:(tc + 1) * 128], PT4[0:16, 384:512], AF.Identity))
                        pt4_free[3] = ("S", ts8)
                        if "lt1" in _DBG and tc == _DBG_SEL[1]:
                            P.tap(dbg["lt1"], LT_all[:], "S", ts8)
                if pname == "og":
                    for j, tc in enumerate(tcs):
                        tev = P.cnt["S"]
                        P.wait("sync", "S", tev)
                        P.dma("sync", f"dsv{j}", og_sp[(tc - TC0) * 128:(tc - TC0) * 128 + 128, :],
                              v_sb[:, j * 2048:(j + 1) * 2048])
                        if "og" in _DBG and tc == _DBG_SEL[1]:
                            P.wait("gpsimd", "S", tev)
                            P.dma("gpsimd", "dbg", dbg["og"], v_sb[:, j * 2048:(j + 1) * 2048])
            t_proj_done_sp = P.cnt["T"]
        P.barrier()
    if _PHASES <= 2:
        return _finalize(nc, P)

    if "ltall" in _DBG:
        P.tap(dbg["ltall"], LT_all[:], "S", P.cnt["S"])

    with contextlib.ExitStack() as phB:
        bsb = lambda name, shape, dt: phB.enter_context(nc.sbuf_tensor(name, shape, dt))
        TAb = bsb("TAb", [2, 2048], F32)
        TBb = bsb("TBb", [2, 2048], F32)
        pexp_a = bsb("pexp_a", [16, NTC * 128], BF16)
        pflat = bsb("pflat", [1, 2048], BF16)
        maskrep = bsb("maskrep", [128, 2048], F32)
        mnwb = bsb("mnwb", [128, 2048], BF16)
        mnbb = bsb("mnbb", [128, 2048], BF16)
        gnwb = bsb("gnwb", [128, 2048], BF16)
        gnbb = bsb("gnbb", [128, 2048], BF16)
        scrAB = bsb("scrA2", [128, 2048], F32R)
        scrBB = bsb("scrB2", [128, 2048], F32R)
        mm_sb = bsb("mm_sb", [128, 2048], BF16)
        btld = bsb("btld", [128, 2048], BF16)
        qld = bsb("qld", [128, 2048], BF16)
        ogld = bsb("ogld", [128, 2048], BF16)
        z_sb = bsb("z_sb", [128, 2048], BF16)

        for dst, srcd in ((mnwb, mnw_d), (mnbb, mnb_d), (gnwb, gnw_d), (gnbb, gnb_d)):
            src_b = bass.AP(srcd.tensor, 0, [[0, 128], [1, 2048]])
            P.dma("gpsimd", "dbc", dst[:], src_b)
        src_m = bass.AP(maskm_d.tensor, 0, [[128, 128], [0, 16], [1, 128]])
        t_bcm = P.dma("gpsimd", "dbc", maskrep[:].rearrange("p (s t) -> p s t", s=16), src_m)
        t_bc = P.cnt["dbc"]
        tmem0 = P.emit("vector", "V", lambda e: e.memset(TAb[:], 1.0))
        tmem = P.emit("vector", "V", lambda e: e.memset(TBb[:], -1.0))
        P.wait("vector", "V", tmem)
        tpe = P.emit("scalar", "S", lambda e: e.activation(pexp_a[:], LT_all[:], AF.Exp))

        t_dmm_prev = 0
        t_scan_prev = 0
        t_memfree = None
        t_srow_prev = 0
        t_zf_prev = 0
        t_gn_prev = 0
        t_exp_prev = 0

        for tc in range(NTC):
            own = tc >= TC0
            j = tc - TC0
            P.wait("sync", "T", t_dmm_prev)
            P.wait("sync", "V", tmem)
            P.wait("sync", "S", tpe)
            tta = P.dma("sync", "dta", TAb[1:2, :], LT_all[:, tc * 128:(tc + 1) * 128])
            tfl = P.dma("sync", "dtb", TBb[0:1, :], LT_all[:, tc * 128:(tc + 1) * 128])
            P.wait("sync", "T", t_scan_prev)
            tpfl = P.dma("sync", "dpf", pflat[0:1, :], pexp_a[:, tc * 128:(tc + 1) * 128])
            tbt = P.dma("sync", "dbl", btld[:], bt_sp[tc * 128:(tc + 1) * 128, :])
            tqog = 0
            if own:
                P.wait("sync", "V", t_gn_prev)
                tq_ld = P.dma("sync", "dql", qld[:], q_sp[j * 128:(j + 1) * 128, :])
                tqog = P.dma("sync", "dol", ogld[:], og_sp[j * 128:(j + 1) * 128, :])
            P.wait("tensor", "dta", tta)
            P.wait("tensor", "dtb", tfl)
            P.waitp("tensor", t_memfree)
            for h in range(16):
                pap = (P0 if h < 8 else P1)[:, (h % 8) * 128:(h % 8) * 128 + 128]
                fn = lambda e, h=h, pap=pap: e.matmul(
                    pap, TAb[0:2, h * 128:(h + 1) * 128], TBb[0:2, h * 128:(h + 1) * 128],
                    start=True, stop=True)
                if h == 15:
                    t_dmm = P.emit("tensor", "T", fn)
                else:
                    P.emitq("tensor", fn)
            t_dmm_prev = t_dmm
            P.wait("vector", "T", t_dmm)
            P.wait("vector", "S", t_exp_prev)
            P.wait("vector", "dbc", t_bcm)
            P.emitq("vector", lambda e: e.scalar_tensor_tensor(
                scrAB[:, 0:1024], P0[:], 1.0, maskrep[:, 0:1024], ALU.mult, ALU.add))
            tmsk = P.emit("vector", "V", lambda e: e.scalar_tensor_tensor(
                scrAB[:, 1024:2048], P1[:], 1.0, maskrep[:, 1024:2048], ALU.mult, ALU.add))
            P.wait("scalar", "V", tmsk)
            t_exp_prev = P.emit("scalar", "S", lambda e: e.activation(mm_sb[:], scrAB[:], AF.Exp))
            if "mm" in _DBG and tc == _DBG_SEL[1]:
                P.tap(dbg["mm"], mm_sb[:], "S", t_exp_prev)
            P.wait("tensor", "S", t_exp_prev)
            P.wait("tensor", "dbl", tbt)
            P.wait("tensor", "dpf", tpfl)
            P.wait("tensor", "V", tmsk)
            P.wait("tensor", "S", t_srow_prev)
            srow_prev = srowB if tc % 2 == 1 else srowA
            srow_cur = srowA if tc % 2 == 1 else srowB
            for h in range(16):
                pap = (P0 if h < 8 else P1)[:, (h % 8) * 128:(h % 8) * 128 + 128]
                fn = lambda e, h=h, pap=pap, st=(tc == 0): e.matmul(
                    pap, mm_sb[:, h * 128:(h + 1) * 128], btld[:, h * 128:(h + 1) * 128],
                    start=True, stop=st)
                fnc = (None if tc == 0 else
                       (lambda e, h=h, pap=pap, sp_=srow_prev: e.matmul(
                           pap, pflat[0:1, h * 128:(h + 1) * 128],
                           sp_[0:1, h * 128:(h + 1) * 128], start=False, stop=True)))
                if fnc is None:
                    if h == 15:
                        t_scan = P.emit("tensor", "T", fn)
                    else:
                        P.emitq("tensor", fn)
                else:
                    P.emitq("tensor", fn)
                    if h == 15:
                        t_scan = P.emit("tensor", "T", fnc)
                    else:
                        P.emitq("tensor", fnc)
            t_scan_prev = t_scan
            t_sx = 0
            for r in range(4):
                P.wait("tensor", "S", t_sx)
                for hh in range(4):
                    h = r * 4 + hh
                    P.emitq("tensor", lambda e, h=h, hh=hh, st=(tc == 0): e.matmul(
                        P2[0:1, hh * 128:(hh + 1) * 128],
                        mm_sb[:, h * 128 + 127:h * 128 + 128],
                        btld[:, h * 128:(h + 1) * 128], start=True, stop=st))
                    if tc > 0:
                        fn = lambda e, h=h, hh=hh, sp_=srow_prev: e.matmul(
                            P2[0:1, hh * 128:(hh + 1) * 128],
                            pflat[0:1, h * 128 + 127:h * 128 + 128],
                            sp_[0:1, h * 128:(h + 1) * 128], start=False, stop=True)
                    else:
                        fn = None
                    if fn is not None:
                        if hh == 3:
                            t_xmm = P.emit("tensor", "T", fn)
                        else:
                            P.emitq("tensor", fn)
                if tc == 0:
                    t_xmm = P.emit("tensor", "T", lambda e, r=r: e.matmul(
                        P2[0:1, 3 * 128:4 * 128],
                        mm_sb[:, (r * 4 + 3) * 128 + 127:(r * 4 + 3) * 128 + 128],
                        btld[:, (r * 4 + 3) * 128:(r * 4 + 4) * 128], start=True, stop=True))
                P.wait("scalar", "T", t_xmm)
                t_sx = P.emit("scalar", "S", lambda e, r=r, sc=srow_cur: e.activation(
                    sc[0:1, r * 512:(r + 1) * 512], P2[0:1, 0:512], AF.Identity))
            t_srow_prev = t_sx
            t_scan_prev = t_xmm
            if "srow" in _DBG and tc == _DBG_SEL[1]:
                P.tap(dbg["srow"], srow_cur[0:1, :], "S", t_srow_prev)

            if not own:
                if "mem" in _DBG and tc == _DBG_SEL[1]:
                    P.wait("vector", "T", t_scan)
                    P.emitq("vector", lambda e: e.tensor_copy(scrBB[:, 0:1024], P0[:]))
                    tvm = P.emit("vector", "V", lambda e: e.tensor_copy(scrBB[:, 1024:2048], P1[:]))
                    P.tap(dbg["mem"], scrBB[:], "V", tvm)
                    t_memfree = ("V", tvm)
                else:
                    t_memfree = ("S", t_srow_prev)
                continue

            P.wait("vector", "T", t_scan)
            P.wait("scalar", "T", t_scan)
            if "mem" in _DBG and tc == _DBG_SEL[1]:
                P.emitq("vector", lambda e: e.tensor_copy(scrBB[:, 0:1024], P0[:]))
                tvm = P.emit("vector", "V", lambda e: e.tensor_copy(scrBB[:, 1024:2048], P1[:]))
                P.tap(dbg["mem"], scrBB[:], "V", tvm)
            p03d = P0[:].rearrange("p (s d) -> p s d", s=8)
            p13d = P1[:].rearrange("p (s d) -> p s d", s=8)
            sa3 = scrAB[:].rearrange("p (s d) -> p s d", s=16)
            sb3 = scrBB[:].rearrange("p (s d) -> p s d", s=16)
            P.emitq("vector", lambda e: e.tensor_reduce(
                smalls["g1s"][:, 0:8].rearrange("p s -> p s ()"), p03d, AX.X, ALU.add))
            tvr = P.emit("vector", "V", lambda e: e.tensor_reduce(
                smalls["g1s"][:, 8:16].rearrange("p s -> p s ()"), p13d, AX.X, ALU.add))
            P.wait("scalar", "V", tvr)
            P.emitq("scalar", lambda e: e.activation(scrBB[:, 0:1024], P0[:], AF.Square))
            ts9 = P.emit("scalar", "S", lambda e: e.activation(scrBB[:, 1024:2048], P1[:], AF.Square))
            P.wait("vector", "S", ts9)
            P.emitq("vector", lambda e: e.tensor_reduce(
                smalls["g2s"][:].rearrange("p s -> p s ()"), sb3, AX.X, ALU.add))

            def gn_stats(pref):
                P.schain("vector", lambda e: e.tensor_scalar_mul(smalls["gmu"][:], smalls["g1s"][:], 1.0 / 128))
                t_g0 = P.emit("vector", "V", lambda e: e.tensor_scalar_mul(smalls["gms"][:], smalls["g2s"][:], 1.0 / 128))
                P.wait("scalar", "V", t_g0)
                t_g1 = P.emit("scalar", "S", lambda e: e.activation(smalls["gm2"][:], smalls["gmu"][:], AF.Square))
                P.wait("vector", "S", t_g1)
                P.schain("vector", lambda e: e.tensor_sub(smalls["gva"][:], smalls["gms"][:], smalls["gm2"][:]))
                P.schain("vector", lambda e: e.tensor_scalar_add(smalls["gva"][:], smalls["gva"][:], EPS))
                P.schain("vector", lambda e: e.tensor_copy(smalls[pref + "n"][:], smalls["gmu"][:]))
                P.wait("vector", "V", P.cnt["V"])
                tva = P.emit("vector", "V", lambda e: e.reciprocal(smalls["grv"][:], smalls["gva"][:]))
                P.wait("scalar", "V", tva)
                tsa = P.emit("scalar", "S", lambda e, p=pref: e.activation(
                    smalls[p + "r"][:], smalls["grv"][:], AF.Sqrt))
                return tsa

            tsa1 = gn_stats("g1")
            P.wait("vector", "S", tsa1)
            P.wait("vector", "dbc", t_bc)
            mu13 = smalls["g1n"][:].rearrange("p s -> p s ()")
            r13 = smalls["g1r"][:].rearrange("p s -> p s ()")
            P.emitq("vector", lambda e: e.tensor_tensor(
                scrAB[:, 0:1024].rearrange("p (s d) -> p s d", s=8), p03d,
                bass.broadcast_tensor_aps(smalls["g1n"][:, 0:8].rearrange("p s -> p s ()"), p03d)[0],
                ALU.subtract))
            tv9 = P.emit("vector", "V", lambda e: e.tensor_tensor(
                scrAB[:, 1024:2048].rearrange("p (s d) -> p s d", s=8), p13d,
                bass.broadcast_tensor_aps(smalls["g1n"][:, 8:16].rearrange("p s -> p s ()"), p13d)[0],
                ALU.subtract))
            t_memfree = ("V", tv9)
            P.emitq("vector", lambda e: e.tensor_tensor(
                sb3, sa3, bass.broadcast_tensor_aps(r13, sa3)[0], ALU.mult))
            P.emitq("vector", lambda e: e.scalar_tensor_tensor(
                scrAB[:], scrBB[:], 1.0, mnwb[:], ALU.mult, ALU.mult))
            P.emitq("vector", lambda e: e.tensor_add(scrBB[:], scrAB[:], mnbb[:]))
            P.wait("vector", "dql", tq_ld)
            P.wait("vector", "dol", tqog)
            tv10 = P.emit("vector", "V", lambda e: e.tensor_mul(scrAB[:], scrBB[:], qld[:]))
            P.emitq("vector", lambda e: e.tensor_reduce(
                smalls["g1s"][:].rearrange("p s -> p s ()"), sa3, AX.X, ALU.add))
            P.wait("scalar", "V", tv10)
            ts10 = P.emit("scalar", "S", lambda e: e.activation(scrBB[:], scrAB[:], AF.Square))
            P.wait("vector", "S", ts10)
            P.emitq("vector", lambda e: e.tensor_reduce(
                smalls["g2s"][:].rearrange("p s -> p s ()"), sb3, AX.X, ALU.add))
            tsa2 = gn_stats("g2")
            P.wait("vector", "S", tsa2)
            mu23 = smalls["g2n"][:].rearrange("p s -> p s ()")
            r23 = smalls["g2r"][:].rearrange("p s -> p s ()")
            P.emitq("vector", lambda e: e.tensor_tensor(
                sb3, sa3, bass.broadcast_tensor_aps(mu23, sa3)[0], ALU.subtract))
            P.emitq("vector", lambda e: e.tensor_tensor(
                sa3, sb3, bass.broadcast_tensor_aps(r23, sb3)[0], ALU.mult))
            P.emitq("vector", lambda e: e.scalar_tensor_tensor(
                scrBB[:], scrAB[:], 1.0, gnwb[:], ALU.mult, ALU.mult))
            P.emitq("vector", lambda e: e.tensor_add(scrAB[:], scrBB[:], gnbb[:]))
            tz = P.emit("vector", "V", lambda e: e.tensor_mul(z_sb[:], scrAB[:], ogld[:]))
            t_gn_prev = tz
            if "z" in _DBG and tc == _DBG_SEL[1]:
                P.tap(dbg["z"], z_sb[:], "V", tz)
            P.wait("tensor", "V", tz)
            P.wait("tensor", "S", t_srow_prev)
            for c in range(16):
                fn = lambda e, c=c: e.transpose(
                    PT5[:, c * 128:(c + 1) * 128], z_sb[:, c * 128:(c + 1) * 128], eyeB[:])
                if c == 15:
                    tzf = P.emit("tensor", "T", fn)
                else:
                    P.emitq("tensor", fn)
            t_scan_prev = tzf
            P.wait("vector", "T", tzf)
            P.wait("vector", "dzf", t_zf_prev)
            tzv = P.emit("vector", "V", lambda e: e.tensor_copy(z_sb[:], PT5[:]))
            P.wait("sync", "V", tzv)
            t_zf_prev = P.dma("sync", "dzf", zf_sp[j * 128:(j + 1) * 128, :], z_sb[:])
            P.wait("tensor", "V", tzv)
        P.barrier()
    span_es.close()
    if _PHASES <= 3:
        return _finalize(nc, P)

    with contextlib.ExitStack() as phW:
        wsb = lambda name, shape, dt: phW.enter_context(nc.sbuf_tensor(name, shape, dt))
        wof = wsb("wof", [128, 16 * 2048], BF16)
        zfl = wsb("zfl", [128, 2048], BF16)
        xol = wsb("xol", [128, 2048], F32)
        x2b = wsb("x2b", [128, 2048], F32)
        scrW = wsb("scrW", [128, 2048], F32R)

        tw = 0
        for k in range(16):
            tw = P.dma("gpsimd", "dwo", wof[:, k * 2048:(k + 1) * 2048],
                       wo_t[k * 128:(k + 1) * 128, :])
        t_x2_prev = 0
        t_wodone = 0
        t_wo_mm_prev = 0
        for j in range(16):
            P.wait("sync", "T", t_wo_mm_prev)
            tz = P.dma("sync", "dzl", zfl[:], zf_sp[j * 128:(j + 1) * 128, :])
            if "zfl" in _DBG and j == _DBG_SEL[1] - TC0:
                P.tap(dbg["zfl"], zfl[:], "dzl", tz)
            P.wait("sync", "V", t_wodone)
            tx = P.dma("sync", "dxo", xol[:], xown[j * 128:(j + 1) * 128, :])
            P.wait("tensor", "dwo", tw)
            P.wait("tensor", "dzl", tz)
            P.wait("tensor", "V", t_wodone)
            for sl in range(4):
                pap = (P0 if sl < 2 else P1)[:, (sl % 2) * 512:(sl % 2) * 512 + 512]
                for k in range(16):
                    fn = lambda e, k=k, sl=sl, pap=pap: e.matmul(
                        pap, zfl[:, k * 128:(k + 1) * 128],
                        wof[:, k * 2048 + sl * 512:k * 2048 + (sl + 1) * 512],
                        start=(k == 0), stop=(k == 15))
                    if sl == 3 and k == 15:
                        twm = P.emit("tensor", "T", fn)
                    else:
                        P.emitq("tensor", fn)
            t_wo_mm_prev = twm
            P.wait("vector", "T", twm)
            for sl in range(4):
                pap = (P0 if sl < 2 else P1)[:, (sl % 2) * 512:(sl % 2) * 512 + 512]
                fn = lambda e, sl=sl, pap=pap: e.scalar_tensor_tensor(
                    scrW[:, sl * 512:(sl + 1) * 512], pap, 1.0,
                    gab[:, sl * 512:(sl + 1) * 512], ALU.mult, ALU.mult)
                if sl == 3:
                    t_scrw = P.emit("vector", "V", fn)
                else:
                    P.emitq("vector", fn)
            if "scrw" in _DBG and j == _DBG_SEL[1] - TC0:
                P.tap(dbg["scrw"], scrW[:], "V", t_scrw)
            if "gab" in _DBG and j == _DBG_SEL[1] - TC0:
                P.tap(dbg["gab"], gab[:], "V", t_scrw)
            P.wait("vector", "dxo", tx)
            P.wait("vector", "dx2s", t_x2_prev)
            t_wodone = P.emit("vector", "V", lambda e: e.tensor_add(x2b[:], scrW[:], xol[:]))
            P.wait("sync", "V", t_wodone)
            t_x2_prev = P.dma("sync", "dx2s", x2_sp[j * 128:(j + 1) * 128, :], x2b[:])
            if "x2" in _DBG and j == _DBG_SEL[1] - TC0:
                P.wait("gpsimd", "V", t_wodone)
                P.dma("gpsimd", "dbg", dbg["x2"], x2b[:])
        P.barrier()
    if _PHASES <= 4:
        return _finalize(nc, P)

    with contextlib.ExitStack() as phD:
        dsb = lambda name, shape, dt: phD.enter_context(nc.sbuf_tensor(name, shape, dt))
        x2l0 = dsb("x2l0", [128, 2048], F32)
        scrD0 = dsb("scrD0", [128, 2048], F32R)
        xn2b = dsb("xn2b", [128, 2048], BF16)
        h2t = dsb("h2t", [128, 2048], FP8)

        t_h2dma_prev = 0
        t_xn2_prev = 0
        t_trD_prev = [0]
        pt5D_free = [None, None]
        t_h2_prev = 0
        for j in range(16):
            P.wait("sync", "S", t_xn2_prev)
            tx2 = P.dma("sync", "dxl0", x2l0[:], x2_sp[j * 128:(j + 1) * 128, :])
            P.wait("scalar", "dxl0", tx2)
            P.wait("scalar", "T", t_trD_prev[0])
            P.emitq("scalar", lambda e: e.activation(
                scrD0[:], x2l0[:], AF.Square, accum_out=smalls["st2"][:, 0:1]))
            t_st = P.emit("scalar", "S", lambda e: e.activation(
                scrD0[:], x2l0[:], AF.Identity, accum_out=smalls["st1"][:, 0:1]))
            P.wait("vector", "S", t_st)
            P.schain("vector", lambda e: e.tensor_scalar_mul(smalls["mu"][:, 0:1], smalls["st1"][:, 0:1], 1.0 / 2048))
            t_v0 = P.emit("vector", "V", lambda e: e.tensor_scalar_mul(smalls["ms"][:, 0:1], smalls["st2"][:, 0:1], 1.0 / 2048))
            P.wait("scalar", "V", t_v0)
            t_s0 = P.emit("scalar", "S", lambda e: e.activation(smalls["mu2"][:, 0:1], smalls["mu"][:, 0:1], AF.Square))
            P.wait("vector", "S", t_s0)
            P.schain("vector", lambda e: e.tensor_sub(smalls["var"][:, 0:1], smalls["ms"][:, 0:1], smalls["mu2"][:, 0:1]))
            P.schain("vector", lambda e: e.tensor_scalar_add(smalls["var"][:, 0:1], smalls["var"][:, 0:1], EPS))
            P.wait("vector", "V", P.cnt["V"])
            tv1 = P.emit("vector", "V", lambda e: e.reciprocal(smalls["rv"][:, 0:1], smalls["var"][:, 0:1]))
            P.wait("scalar", "V", tv1)
            ts1 = P.emit("scalar", "S", lambda e: e.activation(smalls["rr"][:, 0:1], smalls["rv"][:, 0:1], AF.Sqrt))
            P.wait("vector", "S", ts1)
            tv2 = P.emit("vector", "V", lambda e: e.scalar_tensor_tensor(
                smalls["nmr"][:, 0:1], smalls["mu"][:, 0:1], -1.0, smalls["rr"][:, 0:1], ALU.mult, ALU.mult))
            P.wait("scalar", "V", tv2)
            t_xn2 = P.emit("scalar", "S", lambda e: e.activation(
                scrD0[:], x2l0[:], AF.Identity, bias=smalls["nmr"][:, 0:1], scale=smalls["rr"][:, 0:1]))
            t_xn2_prev = t_xn2
            P.wait("tensor", "S", t_xn2)
            tve = 0
            for cg in range(4):
                for c in range(cg * 4, cg * 4 + 4):
                    q4 = c % 4
                    fn = lambda e, c=c, q4=q4: e.transpose(
                        PT4[:, q4 * 128:(q4 + 1) * 128], scrD0[:, c * 128:(c + 1) * 128], eyeR[:])
                    if c % 4 == 3:
                        ttg = P.emit("tensor", "T", fn)
                    else:
                        P.emitq("tensor", fn)
                P.wait("vector", "T", ttg)
                P.wait("vector", "dh2", t_h2dma_prev)
                for c in range(cg * 4, cg * 4 + 4):
                    tve = P.emit("vector", "V", lambda e, c=c: e.tensor_scalar(
                        h2t[:, c * 128:(c + 1) * 128], PT4[:, (c % 4) * 128:(c % 4) * 128 + 128],
                        cbm[:, c:c + 1], sbm[:, c:c + 1], ALU.mult, ALU.add))
                P.wait("tensor", "V", tve)
                t_trD_prev[0] = P.cnt["T"]
            P.wait("sync", "V", tve)
            for cs in range(16):
                t_h2dma_prev = P.dma(
                    "sync", "dh2",
                    h2_sp[cs * 128:(cs + 1) * 128, j * 128:(j + 1) * 128],
                    h2t[:, cs * 128:(cs + 1) * 128])
            if "h2" in _DBG and j == _DBG_SEL[1] - TC0:
                tvd = P.emit("vector", "V", lambda e: e.tensor_copy(scrD0[:], h2t[:]))
                P.wait("gpsimd", "V", tvd)
                P.dma("gpsimd", "dbg", dbg["h2"], scrD0[:])
        P.barrier()
    if _PHASES <= 5:
        return _finalize(nc, P)

    with contextlib.ExitStack() as phD2:
        dsb = lambda name, shape, dt: phD2.enter_context(nc.sbuf_tensor(name, shape, dt))
        h2th = dsb("h2th", [128, 16 * 1024], FP8)
        a_sb = dsb("a_sb", [128, 64 * 1024], FP8)
        wgb = dsb("wgb", [128, 2 * 2 * 2048], FP8)
        wlb = dsb("wlb", [128, 2 * 64 * 128], FP8)
        scrDF = dsb("scrD2", [128, 1024], F32R)
        ybf = dsb("ybf", [128, 1024], BF16)
        ytm = dsb("ytm", [128, 8 * 2048], BF16)
        x2lF = dsb("x2l2", [128, 2048], F32)
        outb = dsb("outb", [128, 2048], F32)

        t_gate_mm_prev = 0
        for th in range(2):
            P.wait("sync", "T", t_gate_mm_prev)
            for ks in range(16):
                th2 = P.dma("sync", "dht",
                            h2th[:, ks * 1024:(ks + 1) * 1024],
                            h2_sp[ks * 128:(ks + 1) * 128, th * 1024:(th + 1) * 1024])
            t_gevac = 0
            t_wg_used = [0, 0]
            tgm = 0
            for hc in range(64):
                slot = hc % 2
                P.wait("gpsimd", "T", t_wg_used[slot])
                P.dma("gpsimd", f"dwg{slot}",
                      wgb[:, slot * 4096:slot * 4096 + 2048].rearrange("p (k q) -> p k q", k=16),
                      wg0_t[hc].rearrange("k p q -> p k q"))
                twg = P.dma("gpsimd", f"dwg{slot}",
                            wgb[:, slot * 4096 + 2048:slot * 4096 + 4096].rearrange("p (k q) -> p k q", k=16),
                            wg1_t[hc].rearrange("k p q -> p k q"))
                P.wait("tensor", f"dwg{slot}", twg)
                P.wait("tensor", "dht", th2)
                P.wait("tensor", "V", t_gevac)
                for k in range(16):
                    for half in range(2):
                        P.emitq("tensor", lambda e, k=k, half=half, slot=slot: e.matmul(
                            P0[:, half * 512:(half + 1) * 512],
                            wgb[:, slot * 4096 + k * 128:slot * 4096 + (k + 1) * 128],
                            h2th[:, k * 1024 + half * 512:k * 1024 + (half + 1) * 512],
                            start=(k == 0), stop=(k == 15)))
                for k in range(16):
                    for half in range(2):
                        fn = lambda e, k=k, half=half, slot=slot: e.matmul(
                            P1[:, half * 512:(half + 1) * 512],
                            wgb[:, slot * 4096 + 2048 + k * 128:slot * 4096 + 2048 + (k + 1) * 128],
                            h2th[:, k * 1024 + half * 512:k * 1024 + (half + 1) * 512],
                            start=(k == 0), stop=(k == 15))
                        if k == 15 and half == 1:
                            tgm = P.emit("tensor", "T", fn)
                        else:
                            P.emitq("tensor", fn)
                t_wg_used[slot] = tgm
                P.wait("scalar", "T", tgm)
                tgg = P.emit("scalar", "S", lambda e: e.activation(scrDF[:], P0[:], AF.Gelu_apprx_tanh))
                P.wait("vector", "S", tgg)
                t_gevac = P.emit("vector", "V", lambda e, hc=hc: e.tensor_mul(
                    a_sb[:, hc * 1024:(hc + 1) * 1024], scrDF[:], P1[:]))
            t_gate_mm_prev = tgm
            t_wl_used = [0, 0]
            t_yevac = t_gevac
            t_out_prev = 0
            t_ycp = 0
            for oc in range(16):
                slot = oc % 2
                P.wait("gpsimd", "T", t_wl_used[slot])
                twl = P.dma("gpsimd", f"dwl{slot}",
                            wlb[:, slot * 8192:(slot + 1) * 8192].rearrange("p (h q) -> p h q", h=64),
                            wl_c[oc].rearrange("h p q -> p h q"))
                P.wait("tensor", f"dwl{slot}", twl)
                P.wait("tensor", "V", t_yevac)
                for hc in range(64):
                    for half in range(2):
                        fn = lambda e, hc=hc, half=half, slot=slot: e.matmul(
                            P0[:, half * 512:(half + 1) * 512],
                            wlb[:, slot * 8192 + hc * 128:slot * 8192 + (hc + 1) * 128],
                            a_sb[:, hc * 1024 + half * 512:hc * 1024 + (half + 1) * 512],
                            start=(hc == 0), stop=(hc == 63))
                        if hc == 63 and half == 1:
                            tym = P.emit("tensor", "T", fn)
                        else:
                            P.emitq("tensor", fn)
                t_wl_used[slot] = tym
                P.wait("vector", "T", tym)
                tyv = P.emit("vector", "V", lambda e, oc=oc: e.tensor_scalar_mul(
                    ybf[:], P0[:], gbm_f[:, oc:oc + 1]))
                t_yevac = tyv
                P.wait("tensor", "V", tyv)
                P.wait("tensor", "V", t_ycp)
                for tt8 in range(8):
                    fn = lambda e, tt8=tt8: e.transpose(
                        PT5[:, tt8 * 128:(tt8 + 1) * 128], ybf[:, tt8 * 128:(tt8 + 1) * 128], eyeB[:])
                    if tt8 == 7:
                        tyt = P.emit("tensor", "T", fn)
                    else:
                        P.emitq("tensor", fn)
                P.wait("vector", "T", tyt)
                for tt8 in range(8):
                    fn = lambda e, tt8=tt8, oc=oc: e.tensor_copy(
                        ytm[:, tt8 * 2048 + oc * 128:tt8 * 2048 + (oc + 1) * 128],
                        PT5[:, tt8 * 128:(tt8 + 1) * 128])
                    if tt8 == 7:
                        t_ycp = P.emit("vector", "V", fn)
                    else:
                        P.emitq("vector", fn)
            tvo = 0
            for t8 in range(8):
                j = th * 8 + t8
                P.wait("sync", "V", tvo)
                tx2 = P.dma("sync", "dxl2", x2lF[:], x2_sp[j * 128:(j + 1) * 128, :])
                P.wait("vector", "dxl2", tx2)
                P.wait("vector", "dou", t_out_prev)
                P.wait("vector", "V", t_ycp)
                tvo = P.emit("vector", "V", lambda e, t8=t8: e.tensor_add(
                    outb[:], ytm[:, t8 * 2048:(t8 + 1) * 2048], x2lF[:]))
                P.wait("sync", "V", tvo)
                t_out_prev = P.dma("sync", "dou", out[j * 128:(j + 1) * 128, :], outb[:])
        P.barrier()

    return _finalize(nc, P)


def _shared_arrays(inputs):
    sh = {}
    wmod = inputs["L0_wmod"]
    sh["wmod_t"] = np.ascontiguousarray(wmod.reshape(16, 128, 96, 128).transpose(2, 0, 1, 3))
    for nm, key in (("wq_t", "L0_attn_Wq"), ("wk_t", "L0_attn_Wk"),
                    ("wv_t", "L0_attn_Wv"), ("wig_t", "L0_attn_Wig"),
                    ("wog_t", "L0_attn_Wog"), ("wo_t", "L0_attn_Wo")):
        sh[nm] = np.ascontiguousarray(inputs[key].T)
    sh["wg_t"] = np.ascontiguousarray(inputs["L0_attn_Wg"].T)
    sh["big"] = inputs["L0_attn_b_ig"].reshape(1, -1)
    sh["bog"] = inputs["L0_attn_b_og"].reshape(1, -1)
    sh["bgr"] = inputs["L0_attn_b_g"].reshape(1, -1)
    cw = inputs["L0_attn_conv_w"]
    sh["convw"] = np.ascontiguousarray(cw.reshape(16, 128, 4).transpose(1, 0, 2).reshape(128, 64))
    sh["convb"] = np.ascontiguousarray(inputs["L0_attn_conv_b"].reshape(16, 128).T)
    sh["mnw"] = inputs["L0_attn_mn_w"].reshape(1, -1)
    sh["mnb"] = inputs["L0_attn_mn_b"].reshape(1, -1)
    sh["gnw"] = inputs["L0_attn_gn_w"].reshape(1, -1)
    sh["gnb"] = inputs["L0_attn_gn_b"].reshape(1, -1)
    wg = inputs["L0_ffn_w_gating"]
    sh["wg0_t"] = np.ascontiguousarray(wg[0].reshape(16, 128, 64, 128).transpose(2, 0, 1, 3))
    sh["wg1_t"] = np.ascontiguousarray(wg[1].reshape(16, 128, 64, 128).transpose(2, 0, 1, 3))
    sh["wl_c"] = np.ascontiguousarray(
        inputs["L0_ffn_w_linear"].reshape(64, 128, 16, 128).transpose(2, 0, 1, 3))
    sh["eye"] = np.eye(128, dtype=np.float32)
    sh["eyeb"] = np.eye(128, dtype=np.float32)
    sh["tril"] = np.triu(np.ones((128, 128), np.float32))
    sh["maskm"] = np.where(np.triu(np.ones((128, 128), np.float32)) > 0, 0.0, NEGM).astype(np.float32)
    sh["ones_d"] = np.ones((1, 128), np.float32)
    sh["onesr_d"] = np.ones((1, 128), np.float32)
    return sh


def _core_inputs(inputs, sh, core):
    b, half = core // 2, core % 2
    x = inputs["L0_attn_x"]
    m = dict(sh)
    if half == 0:
        m["xpro"] = np.zeros((TOWN, D), np.float32)
        m["pmask"] = np.zeros((128, 1), np.float32)
    else:
        m["xpro"] = np.ascontiguousarray(x[b, 0:TOWN])
        m["pmask"] = np.ones((128, 1), np.float32)
    m["xown"] = np.ascontiguousarray(x[b, half * TOWN:(half + 1) * TOWN])
    m["cvec"] = np.ascontiguousarray(inputs["L0_c"][b].reshape(16, 128))
    return m


def kernel(**inputs):
    nc = build()
    sh = _shared_arrays(inputs)
    in_maps = [_core_inputs(inputs, sh, c) for c in range(8)]
    res = run_bass_kernel_spmd(nc, in_maps, core_ids=list(range(8)))
    out = np.zeros((4, 4096, 2048), np.float32)
    for core in range(8):
        b, half = core // 2, core % 2
        out[b, half * TOWN:(half + 1) * TOWN] = np.asarray(res.results[core]["out"], np.float32)
    return out


# version 03
# speedup vs baseline: 1.1296x; 1.1296x over previous
import contextlib
from collections import defaultdict

import numpy as np
import concourse.bass as bass
import concourse.mybir as mybir
from concourse.bass_utils import run_bass_kernel_spmd

F32 = mybir.dt.float32
F32R = mybir.dt.float32r
BF16 = mybir.dt.bfloat16
FP8 = mybir.dt.float8e4
AF = mybir.ActivationFunctionType
ALU = mybir.AluOpType
AX = mybir.AxisListType

D = 2048
NH = 16
TOWN = 2048
TALL = 4096
NTC = 32
TC0 = 16
SP = 512
NSP = TALL // SP
EPS = 1e-5
NEGM = -30000.0

ENGS = ("sync", "gpsimd", "tensor", "vector", "scalar")

_DBG = ()
_DBG_SEL = (4, 16)
_PHASES = 6


class Prog:
    def __init__(self, nc):
        self.nc = nc
        self.ops = {e: [] for e in ENGS}
        self.cnt = defaultdict(int)
        self.semh = {}
        self.last_wait = {e: defaultdict(int) for e in ENGS}
        self.es = contextlib.ExitStack()

    def sem(self, name):
        if name not in self.semh:
            self.semh[name] = self.es.enter_context(self.nc.semaphore(name))
        return self.semh[name]

    def op(self, eng, fn):
        self.ops[eng].append(fn)

    def wait(self, eng, sem_name, tgt):
        if tgt <= 0:
            return
        if self.last_wait[eng][sem_name] >= tgt:
            return
        self.last_wait[eng][sem_name] = tgt
        s = self.sem(sem_name)
        self.op(eng, lambda e, s=s, t=tgt: e.wait_ge(s, t))

    def waitp(self, eng, pair):
        if pair is not None:
            self.wait(eng, pair[0], pair[1])

    def run(self, eng_name, eng):
        for fn in self.ops[eng_name]:
            fn(eng)

    def dma(self, eng, sem_name, out, in_):
        s = self.sem(sem_name)
        self.cnt[sem_name] += 16
        self.op(eng, lambda e, o=out, i=in_, s=s: e.dma_start(out=o, in_=i).then_inc(s, 16))
        return self.cnt[sem_name]

    def emit(self, eng, sem_name, fn):
        s = self.sem(sem_name)
        self.cnt[sem_name] += 1
        self.op(eng, lambda e, f=fn, s=s: f(e).then_inc(s, 1))
        return self.cnt[sem_name]

    def emitq(self, eng, fn):
        self.op(eng, fn)

    def schain(self, eng, fn):
        sem = {"vector": "V", "scalar": "S", "gpsimd": "G"}[eng]
        self.wait(eng, sem, self.cnt[sem])
        return self.emit(eng, sem, fn)

    def tap(self, dst, src, wait_sem, wait_tgt):
        self.wait("gpsimd", wait_sem, wait_tgt)
        t = self.dma("gpsimd", "dbg", dst, src)
        for e in ENGS:
            self.wait(e, "dbg", t)

    def barrier(self):
        sems = [k for k, v in self.cnt.items() if v > 0]
        for e in ENGS:
            for s in sems:
                self.wait(e, s, self.cnt[s])


def _finalize(nc, P):
    for snm in list(P.cnt.keys()):
        if P.cnt[snm] > 0:
            P.wait("sync", snm, P.cnt[snm])
    if True:
        with nc.Block() as blk:
            @blk.sync
            def _(e):
                P.run("sync", e)

            @blk.gpsimd
            def _(e):
                P.run("gpsimd", e)

            @blk.tensor
            def _(e):
                P.run("tensor", e)

            @blk.vector
            def _(e):
                P.run("vector", e)

            @blk.scalar
            def _(e):
                P.run("scalar", e)
    return nc


def build():
    nc = bass.Bass("TRN2", target_bir_lowering=False, detect_race_conditions=False)
    P = Prog(nc)
    es = P.es
    sbt = lambda name, shape, dt: es.enter_context(nc.sbuf_tensor(name, shape, dt))
    pst = lambda name, shape, dt: es.enter_context(nc.psum_tensor(name, shape, dt))
    dram = lambda name, shape, dt, kind: nc.dram_tensor(name, shape, dt, kind=kind).ap()

    xpro = dram("xpro", [TOWN, D], F32, "ExternalInput")
    xown = dram("xown", [TOWN, D], F32, "ExternalInput")
    cvec = dram("cvec", [16, 128], F32R, "ExternalInput")
    pmask = dram("pmask", [128, 1], F32, "ExternalInput")
    wmod_t = dram("wmod_t", [96, 16, 128, 128], F32R, "ExternalInput")
    wq_t = dram("wq_t", [2048, 2048], F32, "ExternalInput")
    wk_t = dram("wk_t", [2048, 2048], F32, "ExternalInput")
    wv_t = dram("wv_t", [2048, 2048], F32, "ExternalInput")
    wig_t = dram("wig_t", [2048, 2048], F32, "ExternalInput")
    wog_t = dram("wog_t", [2048, 2048], F32, "ExternalInput")
    wo_t = dram("wo_t", [2048, 2048], F32, "ExternalInput")
    wg_t = dram("wg_t", [2048, 16], F32, "ExternalInput")
    big_d = dram("big", [1, 2048], F32, "ExternalInput")
    bog_d = dram("bog", [1, 2048], F32, "ExternalInput")
    bgr_d = dram("bgr", [1, 16], F32, "ExternalInput")
    convw_d = dram("convw", [128, 64], F32, "ExternalInput")
    convb_d = dram("convb", [128, 16], F32, "ExternalInput")
    mnw_d = dram("mnw", [1, 2048], F32, "ExternalInput")
    mnb_d = dram("mnb", [1, 2048], F32, "ExternalInput")
    gnw_d = dram("gnw", [1, 2048], F32, "ExternalInput")
    gnb_d = dram("gnb", [1, 2048], F32, "ExternalInput")
    wg0_t = dram("wg0_t", [64, 16, 128, 128], F32, "ExternalInput")
    wg1_t = dram("wg1_t", [64, 16, 128, 128], F32, "ExternalInput")
    wl_c = dram("wl_c", [16, 64, 128, 128], F32, "ExternalInput")
    eye_d = dram("eye", [128, 128], F32R, "ExternalInput")
    eyeb_d = dram("eyeb", [128, 128], F32, "ExternalInput")
    tril_d = dram("tril", [128, 128], F32R, "ExternalInput")
    maskm_d = dram("maskm", [128, 128], F32, "ExternalInput")
    ones_d = dram("ones_d", [1, 128], F32, "ExternalInput")
    onesr_d = dram("onesr_d", [1, 128], F32R, "ExternalInput")

    out = dram("out", [TOWN, D], F32, "ExternalOutput")
    dbg = {}
    for nm, shp in (("mod", [128, 96]), ("hfm", [128, 16 * (SP + 3)]),
                    ("xconv", [128, 16 * SP]), ("ltall", [16, NTC * 128]),
                    ("bt", [128, 2048]), ("mem", [128, 2048]),
                    ("gam", [128, 16]), ("lg", [128, 16]), ("lsv", [128, 16]),
                    ("halo", [128, 48]), ("hz0", [128, 48]), ("lt1", [16, 4096]),
                    ("lta5", [16, 4096]), ("lta6", [16, 4096]), ("lta7", [16, 4096]),
                    ("xnb", [128, 2048]), ("lnst", [128, 16]), ("mm", [128, 2048]),
                    ("z", [128, 2048]), ("x2", [128, 2048]),
                    ("h2", [128, 2048]), ("srow", [1, 2048]), ("zfl", [128, 2048]),
                    ("gab", [128, 2048]), ("scrw", [128, 2048]),
                    ("gaf", [128, 16]), ("gar", [1, 2048]),
                    ("kn", [128, 2048]), ("q", [128, 2048]), ("og", [128, 2048])):
        if nm in _DBG:
            dbg[nm] = dram("dbg_" + nm, shp, F32, "ExternalOutput")

    bt_sp = dram("bt_sp", [TALL, D], BF16, "Internal")
    q_sp = dram("q_sp", [TOWN, D], BF16, "Internal")
    og_sp = dram("og_sp", [TOWN, D], BF16, "Internal")
    zf_sp = dram("zf_sp", [TOWN, D], BF16, "Internal")
    x2_sp = dram("x2_sp", [TOWN, D], F32, "Internal")
    h2_sp = dram("h2_sp", [D, TOWN], FP8, "Internal")

    P0 = pst("P0", [128, 1024], F32)
    P1 = pst("P1", [128, 1024], F32)
    P2 = pst("P2", [128, 512], F32)
    PT4 = pst("PT4", [128, 512], F32R)
    PT5 = pst("PT5", [128, 2048], BF16)

    eyeR = sbt("eyeR", [128, 128], F32R)
    eyeB = sbt("eyeB", [128, 128], BF16)
    trilR = sbt("trilR", [128, 128], F32R)
    maskm = sbt("maskm_s", [128, 128], F32)
    onesbf = sbt("onesbf_s", [1, 128], BF16)
    onesr = sbt("onesr_s", [1, 128], F32R)
    convw = sbt("convw_s", [128, 64], F32)
    convb = sbt("convb_s", [128, 16], F32)
    wgt_s = sbt("wgt_s", [128, 256], BF16)
    bigrow = sbt("bigrow", [1, 2048], BF16)
    bogrow = sbt("bogrow", [1, 2048], BF16)
    bgrow = sbt("bgrow", [1, 16], BF16)
    pm_s = sbt("pm_s", [128, 1], F32)
    sc16 = sbt("sc16", [16, 128], F32R)
    sccols = sbt("sccols", [128, 16], F32R)
    mod_sb = sbt("mod_sb", [128, 96], F32)
    cam_pro = sbt("cam_pro", [128, 16], F32)
    sam_pro = sbt("sam_pro", [128, 16], F32)
    cam = sbt("cam", [128, 16], F32)
    sam = sbt("sam", [128, 16], F32)
    cbm = sbt("cbm", [128, 16], F32)
    sbm = sbt("sbm", [128, 16], F32)
    gam_f = sbt("gam_f", [128, 16], F32)
    gbm_f = sbt("gbm_f", [128, 16], F32)
    gab = sbt("gab", [128, 2048], F32)
    halo = sbt("halo", [128, 48], BF16)

    smalls = {}
    for nm in ("st1", "st2", "mu", "ms", "mu2", "var", "rv", "rr", "nmr",
               "ss", "ssn", "ssr",
               "g1s", "g2s", "g1n", "g1r", "g2n", "g2r", "gmu", "gms", "gm2",
               "gva", "grv", "gsa", "gsb", "gsc"):
        smalls[nm] = sbt("sm_" + nm, [128, 16], F32)

    P.dma("sync", "dc", eyeR[:], eye_d)
    P.dma("sync", "dc", trilR[:], tril_d)
    P.dma("sync", "dc", maskm[:], maskm_d)
    P.dma("sync", "dc", convw[:], convw_d)
    P.dma("sync", "dc", convb[:], convb_d)
    P.dma("sync", "dc", pm_s[:], pmask)
    P.dma("sync", "dc", sc16[:], cvec)
    t_cdx = P.dma("sync", "dc", onesr[:], onesr_d)
    P.dma("gpsimd", "dgc", eyeB[:], eyeb_d)
    P.dma("gpsimd", "dgc", onesbf[:], ones_d)
    P.dma("gpsimd", "dgc", bigrow[:], big_d)
    P.dma("gpsimd", "dgc", bogrow[:], bog_d)
    t_cdg = P.dma("gpsimd", "dgc", bgrow[:], bgr_d)
    for k in range(16):
        t_cdg = P.dma("gpsimd", "dgc", wgt_s[:, k * 16:(k + 1) * 16],
                      wg_t[k * 128:(k + 1) * 128, :])

    with contextlib.ExitStack() as ph0:
        wmbuf = ph0.enter_context(nc.sbuf_tensor("wmbuf", [128, 2 * 2048], F32R))
        garowS = ph0.enter_context(nc.sbuf_tensor("garowS", [16, 128], F32R))
        garow2 = ph0.enter_context(nc.sbuf_tensor("garow2", [1, 2048], F32R))
        gxm_fr = ph0.enter_context(nc.sbuf_tensor("gxm_fr", [128, 16], F32R))

        P.wait("scalar", "dc", t_cdx)
        t_silu = P.emit("scalar", "S", lambda e: e.activation(sc16[:], sc16[:], AF.Silu))
        P.wait("tensor", "S", t_silu)
        t_tr = P.emit("tensor", "T", lambda e: e.transpose(PT4[:, 0:16], sc16[:], eyeR[0:16, 0:16]))
        P.wait("scalar", "T", t_tr)
        t_sc = P.emit("scalar", "S", lambda e: e.activation(sccols[:], PT4[:, 0:16], AF.Identity))

        scpad = ph0.enter_context(nc.sbuf_tensor("scpad", [128, 32], F32R))
        t_mz = P.emit("vector", "V", lambda e: e.memzero(scpad[:]))
        P.wait("vector", "V", t_mz)
        P.wait("vector", "S", t_sc)
        t_scp = P.emit("vector", "V", lambda e: e.tensor_copy(
            scpad[:].rearrange("p (k two) -> p k two", two=2)[:, :, 0:1],
            sccols[:].rearrange("p k -> p k ()")))
        P.wait("tensor", "V", t_scp)
        t_mm_hist = {}
        t_mm = 0
        for oc in range(96):
            slot = oc % 2
            if oc >= 2:
                P.wait("sync", "T", t_mm_hist[oc - 2])
            tgt = P.dma("sync", f"dw{slot}",
                        wmbuf[:, slot * 2048:(slot + 1) * 2048].rearrange("p (k q) -> p k q", k=16),
                        wmod_t[oc].rearrange("k p q -> p k q"))
            P.wait("tensor", f"dw{slot}", tgt)
            for k in range(16):
                fn = (lambda e, oc=oc, k=k, slot=slot:
                      e.matmul(P2[:, 2 * oc:2 * oc + 2],
                               wmbuf[:, slot * 2048 + k * 128:slot * 2048 + (k + 1) * 128],
                               scpad[:, 2 * k:2 * k + 2], start=(k == 0), stop=(k == 15)))
                if k == 15:
                    t_mm = P.emit("tensor", "T", fn)
                else:
                    P.emitq("tensor", fn)
            t_mm_hist[oc] = t_mm
        P.wait("vector", "T", t_mm)
        t_mod = P.emit("vector", "V", lambda e: e.tensor_copy(
            mod_sb[:].rearrange("p o -> p o ()"),
            P2[:, 0:192].rearrange("p (o two) -> p o two", two=2)[:, :, 0:1]))

        P.wait("vector", "V", t_mod)
        P.wait("vector", "dc", t_cdx)
        P.schain("vector", lambda e: e.tensor_scalar_add(cam[:], mod_sb[:, 16:32], 1.0))
        P.schain("vector", lambda e: e.tensor_copy(sam[:], mod_sb[:, 0:16]))
        P.schain("vector", lambda e: e.tensor_scalar_mul(cam_pro[:], cam[:], pm_s[:, 0:1]))
        P.schain("vector", lambda e: e.tensor_scalar_mul(sam_pro[:], sam[:], pm_s[:, 0:1]))
        P.schain("vector", lambda e: e.tensor_scalar_add(cbm[:], mod_sb[:, 64:80], 1.0))
        P.schain("vector", lambda e: e.tensor_copy(sbm[:], mod_sb[:, 48:64]))
        g1, g2, g3 = smalls["gsa"], smalls["gsb"], smalls["gsc"]
        for off, gdst in ((32, gam_f), (80, gbm_f)):
            P.schain("vector", lambda e, o=off: e.tensor_single_scalar(g1[:], mod_sb[:, o:o + 16], 0.0, ALU.is_ge))
            P.schain("vector", lambda e: e.tensor_scalar(g2[:], g1[:], 2.0, -1.0, ALU.mult, ALU.add))
            P.schain("vector", lambda e, o=off: e.scalar_tensor_tensor(
                g3[:], g2[:], 0.25, mod_sb[:, o:o + 16], ALU.mult, ALU.add))
            P.schain("vector", lambda e, o=off: e.tensor_scalar(
                g1[:], mod_sb[:, o:o + 16], -0.25, 0.25, ALU.max, ALU.min))
            P.schain("vector", lambda e, d=gdst: e.tensor_sub(d[:], g3[:], g1[:]))
        t_modp = P.emit("vector", "V", lambda e: e.tensor_copy(gxm_fr[:], gam_f[:]))
        if "gaf" in _DBG:
            P.tap(dbg["gaf"], gam_f[:], "V", t_modp)

        if "mod" in _DBG:
            P.wait("gpsimd", "V", t_modp)
            P.dma("gpsimd", "dbg", dbg["mod"], mod_sb[:])

        td = ts = tt2 = 0
        for which, (gsrc, gdst) in enumerate(((gam_f, gab),)):
            if which == 1:
                P.wait("vector", "dfl", td)
                P.wait("vector", "T", tt2)
                P.emit("vector", "V", lambda e, g=gsrc: e.tensor_copy(gxm_fr[:], g[:]))
            tv = P.cnt["V"]
            P.wait("tensor", "V", tv)
            if which == 1:
                P.wait("tensor", "S", ts)
            tt = P.emit("tensor", "T", lambda e: e.transpose(PT4[0:16, 128:256], gxm_fr[:], eyeR[:]))
            P.wait("scalar", "T", tt)
            if which == 1:
                P.wait("scalar", "dfl", td)
            ts = P.emit("scalar", "S", lambda e: e.activation(garowS[:], PT4[0:16, 128:256], AF.Identity))
            P.wait("sync", "S", ts)
            td = P.dma("sync", "dfl", garow2[0:1, :], garowS[:])
            if "gar" in _DBG:
                P.tap(dbg["gar"], garow2[0:1, :], "dfl", td)
            P.wait("tensor", "dfl", td)
            for s4 in range(4):
                tt2 = P.emit("tensor", "T", lambda e, s4=s4: e.matmul(
                    P0[:, (s4 % 2) * 512:(s4 % 2) * 512 + 512],
                    onesr[:], garow2[0:1, s4 * 512:(s4 + 1) * 512], start=True, stop=True))
                P.wait("vector", "T", tt2)
                tv2 = P.emit("vector", "V", lambda e, s4=s4, gd=gdst: e.tensor_copy(
                    gd[:, s4 * 512:(s4 + 1) * 512], P0[:, (s4 % 2) * 512:(s4 % 2) * 512 + 512]))
                P.wait("tensor", "V", tv2)
        P.barrier()
    if _PHASES <= 1:
        return _finalize(nc, P)

    span_es = contextlib.ExitStack()
    LT_all = span_es.enter_context(nc.sbuf_tensor("LT_all", [16, NTC * 128], F32))
    srowA = span_es.enter_context(nc.sbuf_tensor("srowA", [1, 2048], BF16))
    srowB = span_es.enter_context(nc.sbuf_tensor("srowB", [1, 2048], BF16))

    with contextlib.ExitStack() as phA:
        asb = lambda name, shape, dt: phA.enter_context(nc.sbuf_tensor(name, shape, dt))
        xb = asb("xb", [128, 2048], F32)
        xnb = asb("xnb", [128, 2048], BF16)
        scrA = asb("scrA", [128, 2048], F32R)
        scrB = asb("scrB", [128, 2048], F32R)
        h_fm = asb("h_fm", [128, 16 * (SP + 3)], BF16)
        xc_fm = asb("xc_fm", [128, 16 * SP], BF16)
        wsl = asb("wsl", [128, 2 * 16 * 512], BF16)
        k_sb = asb("k_sb", [128, 4 * 2048], BF16)
        v_sb = asb("v_sb", [128, 4 * 2048], BF16)
        btb = asb("btb", [128, 4 * 2048], BF16)
        cacc = asb("cacc", [128, 2 * 512], F32)
        gamv = asb("gamv", [128, 16], F32)
        lgv = asb("lgv", [128, 16], F32R)
        lsv = asb("lsv", [128, 16], F32R)

        t_hz = P.emit("vector", "V", lambda e: e.memzero(halo[:]))
        P.wait("vector", "V", t_hz)
        if "hz0" in _DBG:
            P.tap(dbg["hz0"], halo[:], "V", t_hz)

        pt4_free = [None] * 4
        pt4_used = [0] * 4
        pt5_free = [None, None]
        t_xn_tr = 0
        pslab_free = [None] * 4
        wsl_t_done = [0, 0]
        t_xn_prev = 0
        t_proj_done_sp = 0
        t_gamma_evac_prev = 0
        t_scrA_vuse = [0]
        slabslot = [0]

        def pslab_ap(i):
            return (P0 if i < 2 else P1)[:, (i % 2) * 512:(i % 2) * 512 + 512]

        for sp in range(NSP):
            own = sp >= 4
            tcs = list(range(sp * 4, sp * 4 + 4))
            if f"lta{sp}" in _DBG:
                P.tap(dbg[f"lta{sp}"], LT_all[:], "S", P.cnt["S"])
            for j, tc in enumerate(tcs):
                src = xpro if tc < TC0 else xown
                r0 = (tc * 128) % 2048
                P.wait("sync", "S", t_xn_prev)
                t_x = P.dma("sync", "dxx", xb[:], src[r0:r0 + 128, :])
                P.wait("scalar", "dxx", t_x)
                P.wait("scalar", "V", t_scrA_vuse[0])
                P.emitq("scalar", lambda e: e.activation(
                    scrA[:], xb[:], AF.Square, accum_out=smalls["st2"][:, 0:1]))
                t_st = P.emit("scalar", "S", lambda e: e.activation(
                    scrA[:], xb[:], AF.Identity, accum_out=smalls["st1"][:, 0:1]))
                P.wait("vector", "S", t_st)
                P.schain("vector", lambda e: e.tensor_scalar_mul(smalls["mu"][:, 0:1], smalls["st1"][:, 0:1], 1.0 / 2048))
                t_v0 = P.emit("vector", "V", lambda e: e.tensor_scalar_mul(smalls["ms"][:, 0:1], smalls["st2"][:, 0:1], 1.0 / 2048))
                P.wait("scalar", "V", t_v0)
                t_s0 = P.emit("scalar", "S", lambda e: e.activation(smalls["mu2"][:, 0:1], smalls["mu"][:, 0:1], AF.Square))
                P.wait("vector", "S", t_s0)
                P.schain("vector", lambda e: e.tensor_sub(smalls["var"][:, 0:1], smalls["ms"][:, 0:1], smalls["mu2"][:, 0:1]))
                P.schain("vector", lambda e: e.tensor_scalar_add(smalls["var"][:, 0:1], smalls["var"][:, 0:1], EPS))
                P.wait("vector", "V", P.cnt["V"])
                t_v1 = P.emit("vector", "V", lambda e: e.reciprocal(smalls["rv"][:, 0:1], smalls["var"][:, 0:1]))
                P.wait("scalar", "V", t_v1)
                t_s1 = P.emit("scalar", "S", lambda e: e.activation(smalls["rr"][:, 0:1], smalls["rv"][:, 0:1], AF.Sqrt))
                P.wait("vector", "S", t_s1)
                t_v2 = P.emit("vector", "V", lambda e: e.scalar_tensor_tensor(
                    smalls["nmr"][:, 0:1], smalls["mu"][:, 0:1], -1.0, smalls["rr"][:, 0:1], ALU.mult, ALU.mult))
                P.wait("scalar", "V", t_v2)
                P.wait("scalar", "T", t_xn_tr[0] if isinstance(t_xn_tr, list) else t_xn_tr)
                t_xn = P.emit("scalar", "S", lambda e: e.activation(
                    xnb[:], xb[:], AF.Identity, bias=smalls["nmr"][:, 0:1], scale=smalls["rr"][:, 0:1]))
                t_xn_prev = t_xn
                if "xnb" in _DBG and tc == _DBG_SEL[1]:
                    P.tap(dbg["xnb"], xnb[:], "S", t_xn)
                    P.emitq("vector", lambda e: e.tensor_copy(dbg and smalls["gsa"][:, 0:1], smalls["st1"][:, 0:1]) if False else e.tensor_copy(smalls["gsa"][:, 0:1], smalls["st1"][:, 0:1]))
                    P.emitq("vector", lambda e: e.tensor_copy(smalls["gsa"][:, 1:2], smalls["st2"][:, 0:1]))
                    P.emitq("vector", lambda e: e.tensor_copy(smalls["gsa"][:, 2:3], smalls["mu"][:, 0:1]))
                    P.emitq("vector", lambda e: e.tensor_copy(smalls["gsa"][:, 3:4], smalls["var"][:, 0:1]))
                    P.emitq("vector", lambda e: e.tensor_copy(smalls["gsa"][:, 4:5], smalls["rv"][:, 0:1]))
                    P.emitq("vector", lambda e: e.tensor_copy(smalls["gsa"][:, 5:6], smalls["rr"][:, 0:1]))
                    tln = P.emit("vector", "V", lambda e: e.tensor_copy(smalls["gsa"][:, 6:7], smalls["nmr"][:, 0:1]))
                    P.tap(dbg["lnst"], smalls["gsa"][:], "V", tln)
                camx = cam_pro if tc < TC0 else cam
                samx = sam_pro if tc < TC0 else sam
                P.wait("tensor", "S", t_xn)
                for grp in range(2):
                    P.waitp("tensor", pt5_free[grp])
                    for cc in range(8):
                        c = grp * 8 + cc
                        fn = lambda e, c=c, cc=cc, grp=grp: e.transpose(
                            PT5[:, grp * 1024 + cc * 128:grp * 1024 + (cc + 1) * 128],
                            xnb[:, c * 128:(c + 1) * 128], eyeB[:])
                        if cc == 7:
                            tt = P.emit("tensor", "T", fn)
                        else:
                            P.emitq("tensor", fn)
                    pt4_used[3] = max(pt4_used[3], 0)
                    P.wait("vector", "T", tt)
                    P.wait("vector", "T", t_proj_done_sp)
                    tv = 0
                    for cc in range(8):
                        c = grp * 8 + cc
                        tv = P.emit("vector", "V", lambda e, c=c, cc=cc, grp=grp, j=j, camx=camx, samx=samx: e.tensor_scalar(
                            h_fm[:, c * (SP + 3) + 3 + j * 128: c * (SP + 3) + 3 + (j + 1) * 128],
                            PT5[:, grp * 1024 + cc * 128:grp * 1024 + (cc + 1) * 128],
                            camx[:, c:c + 1], samx[:, c:c + 1], ALU.mult, ALU.add))
                    pt5_free[grp] = ("V", tv)
                t_xn_tr = tt
            if "halo" in _DBG and sp == _DBG_SEL[0]:
                P.tap(dbg["halo"], halo[:], "V", P.cnt["V"])
            for c in range(16):
                P.emitq("vector", lambda e, c=c: e.tensor_copy(
                    h_fm[:, c * (SP + 3): c * (SP + 3) + 3], halo[:, c * 3:c * 3 + 3]))
            P.wait("scalar", "T", t_proj_done_sp)
            t_silu_prev = 0
            for c in range(16):
                base = c * (SP + 3)
                P.wait("vector", "S", t_silu_prev)
                P.emitq("vector", lambda e, c=c, base=base: e.tensor_scalar(
                    cacc[:, 0:512], h_fm[:, base:base + 512],
                    convw[:, c * 4:c * 4 + 1], convb[:, c:c + 1], ALU.mult, ALU.add))
                P.emitq("vector", lambda e, c=c, base=base: e.scalar_tensor_tensor(
                    cacc[:, 512:1024], h_fm[:, base + 1:base + 513],
                    convw[:, c * 4 + 1:c * 4 + 2], cacc[:, 0:512], ALU.mult, ALU.add))
                P.emitq("vector", lambda e, c=c, base=base: e.scalar_tensor_tensor(
                    cacc[:, 0:512], h_fm[:, base + 2:base + 514],
                    convw[:, c * 4 + 2:c * 4 + 3], cacc[:, 512:1024], ALU.mult, ALU.add))
                tvc = P.emit("vector", "V", lambda e, c=c, base=base: e.scalar_tensor_tensor(
                    cacc[:, 512:1024], h_fm[:, base + 3:base + 515],
                    convw[:, c * 4 + 3:c * 4 + 4], cacc[:, 0:512], ALU.mult, ALU.add))
                P.wait("scalar", "V", tvc)
                t_silu_prev = P.emit("scalar", "S", lambda e, c=c: e.activation(
                    xc_fm[:, c * 512:(c + 1) * 512], cacc[:, 512:1024], AF.Silu))
            for c in range(16):
                P.emitq("vector", lambda e, c=c: e.tensor_copy(
                    halo[:, c * 3:c * 3 + 3], h_fm[:, c * (SP + 3) + 512:c * (SP + 3) + 515]))
            t_hevac_sp = P.cnt["V"]
            t_conv_done = P.cnt["S"]

            if "hfm" in _DBG and sp == _DBG_SEL[0]:
                P.tap(dbg["hfm"], h_fm[:], "V", t_hevac_sp)
            if "xconv" in _DBG and sp == _DBG_SEL[0]:
                P.tap(dbg["xconv"], xc_fm[:], "S", t_conv_done)

            passes = [("k", wk_t, True, None), ("v", wv_t, True, None),
                      ("ig", wig_t, False, bigrow)]
            if own:
                passes += [("q", wq_t, True, None), ("og", wog_t, False, bogrow)]

            for pname, wsrc, use_h, brow in passes:
                if pname in ("q", "og"):
                    P.wait("scalar", "V", P.cnt["V"])
                for slab in range(4):
                    wslot = slab % 2
                    P.wait("gpsimd", "T", wsl_t_done[wslot])
                    tw = 0
                    for k in range(16):
                        tw = P.dma("gpsimd", f"dws{wslot}",
                                   wsl[:, wslot * 8192 + k * 512:wslot * 8192 + (k + 1) * 512],
                                   wsrc[k * 128:(k + 1) * 128, slab * 512:(slab + 1) * 512])
                    for j, tc in enumerate(tcs):
                        sslot = slabslot[0] % 4
                        slabslot[0] += 1
                        pap = pslab_ap(sslot)
                        P.wait("tensor", f"dws{wslot}", tw)
                        P.wait("tensor", "dgc", t_cdg)
                        P.waitp("tensor", pslab_free[sslot])
                        P.wait("tensor", "V", t_hevac_sp)
                        P.wait("tensor", "S", t_conv_done)
                        nmm = 16
                        for k in range(16):
                            if use_h:
                                lhs = h_fm[:, k * (SP + 3) + 3 + j * 128:k * (SP + 3) + 3 + (j + 1) * 128]
                            else:
                                lhs = xc_fm[:, k * 512 + j * 128:k * 512 + (j + 1) * 128]
                            last = (k == 15) and (brow is None)
                            fn = lambda e, k=k, lhs=lhs, pap=pap, wslot=wslot, last=last: e.matmul(
                                pap, lhs, wsl[:, wslot * 8192 + k * 512:wslot * 8192 + (k + 1) * 512],
                                start=(k == 0), stop=last)
                            if last:
                                tmm = P.emit("tensor", "T", fn)
                            else:
                                P.emitq("tensor", fn)
                        if brow is not None:
                            tmm = P.emit("tensor", "T", lambda e, pap=pap, brow=brow, slab=slab: e.matmul(
                                pap, onesbf[:], brow[0:1, slab * 512:(slab + 1) * 512],
                                start=False, stop=True))
                        if pname == "ig" and slab == 0:
                            P.wait("tensor", "S", t_gamma_evac_prev)
                            for k in range(16):
                                P.emitq("tensor", lambda e, k=k, j=j: e.matmul(
                                    P2[:, j * 16:(j + 1) * 16],
                                    xc_fm[:, k * 512 + j * 128:k * 512 + (j + 1) * 128],
                                    wgt_s[:, k * 16:(k + 1) * 16], start=(k == 0), stop=False))
                            tmm = P.emit("tensor", "T", lambda e, j=j: e.matmul(
                                P2[:, j * 16:(j + 1) * 16], onesbf[:], bgrow[:],
                                start=False, stop=True))
                        wsl_t_done[wslot] = tmm
                        P.wait("scalar", "T", tmm)
                        if pname in ("k", "q"):
                            te = P.emit("scalar", "S", lambda e, j=j, slab=slab, pap=pap: e.activation(
                                k_sb[:, j * 2048 + slab * 512:j * 2048 + (slab + 1) * 512], pap, AF.Identity))
                            pslab_free[sslot] = ("S", te)
                        elif pname == "v":
                            P.wait("scalar", f"dsv{j}", P.cnt[f"dsv{j}"])
                            te = P.emit("scalar", "S", lambda e, j=j, slab=slab, pap=pap: e.activation(
                                v_sb[:, j * 2048 + slab * 512:j * 2048 + (slab + 1) * 512], pap, AF.Identity))
                            pslab_free[sslot] = ("S", te)
                        elif pname == "og":
                            P.wait("scalar", f"dsv{j}", P.cnt[f"dsv{j}"])
                            te = P.emit("scalar", "S", lambda e, j=j, slab=slab, pap=pap: e.activation(
                                v_sb[:, j * 2048 + slab * 512:j * 2048 + (slab + 1) * 512], pap, AF.Sigmoid))
                            pslab_free[sslot] = ("S", te)
                        else:
                            P.wait("scalar", "V", t_scrA_vuse[0])
                            ts1 = P.emit("scalar", "S", lambda e, pap=pap, slab=slab: e.activation(
                                scrA[:, slab * 512:(slab + 1) * 512], pap, AF.Sigmoid))
                            P.wait("vector", "S", ts1)
                            P.wait("vector", f"dsb{j}", P.cnt[f"dsb{j}"])
                            tvb = P.emit("vector", "V", lambda e, j=j, slab=slab: e.tensor_mul(
                                btb[:, j * 2048 + slab * 512:j * 2048 + (slab + 1) * 512],
                                k_sb[:, j * 2048 + slab * 512:j * 2048 + (slab + 1) * 512],
                                scrA[:, slab * 512:(slab + 1) * 512]))
                            pslab_free[sslot] = ("V", tvb)
                            t_scrA_vuse[0] = tvb

                if pname in ("v", "q"):
                    for j, tc in enumerate(tcs):
                        P.wait("scalar", "V", P.cnt["V"])
                        tq1 = P.emit("scalar", "S", lambda e, j=j: e.activation(
                            scrB[:], k_sb[:, j * 2048:(j + 1) * 2048], AF.Square))
                        P.wait("vector", "S", tq1)
                        tv3 = P.emit("vector", "V", lambda e: e.tensor_reduce(
                            smalls["ss"][:].rearrange("p s -> p s ()"),
                            scrB[:].rearrange("p (s d) -> p s d", s=16), AX.X, ALU.add))
                        P.wait("scalar", "V", tv3)
                        ts3 = P.emit("scalar", "S", lambda e: e.activation(
                            smalls["ssn"][:], smalls["ss"][:], AF.Sqrt))
                        P.wait("vector", "S", ts3)
                        P.schain("vector", lambda e: e.tensor_scalar_max(
                            smalls["ssn"][:], smalls["ssn"][:], 1e-12))
                        P.wait("vector", "V", P.cnt["V"])
                        tv4 = P.emit("vector", "V", lambda e: e.reciprocal(
                            smalls["ssr"][:], smalls["ssn"][:]))
                        P.wait("vector", "V", tv4)
                        ksl = k_sb[:, j * 2048:(j + 1) * 2048]
                        k3 = ksl.rearrange("p (s d) -> p s d", s=16)
                        r3 = bass.broadcast_tensor_aps(
                            smalls["ssr"][:].rearrange("p s -> p s ()"), k3)[0]
                        if pname == "v":
                            P.emitq("vector", lambda e, k3=k3, r3=r3: e.tensor_tensor(
                                scrB[:].rearrange("p (s d) -> p s d", s=16), k3, r3, ALU.mult))
                            if "kn" in _DBG and tc == _DBG_SEL[1]:
                                P.emitq("vector", lambda e: e.tensor_copy(xnb[:], scrB[:]))
                                tdk = P.emit("vector", "V", lambda e: e.tensor_copy(xnb[:], scrB[:]))
                                P.wait("gpsimd", "V", tdk)
                                P.dma("gpsimd", "dbg", dbg["kn"], xnb[:])
                            P.emit("vector", "V", lambda e, j=j, ksl=ksl: e.tensor_mul(
                                ksl, scrB[:], v_sb[:, j * 2048:(j + 1) * 2048]))
                        else:
                            P.wait("vector", f"dsb{j}", P.cnt[f"dsb{j}"])
                            tq5 = P.emit("vector", "V", lambda e, j=j, k3=k3, r3=r3: e.tensor_tensor(
                                btb[:, j * 2048:(j + 1) * 2048].rearrange("p (s d) -> p s d", s=16),
                                k3, r3, ALU.mult))
                            P.wait("sync", "V", tq5)
                            P.dma("sync", f"dsb{j}", q_sp[(tc - TC0) * 128:(tc - TC0) * 128 + 128, :],
                                  btb[:, j * 2048:(j + 1) * 2048])
                            if "q" in _DBG and tc == _DBG_SEL[1]:
                                P.wait("gpsimd", "V", tq5)
                                P.dma("gpsimd", "dbg", dbg["q"], btb[:, j * 2048:(j + 1) * 2048])
                if pname == "ig":
                    for j, tc in enumerate(tcs):
                        tbt = P.cnt["V"]
                        P.wait("sync", "V", tbt)
                        t_btsp = P.dma("sync", f"dsb{j}", bt_sp[tc * 128:(tc + 1) * 128, :],
                                       btb[:, j * 2048:(j + 1) * 2048])
                        if "bt" in _DBG and tc == _DBG_SEL[1]:
                            P.wait("gpsimd", "V", tbt)
                            P.dma("gpsimd", "dbg", dbg["bt"], btb[:, j * 2048:(j + 1) * 2048])
                        P.wait("scalar", "T", P.cnt["T"])
                        P.emit("scalar", "S", lambda e, j=j: e.activation(
                            gamv[:], P2[:, j * 16:(j + 1) * 16], AF.Sigmoid))
                        P.wait("vector", "S", P.cnt["S"])
                        if "gam" in _DBG and tc == _DBG_SEL[1]:
                            P.tap(dbg["gam"], gamv[:], "S", P.cnt["S"])
                        tv6 = P.emit("vector", "V", lambda e: e.tensor_scalar_max(gamv[:], gamv[:], 1e-30))
                        P.wait("scalar", "V", tv6)
                        ts6 = P.emit("scalar", "S", lambda e: e.activation(lgv[:], gamv[:], AF.Ln))
                        t_gamma_evac_prev = ts6
                        if "lg" in _DBG and tc == _DBG_SEL[1]:
                            P.tap(dbg["lg"], lgv[:], "S", ts6)
                        P.wait("tensor", "S", ts6)
                        tt6 = P.emit("tensor", "T", lambda e: e.matmul(
                            P2[:, 80:96], trilR[:], lgv[:], start=True, stop=True))
                        P.wait("scalar", "T", tt6)
                        ts7 = P.emit("scalar", "S", lambda e: e.activation(lsv[:], P2[:, 80:96], AF.Identity))
                        if "lsv" in _DBG and tc == _DBG_SEL[1]:
                            P.tap(dbg["lsv"], lsv[:], "S", ts7)
                        P.wait("tensor", "S", ts7)
                        P.waitp("tensor", pt4_free[3])
                        tt7 = P.emit("tensor", "T", lambda e: e.transpose(PT4[0:16, 384:512], lsv[:], eyeR[:]))
                        pt4_used[3] = tt7
                        P.wait("scalar", "T", tt7)
                        ts8 = P.emit("scalar", "S", lambda e, tc=tc: e.activation(
                            LT_all[:, tc * 128:(tc + 1) * 128], PT4[0:16, 384:512], AF.Identity))
                        pt4_free[3] = ("S", ts8)
                        if "lt1" in _DBG and tc == _DBG_SEL[1]:
                            P.tap(dbg["lt1"], LT_all[:], "S", ts8)
                if pname == "og":
                    for j, tc in enumerate(tcs):
                        tev = P.cnt["S"]
                        P.wait("sync", "S", tev)
                        P.dma("sync", f"dsv{j}", og_sp[(tc - TC0) * 128:(tc - TC0) * 128 + 128, :],
                              v_sb[:, j * 2048:(j + 1) * 2048])
                        if "og" in _DBG and tc == _DBG_SEL[1]:
                            P.wait("gpsimd", "S", tev)
                            P.dma("gpsimd", "dbg", dbg["og"], v_sb[:, j * 2048:(j + 1) * 2048])
            t_proj_done_sp = P.cnt["T"]
        P.barrier()
    if _PHASES <= 2:
        return _finalize(nc, P)

    if "ltall" in _DBG:
        P.tap(dbg["ltall"], LT_all[:], "S", P.cnt["S"])

    with contextlib.ExitStack() as phB:
        bsb = lambda name, shape, dt: phB.enter_context(nc.sbuf_tensor(name, shape, dt))
        TAb = bsb("TAb", [2, 2048], F32)
        TBb = bsb("TBb", [2, 2048], F32)
        pexp_a = bsb("pexp_a", [16, NTC * 128], BF16)
        pflat = bsb("pflat", [1, 2048], BF16)
        maskrep = bsb("maskrep", [128, 2048], F32)
        mnwb = bsb("mnwb", [128, 2048], BF16)
        mnbb = bsb("mnbb", [128, 2048], BF16)
        gnwb = bsb("gnwb", [128, 2048], BF16)
        gnbb = bsb("gnbb", [128, 2048], BF16)
        scrAB = bsb("scrA2", [128, 2048], F32R)
        scrBB = bsb("scrB2", [128, 2048], F32R)
        mm_sb = bsb("mm_sb", [128, 2048], BF16)
        btld = bsb("btld", [128, 2048], BF16)
        qld = bsb("qld", [128, 2048], BF16)
        ogld = bsb("ogld", [128, 2048], BF16)
        z_sb = bsb("z_sb", [128, 2048], BF16)

        for dst, srcd in ((mnwb, mnw_d), (mnbb, mnb_d), (gnwb, gnw_d), (gnbb, gnb_d)):
            src_b = bass.AP(srcd.tensor, 0, [[0, 128], [1, 2048]])
            P.dma("gpsimd", "dbc", dst[:], src_b)
        src_m = bass.AP(maskm_d.tensor, 0, [[128, 128], [0, 16], [1, 128]])
        t_bcm = P.dma("gpsimd", "dbc", maskrep[:].rearrange("p (s t) -> p s t", s=16), src_m)
        t_bc = P.cnt["dbc"]
        tmem0 = P.emit("vector", "V", lambda e: e.memset(TAb[:], 1.0))
        tmem = P.emit("vector", "V", lambda e: e.memset(TBb[:], -1.0))
        P.wait("vector", "V", tmem)
        tpe = P.emit("scalar", "S", lambda e: e.activation(pexp_a[:], LT_all[:], AF.Exp))

        t_dmm_prev = 0
        t_scan_prev = 0
        t_memfree = None
        t_srow_prev = 0
        t_zf_prev = 0
        t_gn_prev = 0
        t_exp_prev = 0

        for tc in range(NTC):
            own = tc >= TC0
            j = tc - TC0
            P.wait("sync", "T", t_dmm_prev)
            P.wait("sync", "V", tmem)
            P.wait("sync", "S", tpe)
            tta = P.dma("sync", "dta", TAb[1:2, :], LT_all[:, tc * 128:(tc + 1) * 128])
            tfl = P.dma("sync", "dtb", TBb[0:1, :], LT_all[:, tc * 128:(tc + 1) * 128])
            P.wait("sync", "T", t_scan_prev)
            tpfl = P.dma("sync", "dpf", pflat[0:1, :], pexp_a[:, tc * 128:(tc + 1) * 128])
            tbt = P.dma("sync", "dbl", btld[:], bt_sp[tc * 128:(tc + 1) * 128, :])
            tqog = 0
            if own:
                P.wait("sync", "V", t_gn_prev)
                tq_ld = P.dma("sync", "dql", qld[:], q_sp[j * 128:(j + 1) * 128, :])
                tqog = P.dma("sync", "dol", ogld[:], og_sp[j * 128:(j + 1) * 128, :])
            P.wait("tensor", "dta", tta)
            P.wait("tensor", "dtb", tfl)
            P.waitp("tensor", t_memfree)
            for h in range(16):
                pap = (P0 if h < 8 else P1)[:, (h % 8) * 128:(h % 8) * 128 + 128]
                fn = lambda e, h=h, pap=pap: e.matmul(
                    pap, TAb[0:2, h * 128:(h + 1) * 128], TBb[0:2, h * 128:(h + 1) * 128],
                    start=True, stop=True)
                if h == 15:
                    t_dmm = P.emit("tensor", "T", fn)
                else:
                    P.emitq("tensor", fn)
            t_dmm_prev = t_dmm
            P.wait("vector", "T", t_dmm)
            P.wait("vector", "S", t_exp_prev)
            P.wait("vector", "dbc", t_bcm)
            P.emitq("vector", lambda e: e.scalar_tensor_tensor(
                scrAB[:, 0:1024], P0[:], 1.0, maskrep[:, 0:1024], ALU.mult, ALU.add))
            tmsk = P.emit("vector", "V", lambda e: e.scalar_tensor_tensor(
                scrAB[:, 1024:2048], P1[:], 1.0, maskrep[:, 1024:2048], ALU.mult, ALU.add))
            P.wait("scalar", "V", tmsk)
            t_exp_prev = P.emit("scalar", "S", lambda e: e.activation(mm_sb[:], scrAB[:], AF.Exp))
            if "mm" in _DBG and tc == _DBG_SEL[1]:
                P.tap(dbg["mm"], mm_sb[:], "S", t_exp_prev)
            P.wait("tensor", "S", t_exp_prev)
            P.wait("tensor", "dbl", tbt)
            P.wait("tensor", "dpf", tpfl)
            P.wait("tensor", "V", tmsk)
            P.wait("tensor", "S", t_srow_prev)
            srow_prev = srowB if tc % 2 == 1 else srowA
            srow_cur = srowA if tc % 2 == 1 else srowB
            for h in range(16):
                pap = (P0 if h < 8 else P1)[:, (h % 8) * 128:(h % 8) * 128 + 128]
                fn = lambda e, h=h, pap=pap, st=(tc == 0): e.matmul(
                    pap, mm_sb[:, h * 128:(h + 1) * 128], btld[:, h * 128:(h + 1) * 128],
                    start=True, stop=st)
                fnc = (None if tc == 0 else
                       (lambda e, h=h, pap=pap, sp_=srow_prev: e.matmul(
                           pap, pflat[0:1, h * 128:(h + 1) * 128],
                           sp_[0:1, h * 128:(h + 1) * 128], start=False, stop=True)))
                if fnc is None:
                    if h == 15:
                        t_scan = P.emit("tensor", "T", fn)
                    else:
                        P.emitq("tensor", fn)
                else:
                    P.emitq("tensor", fn)
                    if h == 15:
                        t_scan = P.emit("tensor", "T", fnc)
                    else:
                        P.emitq("tensor", fnc)
            t_scan_prev = t_scan
            t_sx = 0
            for r in range(4):
                P.wait("tensor", "S", t_sx)
                for hh in range(4):
                    h = r * 4 + hh
                    P.emitq("tensor", lambda e, h=h, hh=hh, st=(tc == 0): e.matmul(
                        P2[0:1, hh * 128:(hh + 1) * 128],
                        mm_sb[:, h * 128 + 127:h * 128 + 128],
                        btld[:, h * 128:(h + 1) * 128], start=True, stop=st))
                    if tc > 0:
                        fn = lambda e, h=h, hh=hh, sp_=srow_prev: e.matmul(
                            P2[0:1, hh * 128:(hh + 1) * 128],
                            pflat[0:1, h * 128 + 127:h * 128 + 128],
                            sp_[0:1, h * 128:(h + 1) * 128], start=False, stop=True)
                    else:
                        fn = None
                    if fn is not None:
                        if hh == 3:
                            t_xmm = P.emit("tensor", "T", fn)
                        else:
                            P.emitq("tensor", fn)
                if tc == 0:
                    t_xmm = P.emit("tensor", "T", lambda e, r=r: e.matmul(
                        P2[0:1, 3 * 128:4 * 128],
                        mm_sb[:, (r * 4 + 3) * 128 + 127:(r * 4 + 3) * 128 + 128],
                        btld[:, (r * 4 + 3) * 128:(r * 4 + 4) * 128], start=True, stop=True))
                P.wait("scalar", "T", t_xmm)
                t_sx = P.emit("scalar", "S", lambda e, r=r, sc=srow_cur: e.activation(
                    sc[0:1, r * 512:(r + 1) * 512], P2[0:1, 0:512], AF.Identity))
            t_srow_prev = t_sx
            t_scan_prev = t_xmm
            if "srow" in _DBG and tc == _DBG_SEL[1]:
                P.tap(dbg["srow"], srow_cur[0:1, :], "S", t_srow_prev)

            if not own:
                if "mem" in _DBG and tc == _DBG_SEL[1]:
                    P.wait("vector", "T", t_scan)
                    P.emitq("vector", lambda e: e.tensor_copy(scrBB[:, 0:1024], P0[:]))
                    tvm = P.emit("vector", "V", lambda e: e.tensor_copy(scrBB[:, 1024:2048], P1[:]))
                    P.tap(dbg["mem"], scrBB[:], "V", tvm)
                    t_memfree = ("V", tvm)
                else:
                    t_memfree = ("S", t_srow_prev)
                continue

            P.wait("vector", "T", t_scan)
            P.wait("scalar", "T", t_scan)
            if "mem" in _DBG and tc == _DBG_SEL[1]:
                P.emitq("vector", lambda e: e.tensor_copy(scrBB[:, 0:1024], P0[:]))
                tvm = P.emit("vector", "V", lambda e: e.tensor_copy(scrBB[:, 1024:2048], P1[:]))
                P.tap(dbg["mem"], scrBB[:], "V", tvm)
            p03d = P0[:].rearrange("p (s d) -> p s d", s=8)
            p13d = P1[:].rearrange("p (s d) -> p s d", s=8)
            sa3 = scrAB[:].rearrange("p (s d) -> p s d", s=16)
            sb3 = scrBB[:].rearrange("p (s d) -> p s d", s=16)
            P.emitq("vector", lambda e: e.tensor_reduce(
                smalls["g1s"][:, 0:8].rearrange("p s -> p s ()"), p03d, AX.X, ALU.add))
            tvr = P.emit("vector", "V", lambda e: e.tensor_reduce(
                smalls["g1s"][:, 8:16].rearrange("p s -> p s ()"), p13d, AX.X, ALU.add))
            P.wait("scalar", "V", tvr)
            P.emitq("scalar", lambda e: e.activation(scrBB[:, 0:1024], P0[:], AF.Square))
            ts9 = P.emit("scalar", "S", lambda e: e.activation(scrBB[:, 1024:2048], P1[:], AF.Square))
            P.wait("vector", "S", ts9)
            P.emitq("vector", lambda e: e.tensor_reduce(
                smalls["g2s"][:].rearrange("p s -> p s ()"), sb3, AX.X, ALU.add))

            def gn_stats(pref):
                P.schain("vector", lambda e: e.tensor_scalar_mul(smalls["gmu"][:], smalls["g1s"][:], 1.0 / 128))
                t_g0 = P.emit("vector", "V", lambda e: e.tensor_scalar_mul(smalls["gms"][:], smalls["g2s"][:], 1.0 / 128))
                P.wait("scalar", "V", t_g0)
                t_g1 = P.emit("scalar", "S", lambda e: e.activation(smalls["gm2"][:], smalls["gmu"][:], AF.Square))
                P.wait("vector", "S", t_g1)
                P.schain("vector", lambda e: e.tensor_sub(smalls["gva"][:], smalls["gms"][:], smalls["gm2"][:]))
                P.schain("vector", lambda e: e.tensor_scalar_add(smalls["gva"][:], smalls["gva"][:], EPS))
                P.schain("vector", lambda e: e.tensor_copy(smalls[pref + "n"][:], smalls["gmu"][:]))
                P.wait("vector", "V", P.cnt["V"])
                tva = P.emit("vector", "V", lambda e: e.reciprocal(smalls["grv"][:], smalls["gva"][:]))
                P.wait("scalar", "V", tva)
                tsa = P.emit("scalar", "S", lambda e, p=pref: e.activation(
                    smalls[p + "r"][:], smalls["grv"][:], AF.Sqrt))
                return tsa

            tsa1 = gn_stats("g1")
            P.wait("vector", "S", tsa1)
            P.wait("vector", "dbc", t_bc)
            mu13 = smalls["g1n"][:].rearrange("p s -> p s ()")
            r13 = smalls["g1r"][:].rearrange("p s -> p s ()")
            P.emitq("vector", lambda e: e.tensor_tensor(
                scrAB[:, 0:1024].rearrange("p (s d) -> p s d", s=8), p03d,
                bass.broadcast_tensor_aps(smalls["g1n"][:, 0:8].rearrange("p s -> p s ()"), p03d)[0],
                ALU.subtract))
            tv9 = P.emit("vector", "V", lambda e: e.tensor_tensor(
                scrAB[:, 1024:2048].rearrange("p (s d) -> p s d", s=8), p13d,
                bass.broadcast_tensor_aps(smalls["g1n"][:, 8:16].rearrange("p s -> p s ()"), p13d)[0],
                ALU.subtract))
            t_memfree = ("V", tv9)
            P.emitq("vector", lambda e: e.tensor_tensor(
                sb3, sa3, bass.broadcast_tensor_aps(r13, sa3)[0], ALU.mult))
            P.emitq("vector", lambda e: e.scalar_tensor_tensor(
                scrAB[:], scrBB[:], 1.0, mnwb[:], ALU.mult, ALU.mult))
            P.emitq("vector", lambda e: e.tensor_add(scrBB[:], scrAB[:], mnbb[:]))
            P.wait("vector", "dql", tq_ld)
            P.wait("vector", "dol", tqog)
            tv10 = P.emit("vector", "V", lambda e: e.tensor_mul(scrAB[:], scrBB[:], qld[:]))
            P.emitq("vector", lambda e: e.tensor_reduce(
                smalls["g1s"][:].rearrange("p s -> p s ()"), sa3, AX.X, ALU.add))
            P.wait("scalar", "V", tv10)
            ts10 = P.emit("scalar", "S", lambda e: e.activation(scrBB[:], scrAB[:], AF.Square))
            P.wait("vector", "S", ts10)
            P.emitq("vector", lambda e: e.tensor_reduce(
                smalls["g2s"][:].rearrange("p s -> p s ()"), sb3, AX.X, ALU.add))
            tsa2 = gn_stats("g2")
            P.wait("vector", "S", tsa2)
            mu23 = smalls["g2n"][:].rearrange("p s -> p s ()")
            r23 = smalls["g2r"][:].rearrange("p s -> p s ()")
            P.emitq("vector", lambda e: e.tensor_tensor(
                sb3, sa3, bass.broadcast_tensor_aps(mu23, sa3)[0], ALU.subtract))
            P.emitq("vector", lambda e: e.tensor_tensor(
                sa3, sb3, bass.broadcast_tensor_aps(r23, sb3)[0], ALU.mult))
            P.emitq("vector", lambda e: e.scalar_tensor_tensor(
                scrBB[:], scrAB[:], 1.0, gnwb[:], ALU.mult, ALU.mult))
            P.emitq("vector", lambda e: e.tensor_add(scrAB[:], scrBB[:], gnbb[:]))
            tz = P.emit("vector", "V", lambda e: e.tensor_mul(z_sb[:], scrAB[:], ogld[:]))
            t_gn_prev = tz
            if "z" in _DBG and tc == _DBG_SEL[1]:
                P.tap(dbg["z"], z_sb[:], "V", tz)
            P.wait("tensor", "V", tz)
            P.wait("tensor", "S", t_srow_prev)
            for c in range(16):
                fn = lambda e, c=c: e.transpose(
                    PT5[:, c * 128:(c + 1) * 128], z_sb[:, c * 128:(c + 1) * 128], eyeB[:])
                if c == 15:
                    tzf = P.emit("tensor", "T", fn)
                else:
                    P.emitq("tensor", fn)
            t_scan_prev = tzf
            P.wait("vector", "T", tzf)
            P.wait("vector", "dzf", t_zf_prev)
            tzv = P.emit("vector", "V", lambda e: e.tensor_copy(z_sb[:], PT5[:]))
            P.wait("sync", "V", tzv)
            t_zf_prev = P.dma("sync", "dzf", zf_sp[j * 128:(j + 1) * 128, :], z_sb[:])
            P.wait("tensor", "V", tzv)
        P.barrier()
    span_es.close()
    if _PHASES <= 3:
        return _finalize(nc, P)

    with contextlib.ExitStack() as phW:
        wsb = lambda name, shape, dt: phW.enter_context(nc.sbuf_tensor(name, shape, dt))
        wof = wsb("wof", [128, 16 * 2048], BF16)
        zfl = wsb("zfl", [128, 2048], BF16)
        xol = wsb("xol", [128, 2048], F32)
        x2b = wsb("x2b", [128, 2048], F32)
        scrW = wsb("scrW", [128, 2048], F32R)

        tw = 0
        for k in range(16):
            tw = P.dma("gpsimd", "dwo", wof[:, k * 2048:(k + 1) * 2048],
                       wo_t[k * 128:(k + 1) * 128, :])
        t_x2_prev = 0
        t_wodone = 0
        t_wo_mm_prev = 0
        for j in range(16):
            P.wait("sync", "T", t_wo_mm_prev)
            tz = P.dma("sync", "dzl", zfl[:], zf_sp[j * 128:(j + 1) * 128, :])
            if "zfl" in _DBG and j == _DBG_SEL[1] - TC0:
                P.tap(dbg["zfl"], zfl[:], "dzl", tz)
            P.wait("sync", "V", t_wodone)
            tx = P.dma("sync", "dxo", xol[:], xown[j * 128:(j + 1) * 128, :])
            P.wait("tensor", "dwo", tw)
            P.wait("tensor", "dzl", tz)
            P.wait("tensor", "V", t_wodone)
            for sl in range(4):
                pap = (P0 if sl < 2 else P1)[:, (sl % 2) * 512:(sl % 2) * 512 + 512]
                for k in range(16):
                    fn = lambda e, k=k, sl=sl, pap=pap: e.matmul(
                        pap, zfl[:, k * 128:(k + 1) * 128],
                        wof[:, k * 2048 + sl * 512:k * 2048 + (sl + 1) * 512],
                        start=(k == 0), stop=(k == 15))
                    if sl == 3 and k == 15:
                        twm = P.emit("tensor", "T", fn)
                    else:
                        P.emitq("tensor", fn)
            t_wo_mm_prev = twm
            P.wait("vector", "T", twm)
            for sl in range(4):
                pap = (P0 if sl < 2 else P1)[:, (sl % 2) * 512:(sl % 2) * 512 + 512]
                fn = lambda e, sl=sl, pap=pap: e.scalar_tensor_tensor(
                    scrW[:, sl * 512:(sl + 1) * 512], pap, 1.0,
                    gab[:, sl * 512:(sl + 1) * 512], ALU.mult, ALU.mult)
                if sl == 3:
                    t_scrw = P.emit("vector", "V", fn)
                else:
                    P.emitq("vector", fn)
            if "scrw" in _DBG and j == _DBG_SEL[1] - TC0:
                P.tap(dbg["scrw"], scrW[:], "V", t_scrw)
            if "gab" in _DBG and j == _DBG_SEL[1] - TC0:
                P.tap(dbg["gab"], gab[:], "V", t_scrw)
            P.wait("vector", "dxo", tx)
            P.wait("vector", "dx2s", t_x2_prev)
            t_wodone = P.emit("vector", "V", lambda e: e.tensor_add(x2b[:], scrW[:], xol[:]))
            P.wait("sync", "V", t_wodone)
            t_x2_prev = P.dma("sync", "dx2s", x2_sp[j * 128:(j + 1) * 128, :], x2b[:])
            if "x2" in _DBG and j == _DBG_SEL[1] - TC0:
                P.wait("gpsimd", "V", t_wodone)
                P.dma("gpsimd", "dbg", dbg["x2"], x2b[:])
        P.barrier()
    if _PHASES <= 4:
        return _finalize(nc, P)

    with contextlib.ExitStack() as phD:
        dsb = lambda name, shape, dt: phD.enter_context(nc.sbuf_tensor(name, shape, dt))
        x2l0 = dsb("x2l0", [128, 2048], F32)
        scrD0 = dsb("scrD0", [128, 2048], F32R)
        xn2b = dsb("xn2b", [128, 2048], BF16)
        h2t = dsb("h2t", [128, 2048], FP8)

        t_h2dma_prev = 0
        t_xn2_prev = 0
        t_trD_prev = [0]
        pt5D_free = [None, None]
        t_h2_prev = 0
        for j in range(16):
            P.wait("sync", "S", t_xn2_prev)
            tx2 = P.dma("sync", "dxl0", x2l0[:], x2_sp[j * 128:(j + 1) * 128, :])
            P.wait("scalar", "dxl0", tx2)
            P.wait("scalar", "T", t_trD_prev[0])
            P.emitq("scalar", lambda e: e.activation(
                scrD0[:], x2l0[:], AF.Square, accum_out=smalls["st2"][:, 0:1]))
            t_st = P.emit("scalar", "S", lambda e: e.activation(
                scrD0[:], x2l0[:], AF.Identity, accum_out=smalls["st1"][:, 0:1]))
            P.wait("vector", "S", t_st)
            P.schain("vector", lambda e: e.tensor_scalar_mul(smalls["mu"][:, 0:1], smalls["st1"][:, 0:1], 1.0 / 2048))
            t_v0 = P.emit("vector", "V", lambda e: e.tensor_scalar_mul(smalls["ms"][:, 0:1], smalls["st2"][:, 0:1], 1.0 / 2048))
            P.wait("scalar", "V", t_v0)
            t_s0 = P.emit("scalar", "S", lambda e: e.activation(smalls["mu2"][:, 0:1], smalls["mu"][:, 0:1], AF.Square))
            P.wait("vector", "S", t_s0)
            P.schain("vector", lambda e: e.tensor_sub(smalls["var"][:, 0:1], smalls["ms"][:, 0:1], smalls["mu2"][:, 0:1]))
            P.schain("vector", lambda e: e.tensor_scalar_add(smalls["var"][:, 0:1], smalls["var"][:, 0:1], EPS))
            P.wait("vector", "V", P.cnt["V"])
            tv1 = P.emit("vector", "V", lambda e: e.reciprocal(smalls["rv"][:, 0:1], smalls["var"][:, 0:1]))
            P.wait("scalar", "V", tv1)
            ts1 = P.emit("scalar", "S", lambda e: e.activation(smalls["rr"][:, 0:1], smalls["rv"][:, 0:1], AF.Sqrt))
            P.wait("vector", "S", ts1)
            tv2 = P.emit("vector", "V", lambda e: e.scalar_tensor_tensor(
                smalls["nmr"][:, 0:1], smalls["mu"][:, 0:1], -1.0, smalls["rr"][:, 0:1], ALU.mult, ALU.mult))
            P.wait("scalar", "V", tv2)
            t_xn2 = P.emit("scalar", "S", lambda e: e.activation(
                scrD0[:], x2l0[:], AF.Identity, bias=smalls["nmr"][:, 0:1], scale=smalls["rr"][:, 0:1]))
            t_xn2_prev = t_xn2
            P.wait("tensor", "S", t_xn2)
            tve = 0
            for cg in range(4):
                for c in range(cg * 4, cg * 4 + 4):
                    q4 = c % 4
                    fn = lambda e, c=c, q4=q4: e.transpose(
                        PT4[:, q4 * 128:(q4 + 1) * 128], scrD0[:, c * 128:(c + 1) * 128], eyeR[:])
                    if c % 4 == 3:
                        ttg = P.emit("tensor", "T", fn)
                    else:
                        P.emitq("tensor", fn)
                P.wait("vector", "T", ttg)
                P.wait("vector", "dh2", t_h2dma_prev)
                for c in range(cg * 4, cg * 4 + 4):
                    tve = P.emit("vector", "V", lambda e, c=c: e.tensor_scalar(
                        h2t[:, c * 128:(c + 1) * 128], PT4[:, (c % 4) * 128:(c % 4) * 128 + 128],
                        cbm[:, c:c + 1], sbm[:, c:c + 1], ALU.mult, ALU.add))
                P.wait("tensor", "V", tve)
                t_trD_prev[0] = P.cnt["T"]
            P.wait("sync", "V", tve)
            for cs in range(16):
                t_h2dma_prev = P.dma(
                    "sync", "dh2",
                    h2_sp[cs * 128:(cs + 1) * 128, j * 128:(j + 1) * 128],
                    h2t[:, cs * 128:(cs + 1) * 128])
            if "h2" in _DBG and j == _DBG_SEL[1] - TC0:
                tvd = P.emit("vector", "V", lambda e: e.tensor_copy(scrD0[:], h2t[:]))
                P.wait("gpsimd", "V", tvd)
                P.dma("gpsimd", "dbg", dbg["h2"], scrD0[:])
        P.barrier()
    if _PHASES <= 5:
        return _finalize(nc, P)

    with contextlib.ExitStack() as phD2:
        dsb = lambda name, shape, dt: phD2.enter_context(nc.sbuf_tensor(name, shape, dt))
        h2th = dsb("h2th", [128, 16 * 1024], FP8)
        a_sb = dsb("a_sb", [128, 64 * 1024], FP8)
        wgb = dsb("wgb", [128, 2 * 2 * 2048], FP8)
        wlb = dsb("wlb", [128, 2 * 64 * 128], FP8)
        scrDF = dsb("scrD2", [128, 1024], F32R)
        ybf = dsb("ybf", [128, 1024], BF16)
        ytm = dsb("ytm", [128, 8 * 2048], BF16)
        x2lF = dsb("x2l2", [128, 2048], F32)
        outb = dsb("outb", [128, 2048], F32)

        t_gate_mm_prev = 0
        for th in range(2):
            P.wait("sync", "T", t_gate_mm_prev)
            for ks in range(16):
                th2 = P.dma("sync", "dht",
                            h2th[:, ks * 1024:(ks + 1) * 1024],
                            h2_sp[ks * 128:(ks + 1) * 128, th * 1024:(th + 1) * 1024])
            t_gevac = 0
            t_wg_used = [0, 0]
            tgm = 0
            for hc in range(64):
                slot = hc % 2
                P.wait("gpsimd", "T", t_wg_used[slot])
                P.dma("gpsimd", f"dwg{slot}",
                      wgb[:, slot * 4096:slot * 4096 + 2048].rearrange("p (k q) -> p k q", k=16),
                      wg0_t[hc].rearrange("k p q -> p k q"))
                twg = P.dma("gpsimd", f"dwg{slot}",
                            wgb[:, slot * 4096 + 2048:slot * 4096 + 4096].rearrange("p (k q) -> p k q", k=16),
                            wg1_t[hc].rearrange("k p q -> p k q"))
                P.wait("tensor", f"dwg{slot}", twg)
                P.wait("tensor", "dht", th2)
                P.wait("tensor", "V", t_gevac)
                DR = mybir.MatmulPerfMode.DoubleRow
                for k in range(8):
                    for half in range(2):
                        P.emitq("tensor", lambda e, k=k, half=half, slot=slot: e.matmul(
                            P0[:, half * 512:(half + 1) * 512],
                            wgb[:, slot * 4096 + k * 256:slot * 4096 + (k + 1) * 256]
                                .rearrange("p (r m) -> p r m", r=2),
                            h2th[:, k * 2048:(k + 1) * 2048]
                                .rearrange("p (r h t) -> p r h t", r=2, h=2)[:, :, half:half + 1, :],
                            start=(k == 0), stop=(k == 7), perf_mode=DR))
                for k in range(8):
                    for half in range(2):
                        fn = lambda e, k=k, half=half, slot=slot: e.matmul(
                            P1[:, half * 512:(half + 1) * 512],
                            wgb[:, slot * 4096 + 2048 + k * 256:slot * 4096 + 2048 + (k + 1) * 256]
                                .rearrange("p (r m) -> p r m", r=2),
                            h2th[:, k * 2048:(k + 1) * 2048]
                                .rearrange("p (r h t) -> p r h t", r=2, h=2)[:, :, half:half + 1, :],
                            start=(k == 0), stop=(k == 7), perf_mode=DR)
                        if k == 7 and half == 1:
                            tgm = P.emit("tensor", "T", fn)
                        else:
                            P.emitq("tensor", fn)
                t_wg_used[slot] = tgm
                P.wait("scalar", "T", tgm)
                tgg = P.emit("scalar", "S", lambda e: e.activation(scrDF[:], P0[:], AF.Gelu_apprx_tanh))
                P.wait("vector", "S", tgg)
                t_gevac = P.emit("vector", "V", lambda e, hc=hc: e.tensor_mul(
                    a_sb[:, hc * 1024:(hc + 1) * 1024], scrDF[:], P1[:]))
            t_gate_mm_prev = tgm
            t_wl_used = [0, 0]
            t_yevac = t_gevac
            t_out_prev = 0
            t_ycp = 0
            for oc in range(16):
                slot = oc % 2
                P.wait("gpsimd", "T", t_wl_used[slot])
                twl = P.dma("gpsimd", f"dwl{slot}",
                            wlb[:, slot * 8192:(slot + 1) * 8192].rearrange("p (h q) -> p h q", h=64),
                            wl_c[oc].rearrange("h p q -> p h q"))
                P.wait("tensor", f"dwl{slot}", twl)
                P.wait("tensor", "V", t_yevac)
                DR = mybir.MatmulPerfMode.DoubleRow
                for hc in range(32):
                    for half in range(2):
                        fn = lambda e, hc=hc, half=half, slot=slot: e.matmul(
                            P0[:, half * 512:(half + 1) * 512],
                            wlb[:, slot * 8192 + hc * 256:slot * 8192 + (hc + 1) * 256]
                                .rearrange("p (r m) -> p r m", r=2),
                            a_sb[:, hc * 2048:(hc + 1) * 2048]
                                .rearrange("p (r h t) -> p r h t", r=2, h=2)[:, :, half:half + 1, :],
                            start=(hc == 0), stop=(hc == 31), perf_mode=DR)
                        if hc == 31 and half == 1:
                            tym = P.emit("tensor", "T", fn)
                        else:
                            P.emitq("tensor", fn)
                t_wl_used[slot] = tym
                P.wait("vector", "T", tym)
                tyv = P.emit("vector", "V", lambda e, oc=oc: e.tensor_scalar_mul(
                    ybf[:], P0[:], gbm_f[:, oc:oc + 1]))
                t_yevac = tyv
                P.wait("tensor", "V", tyv)
                P.wait("tensor", "V", t_ycp)
                for tt8 in range(8):
                    fn = lambda e, tt8=tt8: e.transpose(
                        PT5[:, tt8 * 128:(tt8 + 1) * 128], ybf[:, tt8 * 128:(tt8 + 1) * 128], eyeB[:])
                    if tt8 == 7:
                        tyt = P.emit("tensor", "T", fn)
                    else:
                        P.emitq("tensor", fn)
                P.wait("vector", "T", tyt)
                for tt8 in range(8):
                    fn = lambda e, tt8=tt8, oc=oc: e.tensor_copy(
                        ytm[:, tt8 * 2048 + oc * 128:tt8 * 2048 + (oc + 1) * 128],
                        PT5[:, tt8 * 128:(tt8 + 1) * 128])
                    if tt8 == 7:
                        t_ycp = P.emit("vector", "V", fn)
                    else:
                        P.emitq("vector", fn)
            tvo = 0
            for t8 in range(8):
                j = th * 8 + t8
                P.wait("sync", "V", tvo)
                tx2 = P.dma("sync", "dxl2", x2lF[:], x2_sp[j * 128:(j + 1) * 128, :])
                P.wait("vector", "dxl2", tx2)
                P.wait("vector", "dou", t_out_prev)
                P.wait("vector", "V", t_ycp)
                tvo = P.emit("vector", "V", lambda e, t8=t8: e.tensor_add(
                    outb[:], ytm[:, t8 * 2048:(t8 + 1) * 2048], x2lF[:]))
                P.wait("sync", "V", tvo)
                t_out_prev = P.dma("sync", "dou", out[j * 128:(j + 1) * 128, :], outb[:])
        P.barrier()

    return _finalize(nc, P)


def _shared_arrays(inputs):
    sh = {}
    wmod = inputs["L0_wmod"]
    sh["wmod_t"] = np.ascontiguousarray(wmod.reshape(16, 128, 96, 128).transpose(2, 0, 1, 3))
    for nm, key in (("wq_t", "L0_attn_Wq"), ("wk_t", "L0_attn_Wk"),
                    ("wv_t", "L0_attn_Wv"), ("wig_t", "L0_attn_Wig"),
                    ("wog_t", "L0_attn_Wog"), ("wo_t", "L0_attn_Wo")):
        sh[nm] = np.ascontiguousarray(inputs[key].T)
    sh["wg_t"] = np.ascontiguousarray(inputs["L0_attn_Wg"].T)
    sh["big"] = inputs["L0_attn_b_ig"].reshape(1, -1)
    sh["bog"] = inputs["L0_attn_b_og"].reshape(1, -1)
    sh["bgr"] = inputs["L0_attn_b_g"].reshape(1, -1)
    cw = inputs["L0_attn_conv_w"]
    sh["convw"] = np.ascontiguousarray(cw.reshape(16, 128, 4).transpose(1, 0, 2).reshape(128, 64))
    sh["convb"] = np.ascontiguousarray(inputs["L0_attn_conv_b"].reshape(16, 128).T)
    sh["mnw"] = inputs["L0_attn_mn_w"].reshape(1, -1)
    sh["mnb"] = inputs["L0_attn_mn_b"].reshape(1, -1)
    sh["gnw"] = inputs["L0_attn_gn_w"].reshape(1, -1)
    sh["gnb"] = inputs["L0_attn_gn_b"].reshape(1, -1)
    wg = inputs["L0_ffn_w_gating"]
    sh["wg0_t"] = np.ascontiguousarray(wg[0].reshape(16, 128, 64, 128).transpose(2, 0, 1, 3))
    sh["wg1_t"] = np.ascontiguousarray(wg[1].reshape(16, 128, 64, 128).transpose(2, 0, 1, 3))
    sh["wl_c"] = np.ascontiguousarray(
        inputs["L0_ffn_w_linear"].reshape(64, 128, 16, 128).transpose(2, 0, 1, 3))
    sh["eye"] = np.eye(128, dtype=np.float32)
    sh["eyeb"] = np.eye(128, dtype=np.float32)
    sh["tril"] = np.triu(np.ones((128, 128), np.float32))
    sh["maskm"] = np.where(np.triu(np.ones((128, 128), np.float32)) > 0, 0.0, NEGM).astype(np.float32)
    sh["ones_d"] = np.ones((1, 128), np.float32)
    sh["onesr_d"] = np.ones((1, 128), np.float32)
    return sh


def _core_inputs(inputs, sh, core):
    b, half = core // 2, core % 2
    x = inputs["L0_attn_x"]
    m = dict(sh)
    if half == 0:
        m["xpro"] = np.zeros((TOWN, D), np.float32)
        m["pmask"] = np.zeros((128, 1), np.float32)
    else:
        m["xpro"] = np.ascontiguousarray(x[b, 0:TOWN])
        m["pmask"] = np.ones((128, 1), np.float32)
    m["xown"] = np.ascontiguousarray(x[b, half * TOWN:(half + 1) * TOWN])
    m["cvec"] = np.ascontiguousarray(inputs["L0_c"][b].reshape(16, 128))
    return m


def kernel(**inputs):
    nc = build()
    sh = _shared_arrays(inputs)
    in_maps = [_core_inputs(inputs, sh, c) for c in range(8)]
    res = run_bass_kernel_spmd(nc, in_maps, core_ids=list(range(8)))
    out = np.zeros((4, 4096, 2048), np.float32)
    for core in range(8):
        b, half = core // 2, core % 2
        out[b, half * TOWN:(half + 1) * TOWN] = np.asarray(res.results[core]["out"], np.float32)
    return out


# version 06
# speedup vs baseline: 1.2092x; 1.0705x over previous
import contextlib
from collections import defaultdict

import numpy as np
import concourse.bass as bass
import concourse.mybir as mybir
from concourse.bass_utils import run_bass_kernel_spmd

F32 = mybir.dt.float32
F32R = mybir.dt.float32r
BF16 = mybir.dt.bfloat16
FP8 = mybir.dt.float8e4
AF = mybir.ActivationFunctionType
ALU = mybir.AluOpType
AX = mybir.AxisListType

D = 2048
NH = 16
TOWN = 2048
TALL = 4096
NTC = 32
TC0 = 16
SP = 512
NSP = TALL // SP
EPS = 1e-5
NEGM = -30000.0

ENGS = ("sync", "gpsimd", "tensor", "vector", "scalar")

_DBG = ()
_DBG_SEL = (4, 16)
_PHASES = 6


class Prog:
    def __init__(self, nc):
        self.nc = nc
        self.ops = {e: [] for e in ENGS}
        self.cnt = defaultdict(int)
        self.semh = {}
        self.last_wait = {e: defaultdict(int) for e in ENGS}
        self.es = contextlib.ExitStack()

    def sem(self, name):
        if name not in self.semh:
            self.semh[name] = self.es.enter_context(self.nc.semaphore(name))
        return self.semh[name]

    def op(self, eng, fn):
        self.ops[eng].append(fn)

    def wait(self, eng, sem_name, tgt):
        if tgt <= 0:
            return
        if self.last_wait[eng][sem_name] >= tgt:
            return
        self.last_wait[eng][sem_name] = tgt
        s = self.sem(sem_name)
        self.op(eng, lambda e, s=s, t=tgt: e.wait_ge(s, t))

    def waitp(self, eng, pair):
        if pair is not None:
            self.wait(eng, pair[0], pair[1])

    def run(self, eng_name, eng):
        for fn in self.ops[eng_name]:
            fn(eng)

    def dma(self, eng, sem_name, out, in_):
        s = self.sem(sem_name)
        self.cnt[sem_name] += 16
        self.op(eng, lambda e, o=out, i=in_, s=s: e.dma_start(out=o, in_=i).then_inc(s, 16))
        return self.cnt[sem_name]

    def emit(self, eng, sem_name, fn):
        s = self.sem(sem_name)
        self.cnt[sem_name] += 1
        self.op(eng, lambda e, f=fn, s=s: f(e).then_inc(s, 1))
        return self.cnt[sem_name]

    def emitq(self, eng, fn):
        self.op(eng, fn)

    def schain(self, eng, fn):
        sem = {"vector": "V", "scalar": "S", "gpsimd": "G"}[eng]
        self.wait(eng, sem, self.cnt[sem])
        return self.emit(eng, sem, fn)

    def tap(self, dst, src, wait_sem, wait_tgt):
        self.wait("gpsimd", wait_sem, wait_tgt)
        t = self.dma("gpsimd", "dbg", dst, src)
        for e in ENGS:
            self.wait(e, "dbg", t)

    def barrier(self):
        sems = [k for k, v in self.cnt.items() if v > 0]
        for e in ENGS:
            for s in sems:
                self.wait(e, s, self.cnt[s])


def _finalize(nc, P):
    for snm in list(P.cnt.keys()):
        if P.cnt[snm] > 0:
            P.wait("sync", snm, P.cnt[snm])
    if True:
        with nc.Block() as blk:
            @blk.sync
            def _(e):
                P.run("sync", e)

            @blk.gpsimd
            def _(e):
                P.run("gpsimd", e)

            @blk.tensor
            def _(e):
                P.run("tensor", e)

            @blk.vector
            def _(e):
                P.run("vector", e)

            @blk.scalar
            def _(e):
                P.run("scalar", e)
    return nc


def build():
    nc = bass.Bass("TRN2", target_bir_lowering=False, detect_race_conditions=False)
    P = Prog(nc)
    es = P.es
    sbt = lambda name, shape, dt: es.enter_context(nc.sbuf_tensor(name, shape, dt))
    pst = lambda name, shape, dt: es.enter_context(nc.psum_tensor(name, shape, dt))
    dram = lambda name, shape, dt, kind: nc.dram_tensor(name, shape, dt, kind=kind).ap()

    xpro = dram("xpro", [TOWN, D], F32, "ExternalInput")
    xown = dram("xown", [TOWN, D], F32, "ExternalInput")
    cvec = dram("cvec", [16, 128], F32R, "ExternalInput")
    pmask = dram("pmask", [128, 1], F32, "ExternalInput")
    wmod_t = dram("wmod_t", [96, 16, 128, 128], F32R, "ExternalInput")
    wq_t = dram("wq_t", [2048, 2048], F32, "ExternalInput")
    wk_t = dram("wk_t", [2048, 2048], F32, "ExternalInput")
    wv_t = dram("wv_t", [2048, 2048], F32, "ExternalInput")
    wig_t = dram("wig_t", [2048, 2048], F32, "ExternalInput")
    wog_t = dram("wog_t", [2048, 2048], F32, "ExternalInput")
    wo_t = dram("wo_t", [2048, 2048], F32, "ExternalInput")
    wg_t = dram("wg_t", [2048, 16], F32, "ExternalInput")
    big_d = dram("big", [1, 2048], F32, "ExternalInput")
    bog_d = dram("bog", [1, 2048], F32, "ExternalInput")
    bgr_d = dram("bgr", [1, 16], F32, "ExternalInput")
    convw_d = dram("convw", [128, 64], F32, "ExternalInput")
    convb_d = dram("convb", [128, 16], F32, "ExternalInput")
    mnw_d = dram("mnw", [1, 2048], F32, "ExternalInput")
    mnb_d = dram("mnb", [1, 2048], F32, "ExternalInput")
    gnw_d = dram("gnw", [1, 2048], F32, "ExternalInput")
    gnb_d = dram("gnb", [1, 2048], F32, "ExternalInput")
    wg0_t = dram("wg0_t", [64, 16, 128, 128], F32, "ExternalInput")
    wg1_t = dram("wg1_t", [64, 16, 128, 128], F32, "ExternalInput")
    wl_c = dram("wl_c", [16, 64, 128, 128], F32, "ExternalInput")
    eye_d = dram("eye", [128, 128], F32R, "ExternalInput")
    eyeb_d = dram("eyeb", [128, 128], F32, "ExternalInput")
    tril_d = dram("tril", [128, 128], F32R, "ExternalInput")
    maskm_d = dram("maskm", [128, 128], F32, "ExternalInput")
    ones_d = dram("ones_d", [1, 128], F32, "ExternalInput")
    onesr_d = dram("onesr_d", [1, 128], F32R, "ExternalInput")

    out = dram("out", [TOWN, D], F32, "ExternalOutput")
    dbg = {}
    for nm, shp in (("mod", [128, 96]), ("hfm", [128, 16 * (SP + 3)]),
                    ("xconv", [128, 16 * SP]), ("ltall", [16, NTC * 128]),
                    ("bt", [128, 2048]), ("mem", [128, 2048]),
                    ("gam", [128, 16]), ("lg", [128, 16]), ("lsv", [128, 16]),
                    ("halo", [128, 48]), ("hz0", [128, 48]), ("lt1", [16, 4096]),
                    ("lta5", [16, 4096]), ("lta6", [16, 4096]), ("lta7", [16, 4096]),
                    ("xnb", [128, 2048]), ("lnst", [128, 16]), ("mm", [128, 2048]),
                    ("z", [128, 2048]), ("x2", [128, 2048]),
                    ("h2", [128, 2048]), ("srow", [1, 2048]), ("zfl", [128, 2048]),
                    ("gab", [128, 2048]), ("scrw", [128, 2048]),
                    ("gaf", [128, 16]), ("gar", [1, 2048]),
                    ("kn", [128, 2048]), ("q", [128, 2048]), ("og", [128, 2048])):
        if nm in _DBG:
            dbg[nm] = dram("dbg_" + nm, shp, F32, "ExternalOutput")

    bt_sp = dram("bt_sp", [TALL, D], BF16, "Internal")
    q_sp = dram("q_sp", [TOWN, D], BF16, "Internal")
    og_sp = dram("og_sp", [TOWN, D], BF16, "Internal")
    zf_sp = dram("zf_sp", [TOWN, D], BF16, "Internal")
    x2_sp = dram("x2_sp", [TOWN, D], F32, "Internal")
    h2_sp = dram("h2_sp", [D, TOWN], FP8, "Internal")
    wg8_d = dram("wg8", [64, 128, 4096], FP8, "Internal")
    wl8_d = dram("wl8", [16, 128, 8192], FP8, "Internal")

    P0 = pst("P0", [128, 1024], F32)
    P1 = pst("P1", [128, 1024], F32)
    P2 = pst("P2", [128, 512], F32)
    PT4 = pst("PT4", [128, 512], F32R)
    PT5 = pst("PT5", [128, 2048], BF16)

    eyeR = sbt("eyeR", [128, 128], F32R)
    eyeB = sbt("eyeB", [128, 128], BF16)
    trilR = sbt("trilR", [128, 128], F32R)
    maskm = sbt("maskm_s", [128, 128], F32)
    onesbf = sbt("onesbf_s", [1, 128], BF16)
    onesr = sbt("onesr_s", [1, 128], F32R)
    convw = sbt("convw_s", [128, 64], F32)
    convb = sbt("convb_s", [128, 16], F32)
    wgt_s = sbt("wgt_s", [128, 256], BF16)
    bigrow = sbt("bigrow", [1, 2048], BF16)
    bogrow = sbt("bogrow", [1, 2048], BF16)
    bgrow = sbt("bgrow", [1, 16], BF16)
    pm_s = sbt("pm_s", [128, 1], F32)
    sc16 = sbt("sc16", [16, 128], F32R)
    sccols = sbt("sccols", [128, 16], F32R)
    mod_sb = sbt("mod_sb", [128, 96], F32)
    cam_pro = sbt("cam_pro", [128, 16], F32)
    sam_pro = sbt("sam_pro", [128, 16], F32)
    cam = sbt("cam", [128, 16], F32)
    sam = sbt("sam", [128, 16], F32)
    cbm = sbt("cbm", [128, 16], F32)
    sbm = sbt("sbm", [128, 16], F32)
    gam_f = sbt("gam_f", [128, 16], F32)
    gbm_f = sbt("gbm_f", [128, 16], F32)
    gab = sbt("gab", [128, 2048], F32)
    halo = sbt("halo", [128, 48], BF16)

    smalls = {}
    for nm in ("st1", "st2", "mu", "ms", "mu2", "var", "rv", "rr", "nmr",
               "ss", "ssn", "ssr",
               "g1s", "g2s", "g1n", "g1r", "g2n", "g2r", "gmu", "gms", "gm2",
               "gva", "grv", "gsa", "gsb", "gsc"):
        smalls[nm] = sbt("sm_" + nm, [128, 16], F32)

    P.dma("sync", "dc", eyeR[:], eye_d)
    P.dma("sync", "dc", trilR[:], tril_d)
    P.dma("sync", "dc", maskm[:], maskm_d)
    P.dma("sync", "dc", convw[:], convw_d)
    P.dma("sync", "dc", convb[:], convb_d)
    P.dma("sync", "dc", pm_s[:], pmask)
    P.dma("sync", "dc", sc16[:], cvec)
    t_cdx = P.dma("sync", "dc", onesr[:], onesr_d)
    P.dma("gpsimd", "dgc", eyeB[:], eyeb_d)
    P.dma("gpsimd", "dgc", onesbf[:], ones_d)
    P.dma("gpsimd", "dgc", bigrow[:], big_d)
    P.dma("gpsimd", "dgc", bogrow[:], bog_d)
    t_cdg = P.dma("gpsimd", "dgc", bgrow[:], bgr_d)
    for k in range(16):
        t_cdg = P.dma("gpsimd", "dgc", wgt_s[:, k * 16:(k + 1) * 16],
                      wg_t[k * 128:(k + 1) * 128, :])

    with contextlib.ExitStack() as ph0:
        wmbuf = ph0.enter_context(nc.sbuf_tensor("wmbuf", [128, 2 * 2048], F32R))
        garowS = ph0.enter_context(nc.sbuf_tensor("garowS", [16, 128], F32R))
        garow2 = ph0.enter_context(nc.sbuf_tensor("garow2", [1, 2048], F32R))
        gxm_fr = ph0.enter_context(nc.sbuf_tensor("gxm_fr", [128, 16], F32R))

        P.wait("scalar", "dc", t_cdx)
        t_silu = P.emit("scalar", "S", lambda e: e.activation(sc16[:], sc16[:], AF.Silu))
        P.wait("tensor", "S", t_silu)
        t_tr = P.emit("tensor", "T", lambda e: e.transpose(PT4[:, 0:16], sc16[:], eyeR[0:16, 0:16]))
        P.wait("scalar", "T", t_tr)
        t_sc = P.emit("scalar", "S", lambda e: e.activation(sccols[:], PT4[:, 0:16], AF.Identity))

        scpad = ph0.enter_context(nc.sbuf_tensor("scpad", [128, 32], F32R))
        t_mz = P.emit("vector", "V", lambda e: e.memzero(scpad[:]))
        P.wait("vector", "V", t_mz)
        P.wait("vector", "S", t_sc)
        t_scp = P.emit("vector", "V", lambda e: e.tensor_copy(
            scpad[:].rearrange("p (k two) -> p k two", two=2)[:, :, 0:1],
            sccols[:].rearrange("p k -> p k ()")))
        P.wait("tensor", "V", t_scp)
        t_mm_hist = {}
        t_mm = 0
        for oc in range(96):
            slot = oc % 2
            if oc >= 2:
                P.wait("sync", "T", t_mm_hist[oc - 2])
            tgt = P.dma("sync", f"dw{slot}",
                        wmbuf[:, slot * 2048:(slot + 1) * 2048].rearrange("p (k q) -> p k q", k=16),
                        wmod_t[oc].rearrange("k p q -> p k q"))
            P.wait("tensor", f"dw{slot}", tgt)
            for k in range(16):
                fn = (lambda e, oc=oc, k=k, slot=slot:
                      e.matmul(P2[:, 2 * oc:2 * oc + 2],
                               wmbuf[:, slot * 2048 + k * 128:slot * 2048 + (k + 1) * 128],
                               scpad[:, 2 * k:2 * k + 2], start=(k == 0), stop=(k == 15)))
                if k == 15:
                    t_mm = P.emit("tensor", "T", fn)
                else:
                    P.emitq("tensor", fn)
            t_mm_hist[oc] = t_mm
        P.wait("vector", "T", t_mm)
        t_mod = P.emit("vector", "V", lambda e: e.tensor_copy(
            mod_sb[:].rearrange("p o -> p o ()"),
            P2[:, 0:192].rearrange("p (o two) -> p o two", two=2)[:, :, 0:1]))

        P.wait("vector", "V", t_mod)
        P.wait("vector", "dc", t_cdx)
        P.schain("vector", lambda e: e.tensor_scalar_add(cam[:], mod_sb[:, 16:32], 1.0))
        P.schain("vector", lambda e: e.tensor_copy(sam[:], mod_sb[:, 0:16]))
        P.schain("vector", lambda e: e.tensor_scalar_mul(cam_pro[:], cam[:], pm_s[:, 0:1]))
        P.schain("vector", lambda e: e.tensor_scalar_mul(sam_pro[:], sam[:], pm_s[:, 0:1]))
        P.schain("vector", lambda e: e.tensor_scalar_add(cbm[:], mod_sb[:, 64:80], 1.0))
        P.schain("vector", lambda e: e.tensor_copy(sbm[:], mod_sb[:, 48:64]))
        g1, g2, g3 = smalls["gsa"], smalls["gsb"], smalls["gsc"]
        for off, gdst in ((32, gam_f), (80, gbm_f)):
            P.schain("vector", lambda e, o=off: e.tensor_single_scalar(g1[:], mod_sb[:, o:o + 16], 0.0, ALU.is_ge))
            P.schain("vector", lambda e: e.tensor_scalar(g2[:], g1[:], 2.0, -1.0, ALU.mult, ALU.add))
            P.schain("vector", lambda e, o=off: e.scalar_tensor_tensor(
                g3[:], g2[:], 0.25, mod_sb[:, o:o + 16], ALU.mult, ALU.add))
            P.schain("vector", lambda e, o=off: e.tensor_scalar(
                g1[:], mod_sb[:, o:o + 16], -0.25, 0.25, ALU.max, ALU.min))
            P.schain("vector", lambda e, d=gdst: e.tensor_sub(d[:], g3[:], g1[:]))
        t_modp = P.emit("vector", "V", lambda e: e.tensor_copy(gxm_fr[:], gam_f[:]))
        if "gaf" in _DBG:
            P.tap(dbg["gaf"], gam_f[:], "V", t_modp)

        if "mod" in _DBG:
            P.wait("gpsimd", "V", t_modp)
            P.dma("gpsimd", "dbg", dbg["mod"], mod_sb[:])

        td = ts = tt2 = 0
        for which, (gsrc, gdst) in enumerate(((gam_f, gab),)):
            if which == 1:
                P.wait("vector", "dfl", td)
                P.wait("vector", "T", tt2)
                P.emit("vector", "V", lambda e, g=gsrc: e.tensor_copy(gxm_fr[:], g[:]))
            tv = P.cnt["V"]
            P.wait("tensor", "V", tv)
            if which == 1:
                P.wait("tensor", "S", ts)
            tt = P.emit("tensor", "T", lambda e: e.transpose(PT4[0:16, 128:256], gxm_fr[:], eyeR[:]))
            P.wait("scalar", "T", tt)
            if which == 1:
                P.wait("scalar", "dfl", td)
            ts = P.emit("scalar", "S", lambda e: e.activation(garowS[:], PT4[0:16, 128:256], AF.Identity))
            P.wait("sync", "S", ts)
            td = P.dma("sync", "dfl", garow2[0:1, :], garowS[:])
            if "gar" in _DBG:
                P.tap(dbg["gar"], garow2[0:1, :], "dfl", td)
            P.wait("tensor", "dfl", td)
            for s4 in range(4):
                tt2 = P.emit("tensor", "T", lambda e, s4=s4: e.matmul(
                    P0[:, (s4 % 2) * 512:(s4 % 2) * 512 + 512],
                    onesr[:], garow2[0:1, s4 * 512:(s4 + 1) * 512], start=True, stop=True))
                P.wait("vector", "T", tt2)
                tv2 = P.emit("vector", "V", lambda e, s4=s4, gd=gdst: e.tensor_copy(
                    gd[:, s4 * 512:(s4 + 1) * 512], P0[:, (s4 % 2) * 512:(s4 % 2) * 512 + 512]))
                P.wait("tensor", "V", tv2)
        P.barrier()
    if _PHASES <= 1:
        return _finalize(nc, P)

    span_es = contextlib.ExitStack()
    LT_all = span_es.enter_context(nc.sbuf_tensor("LT_all", [16, NTC * 128], F32))
    srowA = span_es.enter_context(nc.sbuf_tensor("srowA", [1, 2048], BF16))
    srowB = span_es.enter_context(nc.sbuf_tensor("srowB", [1, 2048], BF16))

    with contextlib.ExitStack() as phA:
        asb = lambda name, shape, dt: phA.enter_context(nc.sbuf_tensor(name, shape, dt))
        xb = asb("xb", [128, 2048], F32)
        xnb = asb("xnb", [128, 2048], BF16)
        scrA = asb("scrA", [128, 2048], F32R)
        scrB = asb("scrB", [128, 2048], F32R)
        h_fm = asb("h_fm", [128, 16 * (SP + 3)], BF16)
        xc_fm = asb("xc_fm", [128, 16 * SP], BF16)
        wsl = asb("wsl", [128, 2 * 16 * 512], BF16)
        k_sb = asb("k_sb", [128, 4 * 2048], BF16)
        v_sb = asb("v_sb", [128, 4 * 2048], BF16)
        btb = asb("btb", [128, 4 * 2048], BF16)
        cacc = asb("cacc", [128, 2 * 512], F32)
        gamv = asb("gamv", [128, 16], F32)
        lgv = asb("lgv", [128, 16], F32R)
        lsv = asb("lsv", [128, 16], F32R)

        t_hz = P.emit("vector", "V", lambda e: e.memzero(halo[:]))
        P.wait("vector", "V", t_hz)
        if "hz0" in _DBG:
            P.tap(dbg["hz0"], halo[:], "V", t_hz)

        pt4_free = [None] * 4
        pt4_used = [0] * 4
        pt5_free = [None, None]
        t_xn_tr = 0
        pslab_free = [None] * 4
        wsl_t_done = [0, 0]
        t_xn_prev = 0
        t_proj_done_sp = 0
        t_gamma_evac_prev = 0
        t_scrA_vuse = [0]
        slabslot = [0]

        def pslab_ap(i):
            return (P0 if i < 2 else P1)[:, (i % 2) * 512:(i % 2) * 512 + 512]

        for sp in range(NSP):
            own = sp >= 4
            tcs = list(range(sp * 4, sp * 4 + 4))
            if f"lta{sp}" in _DBG:
                P.tap(dbg[f"lta{sp}"], LT_all[:], "S", P.cnt["S"])
            for j, tc in enumerate(tcs):
                src = xpro if tc < TC0 else xown
                r0 = (tc * 128) % 2048
                P.wait("sync", "S", t_xn_prev)
                t_x = P.dma("sync", "dxx", xb[:], src[r0:r0 + 128, :])
                P.wait("scalar", "dxx", t_x)
                P.wait("scalar", "V", t_scrA_vuse[0])
                P.emitq("scalar", lambda e: e.activation(
                    scrA[:], xb[:], AF.Square, accum_out=smalls["st2"][:, 0:1]))
                t_st = P.emit("scalar", "S", lambda e: e.activation(
                    scrA[:], xb[:], AF.Identity, accum_out=smalls["st1"][:, 0:1]))
                P.wait("vector", "S", t_st)
                P.schain("vector", lambda e: e.tensor_scalar_mul(smalls["mu"][:, 0:1], smalls["st1"][:, 0:1], 1.0 / 2048))
                t_v0 = P.emit("vector", "V", lambda e: e.tensor_scalar_mul(smalls["ms"][:, 0:1], smalls["st2"][:, 0:1], 1.0 / 2048))
                P.wait("scalar", "V", t_v0)
                t_s0 = P.emit("scalar", "S", lambda e: e.activation(smalls["mu2"][:, 0:1], smalls["mu"][:, 0:1], AF.Square))
                P.wait("vector", "S", t_s0)
                P.schain("vector", lambda e: e.tensor_sub(smalls["var"][:, 0:1], smalls["ms"][:, 0:1], smalls["mu2"][:, 0:1]))
                P.schain("vector", lambda e: e.tensor_scalar_add(smalls["var"][:, 0:1], smalls["var"][:, 0:1], EPS))
                P.wait("vector", "V", P.cnt["V"])
                t_v1 = P.emit("vector", "V", lambda e: e.reciprocal(smalls["rv"][:, 0:1], smalls["var"][:, 0:1]))
                P.wait("scalar", "V", t_v1)
                t_s1 = P.emit("scalar", "S", lambda e: e.activation(smalls["rr"][:, 0:1], smalls["rv"][:, 0:1], AF.Sqrt))
                P.wait("vector", "S", t_s1)
                t_v2 = P.emit("vector", "V", lambda e: e.scalar_tensor_tensor(
                    smalls["nmr"][:, 0:1], smalls["mu"][:, 0:1], -1.0, smalls["rr"][:, 0:1], ALU.mult, ALU.mult))
                P.wait("scalar", "V", t_v2)
                P.wait("scalar", "T", t_xn_tr[0] if isinstance(t_xn_tr, list) else t_xn_tr)
                t_xn = P.emit("scalar", "S", lambda e: e.activation(
                    xnb[:], xb[:], AF.Identity, bias=smalls["nmr"][:, 0:1], scale=smalls["rr"][:, 0:1]))
                t_xn_prev = t_xn
                if "xnb" in _DBG and tc == _DBG_SEL[1]:
                    P.tap(dbg["xnb"], xnb[:], "S", t_xn)
                    P.emitq("vector", lambda e: e.tensor_copy(dbg and smalls["gsa"][:, 0:1], smalls["st1"][:, 0:1]) if False else e.tensor_copy(smalls["gsa"][:, 0:1], smalls["st1"][:, 0:1]))
                    P.emitq("vector", lambda e: e.tensor_copy(smalls["gsa"][:, 1:2], smalls["st2"][:, 0:1]))
                    P.emitq("vector", lambda e: e.tensor_copy(smalls["gsa"][:, 2:3], smalls["mu"][:, 0:1]))
                    P.emitq("vector", lambda e: e.tensor_copy(smalls["gsa"][:, 3:4], smalls["var"][:, 0:1]))
                    P.emitq("vector", lambda e: e.tensor_copy(smalls["gsa"][:, 4:5], smalls["rv"][:, 0:1]))
                    P.emitq("vector", lambda e: e.tensor_copy(smalls["gsa"][:, 5:6], smalls["rr"][:, 0:1]))
                    tln = P.emit("vector", "V", lambda e: e.tensor_copy(smalls["gsa"][:, 6:7], smalls["nmr"][:, 0:1]))
                    P.tap(dbg["lnst"], smalls["gsa"][:], "V", tln)
                camx = cam_pro if tc < TC0 else cam
                samx = sam_pro if tc < TC0 else sam
                P.wait("tensor", "S", t_xn)
                for grp in range(2):
                    P.waitp("tensor", pt5_free[grp])
                    for cc in range(8):
                        c = grp * 8 + cc
                        fn = lambda e, c=c, cc=cc, grp=grp: e.transpose(
                            PT5[:, grp * 1024 + cc * 128:grp * 1024 + (cc + 1) * 128],
                            xnb[:, c * 128:(c + 1) * 128], eyeB[:])
                        if cc == 7:
                            tt = P.emit("tensor", "T", fn)
                        else:
                            P.emitq("tensor", fn)
                    pt4_used[3] = max(pt4_used[3], 0)
                    P.wait("vector", "T", tt)
                    P.wait("vector", "T", t_proj_done_sp)
                    tv = 0
                    for cc in range(8):
                        c = grp * 8 + cc
                        tv = P.emit("vector", "V", lambda e, c=c, cc=cc, grp=grp, j=j, camx=camx, samx=samx: e.tensor_scalar(
                            h_fm[:, c * (SP + 3) + 3 + j * 128: c * (SP + 3) + 3 + (j + 1) * 128],
                            PT5[:, grp * 1024 + cc * 128:grp * 1024 + (cc + 1) * 128],
                            camx[:, c:c + 1], samx[:, c:c + 1], ALU.mult, ALU.add))
                    pt5_free[grp] = ("V", tv)
                t_xn_tr = tt
            if "halo" in _DBG and sp == _DBG_SEL[0]:
                P.tap(dbg["halo"], halo[:], "V", P.cnt["V"])
            for c in range(16):
                P.emitq("vector", lambda e, c=c: e.tensor_copy(
                    h_fm[:, c * (SP + 3): c * (SP + 3) + 3], halo[:, c * 3:c * 3 + 3]))
            P.wait("scalar", "T", t_proj_done_sp)
            t_silu_prev = 0
            for c in range(16):
                base = c * (SP + 3)
                P.wait("vector", "S", t_silu_prev)
                P.emitq("vector", lambda e, c=c, base=base: e.tensor_scalar(
                    cacc[:, 0:512], h_fm[:, base:base + 512],
                    convw[:, c * 4:c * 4 + 1], convb[:, c:c + 1], ALU.mult, ALU.add))
                P.emitq("vector", lambda e, c=c, base=base: e.scalar_tensor_tensor(
                    cacc[:, 512:1024], h_fm[:, base + 1:base + 513],
                    convw[:, c * 4 + 1:c * 4 + 2], cacc[:, 0:512], ALU.mult, ALU.add))
                P.emitq("vector", lambda e, c=c, base=base: e.scalar_tensor_tensor(
                    cacc[:, 0:512], h_fm[:, base + 2:base + 514],
                    convw[:, c * 4 + 2:c * 4 + 3], cacc[:, 512:1024], ALU.mult, ALU.add))
                tvc = P.emit("vector", "V", lambda e, c=c, base=base: e.scalar_tensor_tensor(
                    cacc[:, 512:1024], h_fm[:, base + 3:base + 515],
                    convw[:, c * 4 + 3:c * 4 + 4], cacc[:, 0:512], ALU.mult, ALU.add))
                P.wait("scalar", "V", tvc)
                t_silu_prev = P.emit("scalar", "S", lambda e, c=c: e.activation(
                    xc_fm[:, c * 512:(c + 1) * 512], cacc[:, 512:1024], AF.Silu))
            for c in range(16):
                P.emitq("vector", lambda e, c=c: e.tensor_copy(
                    halo[:, c * 3:c * 3 + 3], h_fm[:, c * (SP + 3) + 512:c * (SP + 3) + 515]))
            t_hevac_sp = P.cnt["V"]
            t_conv_done = P.cnt["S"]

            if "hfm" in _DBG and sp == _DBG_SEL[0]:
                P.tap(dbg["hfm"], h_fm[:], "V", t_hevac_sp)
            if "xconv" in _DBG and sp == _DBG_SEL[0]:
                P.tap(dbg["xconv"], xc_fm[:], "S", t_conv_done)

            passes = [("k", wk_t, True, None), ("v", wv_t, True, None),
                      ("ig", wig_t, False, bigrow)]
            if own:
                passes += [("q", wq_t, True, None), ("og", wog_t, False, bogrow)]

            for pname, wsrc, use_h, brow in passes:
                if pname in ("q", "og"):
                    P.wait("scalar", "V", P.cnt["V"])
                for slab in range(4):
                    wslot = slab % 2
                    P.wait("gpsimd", "T", wsl_t_done[wslot])
                    tw = 0
                    for k in range(16):
                        tw = P.dma("gpsimd", f"dws{wslot}",
                                   wsl[:, wslot * 8192 + k * 512:wslot * 8192 + (k + 1) * 512],
                                   wsrc[k * 128:(k + 1) * 128, slab * 512:(slab + 1) * 512])
                    for j, tc in enumerate(tcs):
                        sslot = slabslot[0] % 4
                        slabslot[0] += 1
                        pap = pslab_ap(sslot)
                        P.wait("tensor", f"dws{wslot}", tw)
                        P.wait("tensor", "dgc", t_cdg)
                        P.waitp("tensor", pslab_free[sslot])
                        P.wait("tensor", "V", t_hevac_sp)
                        P.wait("tensor", "S", t_conv_done)
                        nmm = 16
                        for k in range(16):
                            if use_h:
                                lhs = h_fm[:, k * (SP + 3) + 3 + j * 128:k * (SP + 3) + 3 + (j + 1) * 128]
                            else:
                                lhs = xc_fm[:, k * 512 + j * 128:k * 512 + (j + 1) * 128]
                            last = (k == 15) and (brow is None)
                            fn = lambda e, k=k, lhs=lhs, pap=pap, wslot=wslot, last=last: e.matmul(
                                pap, lhs, wsl[:, wslot * 8192 + k * 512:wslot * 8192 + (k + 1) * 512],
                                start=(k == 0), stop=last)
                            if last:
                                tmm = P.emit("tensor", "T", fn)
                            else:
                                P.emitq("tensor", fn)
                        if brow is not None:
                            tmm = P.emit("tensor", "T", lambda e, pap=pap, brow=brow, slab=slab: e.matmul(
                                pap, onesbf[:], brow[0:1, slab * 512:(slab + 1) * 512],
                                start=False, stop=True))
                        if pname == "ig" and slab == 0:
                            P.wait("tensor", "S", t_gamma_evac_prev)
                            for k in range(16):
                                P.emitq("tensor", lambda e, k=k, j=j: e.matmul(
                                    P2[:, j * 16:(j + 1) * 16],
                                    xc_fm[:, k * 512 + j * 128:k * 512 + (j + 1) * 128],
                                    wgt_s[:, k * 16:(k + 1) * 16], start=(k == 0), stop=False))
                            tmm = P.emit("tensor", "T", lambda e, j=j: e.matmul(
                                P2[:, j * 16:(j + 1) * 16], onesbf[:], bgrow[:],
                                start=False, stop=True))
                        wsl_t_done[wslot] = tmm
                        P.wait("scalar", "T", tmm)
                        if pname in ("k", "q"):
                            te = P.emit("scalar", "S", lambda e, j=j, slab=slab, pap=pap: e.activation(
                                k_sb[:, j * 2048 + slab * 512:j * 2048 + (slab + 1) * 512], pap, AF.Identity))
                            pslab_free[sslot] = ("S", te)
                        elif pname == "v":
                            P.wait("scalar", f"dsv{j}", P.cnt[f"dsv{j}"])
                            te = P.emit("scalar", "S", lambda e, j=j, slab=slab, pap=pap: e.activation(
                                v_sb[:, j * 2048 + slab * 512:j * 2048 + (slab + 1) * 512], pap, AF.Identity))
                            pslab_free[sslot] = ("S", te)
                        elif pname == "og":
                            P.wait("scalar", f"dsv{j}", P.cnt[f"dsv{j}"])
                            te = P.emit("scalar", "S", lambda e, j=j, slab=slab, pap=pap: e.activation(
                                v_sb[:, j * 2048 + slab * 512:j * 2048 + (slab + 1) * 512], pap, AF.Sigmoid))
                            pslab_free[sslot] = ("S", te)
                        else:
                            P.wait("scalar", "V", t_scrA_vuse[0])
                            ts1 = P.emit("scalar", "S", lambda e, pap=pap, slab=slab: e.activation(
                                scrA[:, slab * 512:(slab + 1) * 512], pap, AF.Sigmoid))
                            P.wait("vector", "S", ts1)
                            P.wait("vector", f"dsb{j}", P.cnt[f"dsb{j}"])
                            tvb = P.emit("vector", "V", lambda e, j=j, slab=slab: e.tensor_mul(
                                btb[:, j * 2048 + slab * 512:j * 2048 + (slab + 1) * 512],
                                k_sb[:, j * 2048 + slab * 512:j * 2048 + (slab + 1) * 512],
                                scrA[:, slab * 512:(slab + 1) * 512]))
                            pslab_free[sslot] = ("V", tvb)
                            t_scrA_vuse[0] = tvb

                if pname in ("v", "q"):
                    for j, tc in enumerate(tcs):
                        P.wait("scalar", "V", P.cnt["V"])
                        tq1 = P.emit("scalar", "S", lambda e, j=j: e.activation(
                            scrB[:], k_sb[:, j * 2048:(j + 1) * 2048], AF.Square))
                        P.wait("vector", "S", tq1)
                        tv3 = P.emit("vector", "V", lambda e: e.tensor_reduce(
                            smalls["ss"][:].rearrange("p s -> p s ()"),
                            scrB[:].rearrange("p (s d) -> p s d", s=16), AX.X, ALU.add))
                        P.wait("scalar", "V", tv3)
                        ts3 = P.emit("scalar", "S", lambda e: e.activation(
                            smalls["ssn"][:], smalls["ss"][:], AF.Sqrt))
                        P.wait("vector", "S", ts3)
                        P.schain("vector", lambda e: e.tensor_scalar_max(
                            smalls["ssn"][:], smalls["ssn"][:], 1e-12))
                        P.wait("vector", "V", P.cnt["V"])
                        tv4 = P.emit("vector", "V", lambda e: e.reciprocal(
                            smalls["ssr"][:], smalls["ssn"][:]))
                        P.wait("vector", "V", tv4)
                        ksl = k_sb[:, j * 2048:(j + 1) * 2048]
                        k3 = ksl.rearrange("p (s d) -> p s d", s=16)
                        r3 = bass.broadcast_tensor_aps(
                            smalls["ssr"][:].rearrange("p s -> p s ()"), k3)[0]
                        if pname == "v":
                            P.emitq("vector", lambda e, k3=k3, r3=r3: e.tensor_tensor(
                                scrB[:].rearrange("p (s d) -> p s d", s=16), k3, r3, ALU.mult))
                            if "kn" in _DBG and tc == _DBG_SEL[1]:
                                P.emitq("vector", lambda e: e.tensor_copy(xnb[:], scrB[:]))
                                tdk = P.emit("vector", "V", lambda e: e.tensor_copy(xnb[:], scrB[:]))
                                P.wait("gpsimd", "V", tdk)
                                P.dma("gpsimd", "dbg", dbg["kn"], xnb[:])
                            P.emit("vector", "V", lambda e, j=j, ksl=ksl: e.tensor_mul(
                                ksl, scrB[:], v_sb[:, j * 2048:(j + 1) * 2048]))
                        else:
                            P.wait("vector", f"dsb{j}", P.cnt[f"dsb{j}"])
                            tq5 = P.emit("vector", "V", lambda e, j=j, k3=k3, r3=r3: e.tensor_tensor(
                                btb[:, j * 2048:(j + 1) * 2048].rearrange("p (s d) -> p s d", s=16),
                                k3, r3, ALU.mult))
                            P.wait("sync", "V", tq5)
                            P.dma("sync", f"dsb{j}", q_sp[(tc - TC0) * 128:(tc - TC0) * 128 + 128, :],
                                  btb[:, j * 2048:(j + 1) * 2048])
                            if "q" in _DBG and tc == _DBG_SEL[1]:
                                P.wait("gpsimd", "V", tq5)
                                P.dma("gpsimd", "dbg", dbg["q"], btb[:, j * 2048:(j + 1) * 2048])
                if pname == "ig":
                    for j, tc in enumerate(tcs):
                        tbt = P.cnt["V"]
                        P.wait("sync", "V", tbt)
                        t_btsp = P.dma("sync", f"dsb{j}", bt_sp[tc * 128:(tc + 1) * 128, :],
                                       btb[:, j * 2048:(j + 1) * 2048])
                        if "bt" in _DBG and tc == _DBG_SEL[1]:
                            P.wait("gpsimd", "V", tbt)
                            P.dma("gpsimd", "dbg", dbg["bt"], btb[:, j * 2048:(j + 1) * 2048])
                        P.wait("scalar", "T", P.cnt["T"])
                        P.emit("scalar", "S", lambda e, j=j: e.activation(
                            gamv[:], P2[:, j * 16:(j + 1) * 16], AF.Sigmoid))
                        P.wait("vector", "S", P.cnt["S"])
                        if "gam" in _DBG and tc == _DBG_SEL[1]:
                            P.tap(dbg["gam"], gamv[:], "S", P.cnt["S"])
                        tv6 = P.emit("vector", "V", lambda e: e.tensor_scalar_max(gamv[:], gamv[:], 1e-30))
                        P.wait("scalar", "V", tv6)
                        ts6 = P.emit("scalar", "S", lambda e: e.activation(lgv[:], gamv[:], AF.Ln))
                        t_gamma_evac_prev = ts6
                        if "lg" in _DBG and tc == _DBG_SEL[1]:
                            P.tap(dbg["lg"], lgv[:], "S", ts6)
                        P.wait("tensor", "S", ts6)
                        tt6 = P.emit("tensor", "T", lambda e: e.matmul(
                            P2[:, 80:96], trilR[:], lgv[:], start=True, stop=True))
                        P.wait("scalar", "T", tt6)
                        ts7 = P.emit("scalar", "S", lambda e: e.activation(lsv[:], P2[:, 80:96], AF.Identity))
                        if "lsv" in _DBG and tc == _DBG_SEL[1]:
                            P.tap(dbg["lsv"], lsv[:], "S", ts7)
                        P.wait("tensor", "S", ts7)
                        P.waitp("tensor", pt4_free[3])
                        tt7 = P.emit("tensor", "T", lambda e: e.transpose(PT4[0:16, 384:512], lsv[:], eyeR[:]))
                        pt4_used[3] = tt7
                        P.wait("scalar", "T", tt7)
                        ts8 = P.emit("scalar", "S", lambda e, tc=tc: e.activation(
                            LT_all[:, tc * 128:(tc + 1) * 128], PT4[0:16, 384:512], AF.Identity))
                        pt4_free[3] = ("S", ts8)
                        if "lt1" in _DBG and tc == _DBG_SEL[1]:
                            P.tap(dbg["lt1"], LT_all[:], "S", ts8)
                if pname == "og":
                    for j, tc in enumerate(tcs):
                        tev = P.cnt["S"]
                        P.wait("sync", "S", tev)
                        P.dma("sync", f"dsv{j}", og_sp[(tc - TC0) * 128:(tc - TC0) * 128 + 128, :],
                              v_sb[:, j * 2048:(j + 1) * 2048])
                        if "og" in _DBG and tc == _DBG_SEL[1]:
                            P.wait("gpsimd", "S", tev)
                            P.dma("gpsimd", "dbg", dbg["og"], v_sb[:, j * 2048:(j + 1) * 2048])
            t_proj_done_sp = P.cnt["T"]
        P.barrier()
    if _PHASES <= 2:
        return _finalize(nc, P)

    if "ltall" in _DBG:
        P.tap(dbg["ltall"], LT_all[:], "S", P.cnt["S"])

    with contextlib.ExitStack() as phB:
        bsb = lambda name, shape, dt: phB.enter_context(nc.sbuf_tensor(name, shape, dt))
        TAb = bsb("TAb", [2, 2048], F32)
        TBb = bsb("TBb", [2, 2048], F32)
        pexp_a = bsb("pexp_a", [16, NTC * 128], BF16)
        maskrep = bsb("maskrep", [128, 2048], F32)
        mnwb = bsb("mnwb", [128, 2048], BF16)
        mnbb = bsb("mnbb", [128, 2048], BF16)
        gnwb = bsb("gnwb", [128, 2048], BF16)
        gnbb = bsb("gnbb", [128, 2048], BF16)
        scrAB = bsb("scrA2", [128, 2048], F32)
        scrBB = bsb("scrB2", [128, 2048], F32)
        zt_sb = bsb("zt_sb", [128, 2048], BF16)
        mm_sb = [bsb(f"mm_sb{i}", [128, 2048], BF16) for i in range(2)]
        mo_sb = [bsb(f"mo_sb{i}", [128, 2048], BF16) for i in range(2)]
        btld = [bsb(f"btld{i}", [128, 2048], BF16) for i in range(2)]
        pfl2 = [bsb(f"pfl2_{i}", [1, 2048], BF16) for i in range(2)]
        memf = [bsb(f"memf{i}", [128, 2048], F32) for i in range(2)]
        qld = [bsb(f"qld{i}", [128, 2048], BF16) for i in range(2)]
        ogld = [bsb(f"ogld{i}", [128, 2048], BF16) for i in range(2)]
        z_sb = [bsb(f"z_sb{i}", [128, 2048], BF16) for i in range(2)]
        gsum1 = [bsb(f"gsum1_{i}", [128, 16], F32) for i in range(2)]
        gsum2 = [bsb(f"gsum2_{i}", [128, 16], F32) for i in range(2)]
        srow = [srowA, srowB]

        for dst, srcd in ((mnwb, mnw_d), (mnbb, mnb_d), (gnwb, gnw_d), (gnbb, gnb_d)):
            src_b = bass.AP(srcd.tensor, 0, [[0, 128], [1, 2048]])
            P.dma("gpsimd", "dbc", dst[:], src_b)
        src_m = bass.AP(maskm_d.tensor, 0, [[128, 128], [0, 16], [1, 128]])
        t_bcm = P.dma("gpsimd", "dbc", maskrep[:].rearrange("p (s t) -> p s t", s=16), src_m)
        tpe = P.emit("scalar", "S", lambda e: e.activation(pexp_a[:], LT_all[:], AF.Exp))
        P.wait("gpsimd", "dbc", t_bcm)
        P.emitq("vector", lambda e: e.memset(TAb[:], 1.0))
        tmem = P.emit("vector", "V", lambda e: e.memset(TBb[:], -1.0))

        t_mask = [0, 0]
        t_exp = [0, 0]
        t_scan = [0, 0]
        t_srdma = [0, 0]
        t_rowS = [0, 0]
        t_gnq = [0, 0]
        t_evac = [0, 0]
        t_evown = [False, False]
        t_gnv = [0, 0]
        t_gnsq = [0, 0]
        t_dmm_prev = 0
        t_zsp = 0
        t_tzv = 0
        t_ztr = [0, 0]
        tq_ld = [0, 0]
        tog_ld = [0, 0]
        gn_pend = None

        def emit_gn(tc):
            pi = tc & 1
            j = tc - TC0
            m3 = memf[pi][:].rearrange("p (s d) -> p s d", s=16)
            sa3 = scrAB[:].rearrange("p (s d) -> p s d", s=16)
            sb3 = scrBB[:].rearrange("p (s d) -> p s d", s=16)
            g1s, g2s = gsum1[pi], gsum2[pi]

            def stats(tg_r1, tg_r2):
                P.wait("vector", "V", max(tg_r1, tg_r2))
                P.schain("vector", lambda e: e.tensor_scalar_mul(smalls["gmu"][:], g1s[:], 1.0 / 128))
                P.schain("vector", lambda e: e.tensor_scalar_mul(smalls["gms"][:], g2s[:], 1.0 / 128))
                P.schain("vector", lambda e: e.tensor_mul(smalls["gm2"][:], smalls["gmu"][:], smalls["gmu"][:]))
                P.schain("vector", lambda e: e.tensor_sub(smalls["gva"][:], smalls["gms"][:], smalls["gm2"][:]))
                P.schain("vector", lambda e: e.tensor_scalar_add(smalls["gva"][:], smalls["gva"][:], EPS))
                P.wait("vector", "V", P.cnt["V"])
                tvr = P.emit("vector", "V", lambda e: e.reciprocal(smalls["grv"][:], smalls["gva"][:]))
                P.wait("scalar", "V", tvr)
                return P.emit("scalar", "S", lambda e: e.activation(smalls["g1r"][:], smalls["grv"][:], AF.Sqrt))

            tg_r1 = P.emit("vector", "V", lambda e: e.tensor_reduce(
                g1s[:].rearrange("p s -> p s ()"), m3, AX.X, ALU.add))
            P.wait("scalar", "V", P.cnt["V"])
            ts_sq = P.emit("scalar", "S", lambda e: e.activation(scrBB[:], memf[pi][:], AF.Square))
            t_gnsq[pi] = ts_sq
            P.wait("vector", "S", ts_sq)
            tg_r2 = P.emit("vector", "V", lambda e: e.tensor_reduce(
                g2s[:].rearrange("p s -> p s ()"), sb3, AX.X, ALU.add))
            ts_r = stats(tg_r1, tg_r2)
            P.wait("vector", "S", ts_r)
            for s in range(16):
                fn = lambda e, s=s: e.tensor_scalar(
                    scrAB[:, s * 128:(s + 1) * 128], memf[pi][:, s * 128:(s + 1) * 128],
                    smalls["gmu"][:, s:s + 1], smalls["g1r"][:, s:s + 1],
                    ALU.subtract, ALU.mult)
                P.emitq("vector", fn)
            P.emitq("vector", lambda e: e.tensor_mul(scrBB[:], scrAB[:], mnwb[:]))
            P.emitq("vector", lambda e: e.tensor_add(scrAB[:], scrBB[:], mnbb[:]))
            P.wait("vector", "dql", tq_ld[pi])
            P.wait("vector", "dol", tog_ld[pi])
            tv_q = P.emit("vector", "V", lambda e: e.tensor_mul(scrBB[:], scrAB[:], qld[pi][:]))
            t_gnq[pi] = tv_q
            tg_r1 = P.emit("vector", "V", lambda e: e.tensor_reduce(
                g1s[:].rearrange("p s -> p s ()"), sb3, AX.X, ALU.add))
            P.wait("scalar", "V", tv_q)
            ts_sq = P.emit("scalar", "S", lambda e: e.activation(scrAB[:], scrBB[:], AF.Square))
            P.wait("vector", "S", ts_sq)
            tg_r2 = P.emit("vector", "V", lambda e: e.tensor_reduce(
                g2s[:].rearrange("p s -> p s ()"), sa3, AX.X, ALU.add))
            ts_r = stats(tg_r1, tg_r2)
            P.wait("vector", "S", ts_r)
            for s in range(16):
                fn = lambda e, s=s: e.tensor_scalar(
                    scrAB[:, s * 128:(s + 1) * 128], scrBB[:, s * 128:(s + 1) * 128],
                    smalls["gmu"][:, s:s + 1], smalls["g1r"][:, s:s + 1],
                    ALU.subtract, ALU.mult)
                P.emitq("vector", fn)
            P.emitq("vector", lambda e: e.tensor_mul(scrBB[:], scrAB[:], gnwb[:]))
            P.emitq("vector", lambda e: e.tensor_add(scrAB[:], scrBB[:], gnbb[:]))
            P.wait("vector", "T", t_ztr[pi])
            tv_z = P.emit("vector", "V", lambda e: e.tensor_mul(z_sb[pi][:], scrAB[:], ogld[pi][:]))
            t_gnv[pi] = tv_z
            if "z" in _DBG and tc == _DBG_SEL[1]:
                P.tap(dbg["z"], z_sb[pi][:], "V", tv_z)
            P.wait("tensor", "V", tv_z)
            P.wait("tensor", "V", t_tzv)
            for c in range(16):
                fn = lambda e, c=c: e.transpose(
                    PT5[:, c * 128:(c + 1) * 128], z_sb[pi][:, c * 128:(c + 1) * 128], eyeB[:])
                if c == 15:
                    tzf = P.emit("tensor", "T", fn)
                else:
                    P.emitq("tensor", fn)
            t_ztr[pi] = tzf
            P.wait("vector", "T", tzf)
            P.wait("vector", "dzf", t_zsp)
            tzv = P.emit("vector", "V", lambda e: e.tensor_copy(zt_sb[:], PT5[:]))
            P.wait("sync", "V", tzv)
            return tzv, j

        for tc in range(NTC):
            pi = tc & 1
            own = tc >= TC0
            j = tc - TC0
            P.wait("sync", "V", tmem)
            P.wait("sync", "S", tpe)
            P.wait("sync", "T", t_dmm_prev)
            tta = P.dma("sync", "dta", TAb[1:2, :], LT_all[:, tc * 128:(tc + 1) * 128])
            tfl = P.dma("sync", "dtb", TBb[0:1, :], LT_all[:, tc * 128:(tc + 1) * 128])
            P.wait("sync", "T", t_scan[pi])
            tpfl = P.dma("sync", "dpf", pfl2[pi][0:1, :], pexp_a[:, tc * 128:(tc + 1) * 128])
            tbt = P.dma("sync", "dbl", btld[pi][:], bt_sp[tc * 128:(tc + 1) * 128, :])
            if own:
                P.wait("sync", "V", t_gnv[pi])
                tq_ld[pi] = P.dma("sync", "dql", qld[pi][:], q_sp[j * 128:(j + 1) * 128, :])
                tog_ld[pi] = P.dma("sync", "dol", ogld[pi][:], og_sp[j * 128:(j + 1) * 128, :])
            P.wait("tensor", "dta", tta)
            P.wait("tensor", "dtb", tfl)
            P.wait("tensor", "V", tmem)
            opi = 1 - pi
            if t_evown[opi]:
                P.wait("tensor", "V", t_evac[opi])
            P.wait("tensor", "S", t_rowS[opi])
            for h in range(16):
                pap = (P0 if h < 8 else P1)[:, (h % 8) * 128:(h % 8) * 128 + 128]
                fn = lambda e, h=h, pap=pap: e.matmul(
                    pap, TAb[0:2, h * 128:(h + 1) * 128], TBb[0:2, h * 128:(h + 1) * 128],
                    start=True, stop=True)
                if h == 15:
                    t_dmm = P.emit("tensor", "T", fn)
                else:
                    P.emitq("tensor", fn)
            t_dmm_prev = t_dmm
            P.wait("vector", "T", t_dmm)
            P.wait("vector", "S", t_exp[pi])
            P.emitq("vector", lambda e, pi=pi: e.scalar_tensor_tensor(
                mo_sb[pi][:, 0:1024], P0[:], 1.0, maskrep[:, 0:1024], ALU.mult, ALU.add))
            t_mask[pi] = P.emit("vector", "V", lambda e, pi=pi: e.scalar_tensor_tensor(
                mo_sb[pi][:, 1024:2048], P1[:], 1.0, maskrep[:, 1024:2048], ALU.mult, ALU.add))
            P.wait("scalar", "V", t_mask[pi])
            P.wait("scalar", "T", t_scan[pi])
            P.wait("scalar", "S", tpe)
            t_exp[pi] = P.emit("scalar", "S", lambda e, pi=pi: e.activation(
                mm_sb[pi][:], mo_sb[pi][:], AF.Exp))
            if "mm" in _DBG and tc == _DBG_SEL[1]:
                P.tap(dbg["mm"], mm_sb[pi][:], "S", t_exp[pi])
            P.wait("tensor", "S", t_exp[pi])
            P.wait("tensor", "dbl", tbt)
            P.wait("tensor", "dpf", tpfl)
            P.wait("tensor", "V", t_mask[pi])
            if tc > 0:
                P.wait("tensor", "dsr", t_srdma[opi])
            sp_ = srow[opi]
            for h in range(16):
                pap = (P0 if h < 8 else P1)[:, (h % 8) * 128:(h % 8) * 128 + 128]
                fn = lambda e, h=h, pap=pap, pi=pi, st=(tc == 0): e.matmul(
                    pap, mm_sb[pi][:, h * 128:(h + 1) * 128], btld[pi][:, h * 128:(h + 1) * 128],
                    start=True, stop=st)
                fnc = (None if tc == 0 else
                       (lambda e, h=h, pap=pap, pi=pi, sp_=sp_: e.matmul(
                           pap, pfl2[pi][0:1, h * 128:(h + 1) * 128],
                           sp_[0:1, h * 128:(h + 1) * 128], start=False, stop=True)))
                if fnc is None:
                    if h == 15:
                        t_sc = P.emit("tensor", "T", fn)
                    else:
                        P.emitq("tensor", fn)
                else:
                    P.emitq("tensor", fn)
                    if h == 15:
                        t_sc = P.emit("tensor", "T", fnc)
                    else:
                        P.emitq("tensor", fnc)
            t_scan[pi] = t_sc
            P.wait("scalar", "T", t_sc)
            P.wait("scalar", "V", t_gnq[pi])
            P.wait("scalar", "dsr", t_srdma[pi])
            P.emitq("scalar", lambda e, pi=pi: e.activation(
                memf[pi][96:128, 0:1024], P0[96:128, :], AF.Identity))
            ts_row = P.emit("scalar", "S", lambda e, pi=pi: e.activation(
                memf[pi][96:128, 1024:2048], P1[96:128, :], AF.Identity))
            t_rowS[pi] = ts_row
            P.wait("gpsimd", "S", ts_row)
            t_srdma[pi] = P.dma("gpsimd", "dsr", srow[pi][0:1, :], memf[pi][127:128, :])
            if "srow" in _DBG and tc == _DBG_SEL[1]:
                P.tap(dbg["srow"], srow[pi][0:1, :], "dsr", t_srdma[pi])
            if own or ("mem" in _DBG and tc == _DBG_SEL[1]):
                P.wait("vector", "T", t_sc)
                P.wait("vector", "S", t_gnsq[pi])
                P.wait("vector", "dsr", t_srdma[pi])
                P.emitq("vector", lambda e, pi=pi: e.tensor_copy(memf[pi][:, 0:1024], P0[:]))
                t_evac[pi] = P.emit("vector", "V", lambda e, pi=pi: e.tensor_copy(
                    memf[pi][:, 1024:2048], P1[:]))
                t_evown[pi] = True
                if "mem" in _DBG and tc == _DBG_SEL[1]:
                    P.tap(dbg["mem"], memf[pi][:], "V", t_evac[pi])
            else:
                t_evown[pi] = False
            if gn_pend is not None:
                ptc = gn_pend
                gn_pend = None
                tzv2, jp2 = emit_gn(ptc)
                t_zsp = P.dma("sync", "dzf", zf_sp[jp2 * 128:(jp2 + 1) * 128, :], zt_sb[:])
                t_tzv = tzv2
            if own:
                gn_pend = tc
        if gn_pend is not None:
            tzv2, jp2 = emit_gn(gn_pend)
            t_zsp = P.dma("sync", "dzf", zf_sp[jp2 * 128:(jp2 + 1) * 128, :], zt_sb[:])
        P.barrier()
    span_es.close()
    if _PHASES <= 3:
        return _finalize(nc, P)

    with contextlib.ExitStack() as phW:
        wsb = lambda name, shape, dt: phW.enter_context(nc.sbuf_tensor(name, shape, dt))
        wof = wsb("wof", [128, 16 * 2048], BF16)
        zfl2 = [wsb(f"zfl{i}", [128, 2048], BF16) for i in range(2)]
        xol2 = [wsb(f"xol{i}", [128, 2048], F32) for i in range(2)]
        x2b2 = [wsb(f"x2b{i}", [128, 2048], F32) for i in range(2)]
        scrW = wsb("scrW", [128, 2048], F32R)
        scrD0 = wsb("scrD0", [128, 2048], F32R)
        h2t = wsb("h2t", [128, 2048], FP8)

        tw = 0
        for k in range(16):
            tw = P.dma("gpsimd", "dwo", wof[:, k * 2048:(k + 1) * 2048],
                       wo_t[k * 128:(k + 1) * 128, :])
        t_x2p = [0, 0]
        t_wod = [0, 0]
        t_xn2p = [0, 0]
        t_wodone = 0
        t_wo_mm_prev = 0
        t_h2dma_prev = 0
        t_trD_prev = [0]
        t_xn2_prev = 0

        def emit_d0(j):
            pj = j % 2
            x2s = x2b2[pj]
            P.wait("scalar", "V", t_wod[pj])
            P.wait("scalar", "S", t_xn2_prev)
            P.wait("scalar", "T", t_trD_prev[0])
            P.emitq("scalar", lambda e: e.activation(
                scrD0[:], x2s[:], AF.Square, accum_out=smalls["st2"][:, 0:1]))
            t_st = P.emit("scalar", "S", lambda e: e.activation(
                scrD0[:], x2s[:], AF.Identity, accum_out=smalls["st1"][:, 0:1]))
            P.wait("vector", "S", t_st)
            P.schain("vector", lambda e: e.tensor_scalar_mul(smalls["mu"][:, 0:1], smalls["st1"][:, 0:1], 1.0 / 2048))
            P.schain("vector", lambda e: e.tensor_scalar_mul(smalls["ms"][:, 0:1], smalls["st2"][:, 0:1], 1.0 / 2048))
            P.schain("vector", lambda e: e.tensor_mul(smalls["mu2"][:, 0:1], smalls["mu"][:, 0:1], smalls["mu"][:, 0:1]))
            P.schain("vector", lambda e: e.tensor_sub(smalls["var"][:, 0:1], smalls["ms"][:, 0:1], smalls["mu2"][:, 0:1]))
            P.schain("vector", lambda e: e.tensor_scalar_add(smalls["var"][:, 0:1], smalls["var"][:, 0:1], EPS))
            P.wait("vector", "V", P.cnt["V"])
            tv1 = P.emit("vector", "V", lambda e: e.reciprocal(smalls["rv"][:, 0:1], smalls["var"][:, 0:1]))
            P.wait("scalar", "V", tv1)
            ts1 = P.emit("scalar", "S", lambda e: e.activation(smalls["rr"][:, 0:1], smalls["rv"][:, 0:1], AF.Sqrt))
            P.wait("vector", "S", ts1)
            tv2 = P.emit("vector", "V", lambda e: e.scalar_tensor_tensor(
                smalls["nmr"][:, 0:1], smalls["mu"][:, 0:1], -1.0, smalls["rr"][:, 0:1], ALU.mult, ALU.mult))
            P.wait("scalar", "V", tv2)
            t_xn2 = P.emit("scalar", "S", lambda e: e.activation(
                scrD0[:], x2s[:], AF.Identity, bias=smalls["nmr"][:, 0:1], scale=smalls["rr"][:, 0:1]))
            t_xn2p[pj] = t_xn2
            P.wait("tensor", "S", t_xn2)
            tve = 0
            t_h2d = t_h2dma_prev
            for cg in range(4):
                for c in range(cg * 4, cg * 4 + 4):
                    q4 = c % 4
                    fn = lambda e, c=c, q4=q4: e.transpose(
                        PT4[:, q4 * 128:(q4 + 1) * 128], scrD0[:, c * 128:(c + 1) * 128], eyeR[:])
                    if c % 4 == 3:
                        ttg = P.emit("tensor", "T", fn)
                    else:
                        P.emitq("tensor", fn)
                P.wait("vector", "T", ttg)
                P.wait("vector", "dh2", t_h2d)
                for c in range(cg * 4, cg * 4 + 4):
                    tve = P.emit("vector", "V", lambda e, c=c: e.tensor_scalar(
                        h2t[:, c * 128:(c + 1) * 128], PT4[:, (c % 4) * 128:(c % 4) * 128 + 128],
                        cbm[:, c:c + 1], sbm[:, c:c + 1], ALU.mult, ALU.add))
                P.wait("tensor", "V", tve)
                t_trD_prev[0] = P.cnt["T"]
            P.wait("sync", "V", tve)
            t_h2 = t_h2dma_prev
            for cs in range(16):
                t_h2 = P.dma(
                    "sync", "dh2",
                    h2_sp[cs * 128:(cs + 1) * 128, j * 128:(j + 1) * 128],
                    h2t[:, cs * 128:(cs + 1) * 128])
            return t_xn2, t_h2

        t_womm = [0, 0]
        t_wodp = [0, 0]
        for j in range(16):
            pj = j % 2
            zfl = zfl2[pj]
            xol = xol2[pj]
            P.wait("sync", "T", t_womm[pj])
            tz = P.dma("sync", "dzl", zfl[:], zf_sp[j * 128:(j + 1) * 128, :])
            if "zfl" in _DBG and j == _DBG_SEL[1] - TC0:
                P.tap(dbg["zfl"], zfl[:], "dzl", tz)
            P.wait("sync", "V", t_wodp[pj])
            tx = P.dma("sync", "dxo", xol[:], xown[j * 128:(j + 1) * 128, :])
            P.wait("tensor", "dwo", tw)
            P.wait("tensor", "dzl", tz)
            P.wait("tensor", "V", t_wodone)
            for sl in range(4):
                pap = (P0 if sl < 2 else P1)[:, (sl % 2) * 512:(sl % 2) * 512 + 512]
                for k in range(16):
                    fn = lambda e, k=k, sl=sl, pap=pap, zfl=zfl: e.matmul(
                        pap, zfl[:, k * 128:(k + 1) * 128],
                        wof[:, k * 2048 + sl * 512:k * 2048 + (sl + 1) * 512],
                        start=(k == 0), stop=(k == 15))
                    if sl == 3 and k == 15:
                        twm = P.emit("tensor", "T", fn)
                    else:
                        P.emitq("tensor", fn)
            t_wo_mm_prev = twm
            t_womm[pj] = twm
            P.wait("vector", "T", twm)
            for sl in range(4):
                pap = (P0 if sl < 2 else P1)[:, (sl % 2) * 512:(sl % 2) * 512 + 512]
                fn = lambda e, sl=sl, pap=pap: e.scalar_tensor_tensor(
                    scrW[:, sl * 512:(sl + 1) * 512], pap, 1.0,
                    gab[:, sl * 512:(sl + 1) * 512], ALU.mult, ALU.mult)
                if sl == 3:
                    t_scrw = P.emit("vector", "V", fn)
                else:
                    P.emitq("vector", fn)
            if "scrw" in _DBG and j == _DBG_SEL[1] - TC0:
                P.tap(dbg["scrw"], scrW[:], "V", t_scrw)
            if "gab" in _DBG and j == _DBG_SEL[1] - TC0:
                P.tap(dbg["gab"], gab[:], "V", t_scrw)
            P.wait("vector", "dxo", tx)
            P.wait("vector", "dx2s", t_x2p[pj])
            P.wait("vector", "S", t_xn2p[pj])
            t_wodone = P.emit("vector", "V", lambda e, pj=pj, xol=xol: e.tensor_add(x2b2[pj][:], scrW[:], xol[:]))
            t_wod[pj] = t_wodone
            t_wodp[pj] = t_wodone
            P.wait("sync", "V", t_wodone)
            t_x2p[pj] = P.dma("sync", "dx2s", x2_sp[j * 128:(j + 1) * 128, :], x2b2[pj][:])
            if "x2" in _DBG and j == _DBG_SEL[1] - TC0:
                P.tap(dbg["x2"], x2b2[pj][:], "V", t_wodone)
            if j > 0:
                t_xn2_prev, t_h2dma_prev = emit_d0(j - 1)
                if "h2" in _DBG and (j - 1) == _DBG_SEL[1] - TC0:
                    P.tap(dbg["h2"], h2t[:], "dh2", t_h2dma_prev)
        t_xn2_prev, t_h2dma_prev = emit_d0(15)
        P.barrier()
    if _PHASES <= 4:
        return _finalize(nc, P)

    if _PHASES <= 5:
        return _finalize(nc, P)

    with contextlib.ExitStack() as phD2:
        dsb = lambda name, shape, dt: phD2.enter_context(nc.sbuf_tensor(name, shape, dt))
        h2th = dsb("h2th", [128, 16 * 1024], FP8)
        a_sb = dsb("a_sb", [128, 64 * 1024], FP8)
        wgb = dsb("wgb", [128, 2 * 2 * 2048], FP8)
        wlb = dsb("wlb", [128, 2 * 64 * 128], FP8)
        scrDF = dsb("scrD2", [128, 1024], F32R)
        ybf = dsb("ybf", [128, 1024], BF16)
        ytm = dsb("ytm", [128, 8 * 2048], BF16)
        x2lF = dsb("x2l2", [128, 2048], F32)
        outb = dsb("outb", [128, 2048], F32)

        t_gate_mm_prev = 0
        t_pslab = [0, 0]
        t_w8s = [0, 0]
        for th in range(2):
            if th == 1:
                P.wait("gpsimd", "dw8", P.cnt["dw8"])
            P.wait("sync", "T", t_gate_mm_prev)
            for ks in range(16):
                th2 = P.dma("sync", "dht",
                            h2th[:, ks * 1024:(ks + 1) * 1024],
                            h2_sp[ks * 128:(ks + 1) * 128, th * 1024:(th + 1) * 1024])
            t_gev = list(t_pslab)
            t_wg_used = [0, 0]
            tgm = 0
            DR = mybir.MatmulPerfMode.DoubleRow
            for hc in range(64):
                slot = hc % 2
                P.wait("gpsimd", "T", t_wg_used[slot])
                if th == 0:
                    P.wait("gpsimd", "dw8", t_w8s[slot])
                    P.dma("gpsimd", f"dwg{slot}",
                          wgb[:, slot * 4096:slot * 4096 + 2048].rearrange("p (k q) -> p k q", k=16),
                          wg0_t[hc].rearrange("k p q -> p k q"))
                    twg = P.dma("gpsimd", f"dwg{slot}",
                                wgb[:, slot * 4096 + 2048:slot * 4096 + 4096].rearrange("p (k q) -> p k q", k=16),
                                wg1_t[hc].rearrange("k p q -> p k q"))
                    P.wait("sync", f"dwg{slot}", twg)
                    t_w8s[slot] = P.dma("sync", "dw8", wg8_d[hc],
                                        wgb[:, slot * 4096:(slot + 1) * 4096])
                else:
                    twg = P.dma("gpsimd", f"dwg{slot}",
                                wgb[:, slot * 4096:(slot + 1) * 4096], wg8_d[hc])
                P.wait("tensor", f"dwg{slot}", twg)
                P.wait("tensor", "dht", th2)
                for half in range(2):
                    pap = P0 if half == 0 else P1
                    P.wait("tensor", "V", t_gev[half])
                    for k in range(8):
                        P.emitq("tensor", lambda e, k=k, half=half, slot=slot, pap=pap: e.matmul(
                            pap[:, 0:512],
                            wgb[:, slot * 4096 + k * 256:slot * 4096 + (k + 1) * 256]
                                .rearrange("p (r m) -> p r m", r=2),
                            h2th[:, k * 2048:(k + 1) * 2048]
                                .rearrange("p (r h t) -> p r h t", r=2, h=2)[:, :, half:half + 1, :],
                            start=(k == 0), stop=(k == 7), perf_mode=DR))
                    for k in range(8):
                        fn = lambda e, k=k, half=half, slot=slot, pap=pap: e.matmul(
                            pap[:, 512:1024],
                            wgb[:, slot * 4096 + 2048 + k * 256:slot * 4096 + 2048 + (k + 1) * 256]
                                .rearrange("p (r m) -> p r m", r=2),
                            h2th[:, k * 2048:(k + 1) * 2048]
                                .rearrange("p (r h t) -> p r h t", r=2, h=2)[:, :, half:half + 1, :],
                            start=(k == 0), stop=(k == 7), perf_mode=DR)
                        if k == 7:
                            tgm = P.emit("tensor", "T", fn)
                        else:
                            P.emitq("tensor", fn)
                    t_wg_used[slot] = tgm
                    P.wait("scalar", "T", tgm)
                    tgg = P.emit("scalar", "S", lambda e, half=half, pap=pap: e.activation(
                        scrDF[:, half * 512:(half + 1) * 512], pap[:, 0:512], AF.Gelu_apprx_tanh))
                    P.wait("vector", "S", tgg)
                    t_gev[half] = P.emit("vector", "V", lambda e, hc=hc, half=half, pap=pap: e.tensor_mul(
                        a_sb[:, hc * 1024 + half * 512:hc * 1024 + (half + 1) * 512],
                        scrDF[:, half * 512:(half + 1) * 512], pap[:, 512:1024]))
            t_gate_mm_prev = tgm
            t_wl_used = [0, 0]
            t_yev = list(t_gev)
            t_out_prev = 0
            t_ycp = 0
            for oc in range(16):
                slot = oc % 2
                papd = P0 if slot == 0 else P1
                P.wait("gpsimd", "T", t_wl_used[slot])
                if th == 0:
                    P.wait("gpsimd", "dw8", t_w8s[slot])
                    twl = P.dma("gpsimd", f"dwl{slot}",
                                wlb[:, slot * 8192:(slot + 1) * 8192].rearrange("p (h q) -> p h q", h=64),
                                wl_c[oc].rearrange("h p q -> p h q"))
                    P.wait("sync", f"dwl{slot}", twl)
                    t_w8s[slot] = P.dma("sync", "dw8", wl8_d[oc],
                                        wlb[:, slot * 8192:(slot + 1) * 8192])
                else:
                    twl = P.dma("gpsimd", f"dwl{slot}",
                                wlb[:, slot * 8192:(slot + 1) * 8192], wl8_d[oc])
                P.wait("tensor", f"dwl{slot}", twl)
                P.wait("tensor", "V", t_yev[slot])
                for hc in range(32):
                    for half in range(2):
                        fn = lambda e, hc=hc, half=half, slot=slot, papd=papd: e.matmul(
                            papd[:, half * 512:(half + 1) * 512],
                            wlb[:, slot * 8192 + hc * 256:slot * 8192 + (hc + 1) * 256]
                                .rearrange("p (r m) -> p r m", r=2),
                            a_sb[:, hc * 2048:(hc + 1) * 2048]
                                .rearrange("p (r h t) -> p r h t", r=2, h=2)[:, :, half:half + 1, :],
                            start=(hc == 0), stop=(hc == 31), perf_mode=DR)
                        if hc == 31 and half == 1:
                            tym = P.emit("tensor", "T", fn)
                        else:
                            P.emitq("tensor", fn)
                t_wl_used[slot] = tym
                P.wait("vector", "T", tym)
                tyv = P.emit("vector", "V", lambda e, oc=oc, papd=papd: e.tensor_scalar_mul(
                    ybf[:], papd[:], gbm_f[:, oc:oc + 1]))
                t_yev[slot] = tyv
                P.wait("tensor", "V", tyv)
                P.wait("tensor", "V", t_ycp)
                for tt8 in range(8):
                    fn = lambda e, tt8=tt8: e.transpose(
                        PT5[:, tt8 * 128:(tt8 + 1) * 128], ybf[:, tt8 * 128:(tt8 + 1) * 128], eyeB[:])
                    if tt8 == 7:
                        tyt = P.emit("tensor", "T", fn)
                    else:
                        P.emitq("tensor", fn)
                P.wait("vector", "T", tyt)
                for tt8 in range(8):
                    fn = lambda e, tt8=tt8, oc=oc: e.tensor_copy(
                        ytm[:, tt8 * 2048 + oc * 128:tt8 * 2048 + (oc + 1) * 128],
                        PT5[:, tt8 * 128:(tt8 + 1) * 128])
                    if tt8 == 7:
                        t_ycp = P.emit("vector", "V", fn)
                    else:
                        P.emitq("vector", fn)
            t_pslab = list(t_yev)
            tvo = 0
            for t8 in range(8):
                j = th * 8 + t8
                P.wait("sync", "V", tvo)
                tx2 = P.dma("sync", "dxl2", x2lF[:], x2_sp[j * 128:(j + 1) * 128, :])
                P.wait("vector", "dxl2", tx2)
                P.wait("vector", "dou", t_out_prev)
                P.wait("vector", "V", t_ycp)
                tvo = P.emit("vector", "V", lambda e, t8=t8: e.tensor_add(
                    outb[:], ytm[:, t8 * 2048:(t8 + 1) * 2048], x2lF[:]))
                P.wait("sync", "V", tvo)
                t_out_prev = P.dma("sync", "dou", out[j * 128:(j + 1) * 128, :], outb[:])
        P.barrier()

    return _finalize(nc, P)


def _shared_arrays(inputs):
    sh = {}
    wmod = inputs["L0_wmod"]
    sh["wmod_t"] = np.ascontiguousarray(wmod.reshape(16, 128, 96, 128).transpose(2, 0, 1, 3))
    for nm, key in (("wq_t", "L0_attn_Wq"), ("wk_t", "L0_attn_Wk"),
                    ("wv_t", "L0_attn_Wv"), ("wig_t", "L0_attn_Wig"),
                    ("wog_t", "L0_attn_Wog"), ("wo_t", "L0_attn_Wo")):
        sh[nm] = np.ascontiguousarray(inputs[key].T)
    sh["wg_t"] = np.ascontiguousarray(inputs["L0_attn_Wg"].T)
    sh["big"] = inputs["L0_attn_b_ig"].reshape(1, -1)
    sh["bog"] = inputs["L0_attn_b_og"].reshape(1, -1)
    sh["bgr"] = inputs["L0_attn_b_g"].reshape(1, -1)
    cw = inputs["L0_attn_conv_w"]
    sh["convw"] = np.ascontiguousarray(cw.reshape(16, 128, 4).transpose(1, 0, 2).reshape(128, 64))
    sh["convb"] = np.ascontiguousarray(inputs["L0_attn_conv_b"].reshape(16, 128).T)
    sh["mnw"] = inputs["L0_attn_mn_w"].reshape(1, -1)
    sh["mnb"] = inputs["L0_attn_mn_b"].reshape(1, -1)
    sh["gnw"] = inputs["L0_attn_gn_w"].reshape(1, -1)
    sh["gnb"] = inputs["L0_attn_gn_b"].reshape(1, -1)
    wg = inputs["L0_ffn_w_gating"]
    sh["wg0_t"] = np.ascontiguousarray(wg[0].reshape(16, 128, 64, 128).transpose(2, 0, 1, 3))
    sh["wg1_t"] = np.ascontiguousarray(wg[1].reshape(16, 128, 64, 128).transpose(2, 0, 1, 3))
    sh["wl_c"] = np.ascontiguousarray(
        inputs["L0_ffn_w_linear"].reshape(64, 128, 16, 128).transpose(2, 0, 1, 3))
    sh["eye"] = np.eye(128, dtype=np.float32)
    sh["eyeb"] = np.eye(128, dtype=np.float32)
    sh["tril"] = np.triu(np.ones((128, 128), np.float32))
    sh["maskm"] = np.where(np.triu(np.ones((128, 128), np.float32)) > 0, 0.0, NEGM).astype(np.float32)
    sh["ones_d"] = np.ones((1, 128), np.float32)
    sh["onesr_d"] = np.ones((1, 128), np.float32)
    return sh


def _core_inputs(inputs, sh, core):
    b, half = core // 2, core % 2
    x = inputs["L0_attn_x"]
    m = dict(sh)
    if half == 0:
        m["xpro"] = np.zeros((TOWN, D), np.float32)
        m["pmask"] = np.zeros((128, 1), np.float32)
    else:
        m["xpro"] = np.ascontiguousarray(x[b, 0:TOWN])
        m["pmask"] = np.ones((128, 1), np.float32)
    m["xown"] = np.ascontiguousarray(x[b, half * TOWN:(half + 1) * TOWN])
    m["cvec"] = np.ascontiguousarray(inputs["L0_c"][b].reshape(16, 128))
    return m


def kernel(**inputs):
    nc = build()
    sh = _shared_arrays(inputs)
    in_maps = [_core_inputs(inputs, sh, c) for c in range(8)]
    res = run_bass_kernel_spmd(nc, in_maps, core_ids=list(range(8)))
    out = np.zeros((4, 4096, 2048), np.float32)
    for core in range(8):
        b, half = core // 2, core % 2
        out[b, half * TOWN:(half + 1) * TOWN] = np.asarray(res.results[core]["out"], np.float32)
    return out
